# version 02
# baseline (speedup 1.0000x reference)
import contextlib
import numpy as np
import ml_dtypes
import concourse.bass as bass
import concourse.mybir as mybir
from concourse.bass_utils import run_bass_kernel_spmd

F32 = mybir.dt.float32
BF16 = mybir.dt.bfloat16
AF = mybir.ActivationFunctionType
ALU = mybir.AluOpType
AXL = mybir.AxisListType
BF = ml_dtypes.bfloat16

S, D = 4096, 2048
HK, HV, DK, DV = 16, 32, 128, 128
KDIM, VDIM = HK * DK, HV * DV
CONV_DIM = 2 * KDIM + VDIM
KSIZE = 4
T = 256
NT = S // T
NPAIR = T // 128
NKC = D // 128
LN_EPS, NRM_EPS, RMS_EPS, QEPS = 1e-5, 1e-6, 1e-6, 1e-5
MAGIC = float(np.float32(1.5 * 2 ** 23))
NEG = -1e30

DEBUG_SHAPES = {
    "xn": (128, NKC * T), "q": (128, HK * T), "k": (128, HK * T),
    "v": (128, HV * T), "z": (128, HV * T), "g": (32, T), "gc": (32, T),
    "beta": (32, T), "rsq": (16, 2 * T), "A0": (128, 128), "V0": (128, 128),
    "core": (128, HV * T), "x1": (128, (T // 128) * D), "S0": (128, HV * DV),
    "gctok": (128, NPAIR * 32), "u0": (64, 512),
    "lncol": (128, 16), "xt": (128, 2048), "xsb": (128, 4096),
    "idc": (128, 128), "xn_b": (128, NKC * T),
}


class Sem:
    def __init__(self, handle):
        self.h = handle
        self.n = 0

    def inc(self, k=1):
        self.n += k
        return self.n


class NS:
    pass


ORDER_EDGES = False
TRUNC = None
AUTOCHAIN = ("ve", "se", "gp")


def op(n, eng, emit, waits=(), inc=None, k=1, dma=None):
    val = inc.inc(k) if inc is not None else None
    wl = tuple((s.h, v) for (s, v) in waits if v > 0)
    ih = inc.h if inc is not None else None
    oe = None
    is_dma = (eng == "sy") if dma is None else dma
    if ORDER_EDGES or (eng in AUTOCHAIN and not is_dma):
        key = (eng, "d" if is_dma else "c")
        owl = []
        if n.olast.get(key):
            owl.append(n.olast[key])
        if is_dma and eng == "gp" and n.olast.get(("gp", "c")):
            owl.append(n.olast[("gp", "c")])
        if ih is not None:
            n.olast[key] = (ih, val)
            oe = (tuple(owl), None, 0)
        else:
            os_ = n.osem[key]
            step = 16 if is_dma else 1
            os_.inc(step)
            n.olast[key] = (os_.h, os_.n)
            oe = (tuple(owl), os_.h, step)

    def f(e, emit=emit, wl=wl, ih=ih, k=k, oe=oe):
        if oe is not None:
            for h, v in oe[0]:
                e.wait_ge(h, v)
        for h, v in wl:
            e.wait_ge(h, v)
        ins = emit(e)
        if ih is not None:
            ins.then_inc(ih, k)
        if oe is not None and oe[1] is not None:
            ins.then_inc(oe[1], oe[2])
    n.ops[eng].append(f)
    return val


def build_program(debug=()):
    nc = bass.Bass("TRN2", target_bir_lowering=False,
                   detect_race_conditions=False, debug=bool(debug))
    dbg = set(debug)
    n = NS()
    n.nc = nc
    n.dbg = dbg

    def dram(name, shape, dt, kind):
        return nc.dram_tensor(name, list(shape), dt, kind=kind).ap()

    n.x_d = dram("x", (S, D), F32, "ExternalInput")
    n.wqkv_d = dram("w_qkv", (D, CONV_DIM), F32, "ExternalInput")
    n.wz_d = dram("w_z", (D, VDIM), F32, "ExternalInput")
    n.wb_d = dram("w_b", (D, HV), F32, "ExternalInput")
    n.wa_d = dram("w_a", (D, HV), F32, "ExternalInput")
    n.wo_d = dram("w_out", (VDIM, D), F32, "ExternalInput")
    n.convw_d = dram("conv_w", (CONV_DIM, KSIZE), F32, "ExternalInput")
    n.dtb_d = dram("dt_bias", (HV, 1), F32, "ExternalInput")
    n.alog_d = dram("A_log", (HV, 1), F32, "ExternalInput")
    n.nw_d = dram("norm_w", (DV, 1), F32, "ExternalInput")
    n.gam_d = dram("gamma", (1, D), F32, "ExternalInput")
    n.wffn_d = dram("W_ffn", (D, D), F32, "ExternalInput")
    n.ident_d = dram("ident", (128, 128), BF16, "ExternalInput")
    n.maskS_d = dram("maskS", (128, 128), F32, "ExternalInput")
    n.oneh_d = dram("onehot", (128, 16 * 16), BF16, "ExternalInput")
    n.out_d = dram("out", (S, D), F32, "ExternalOutput")
    n.wqkv_b = dram("wqkv_b", (D, CONV_DIM), BF16, "Internal")
    n.wz_b = dram("wz_b", (D, VDIM), BF16, "Internal")
    n.wba_b = dram("wba_b", (D, 2 * HV), BF16, "Internal")
    n.wo_b = dram("wo_b", (VDIM, D), BF16, "Internal")
    n.wf_b = dram("wf_b", (D, D), BF16, "Internal")
    n.dbg_d = {nm: dram("dbg_" + nm, DEBUG_SHAPES[nm],
                        BF16 if nm.endswith("_b") else F32, "ExternalOutput")
               for nm in dbg}

    est = contextlib.ExitStack()
    sbt = lambda nm, shp, dt: est.enter_context(nc.sbuf_tensor(nm, list(shp), dt))
    pst = lambda nm: est.enter_context(nc.psum_tensor(nm, [128, 512], F32))
    mks = lambda nm: Sem(est.enter_context(nc.semaphore(nm)))
    n.osem = {}
    n.olast = {}
    okeys = [("ve", "c"), ("se", "c"), ("gp", "c")]
    if ORDER_EDGES:
        okeys += [("sy", "d"), ("gp", "d"), ("te", "c")]
    for ek, ck in okeys:
        n.osem[(ek, ck)] = mks(f"ord_{ek}_{ck}")

    with est:
        n.x_sb = sbt("x_sb", (128, 2 * (T // 128) * D), F32)
        n.xn_f = sbt("xn_f", (128, NKC * T), BF16)
        n.q_sb = sbt("q_sb", (128, HK * T), BF16)
        n.k_sb = sbt("k_sb", (128, HK * T), BF16)
        n.vc_sb = sbt("vc_sb", (128, HV * T), BF16)
        n.z_sb = sbt("z_sb", (128, HV * T), BF16)
        n.A_sb = sbt("A_sb", (128, HV * 128), BF16)
        n.V_sb = sbt("V_sb", (128, HV * 128), BF16)
        n.vT_sb = sbt("vT_sb", (128, HV * 128), BF16)
        n.kT_sb = sbt("kT_sb", (128, HK * 128), BF16)
        n.S_sb = sbt("S_sb", (128, HV * DV), F32)
        n.Sb_sb = sbt("Sb_sb", (128, HV * DV), BF16)
        n.pw_sb = sbt("pw_sb", (128, 11 * 512), BF16)
        n.ptil_sb = sbt("ptil_sb", (128, 512), BF16)
        n.zz_sb = sbt("zz_sb", (128, 512), BF16)
        n.u_sb = sbt("u_sb", (128, 512), BF16)
        n.rhs_sb = sbt("rhs_sb", (128, 512), BF16)
        n.kh_sb = sbt("kh_sb", (128, 512), BF16)
        n.es_sb = sbt("es_sb", (128, 512), BF16)
        n.ei_sb = sbt("ei_sb", (128, 512), BF16)
        n.o1_sb = sbt("o1_sb", (128, 512), BF16)
        n.on_sb = sbt("on_sb", (128, 512), BF16)
        n.t32a = sbt("t32a", (128, 512), F32)
        n.t32b = sbt("t32b", (128, 512), F32)
        n.arg_sb = sbt("arg_sb", (128, 512), F32)
        n.wbuf = sbt("wbuf", (128, 2 * 2048), BF16)
        n.cbuf = sbt("cbuf", (128, T + 4), BF16)
        n.ta_sb = sbt("ta_sb", (128, T), BF16)
        n.tb_sb = sbt("tb_sb", (128, T), BF16)
        n.tc_sb = sbt("tc_sb", (128, T), BF16)
        n.td_sb = sbt("td_sb", (128, T), BF16)
        n.cv_sb = sbt("cv_sb", (128, T), BF16)
        n.qsq_sb = sbt("qsq_sb", (128, T), BF16)
        n.tails = sbt("tails", (128, 64 * 4), BF16)
        n.convw = sbt("convw", (128, 64 * KSIZE), F32)
        n.g_f = sbt("g_f", (64, T), F32)
        n.gc_f = sbt("gc_f", (64, T), F32)
        n.sm1 = None
        n.sm2 = None
        n.sm3 = None
        n.beta_f = sbt("beta_f", (64, T), F32)
        n.sm1 = n.t32a[0:64, 0:256]
        n.sm2 = n.t32a[0:64, 256:512]
        n.sm3 = n.t32b[0:64, 0:256]
        n.ba_f = n.t32b[0:64, 256:512]
        n.onessm = sbt("onessm", (64, 128), F32)
        n.gc_tok = sbt("gc_tok", (128, NPAIR * 32), F32)
        n.beta_tok = sbt("beta_tok", (128, NPAIR * 32), F32)
        n.gam_tok = sbt("gam_tok", (128, NPAIR * 32), F32)
        n.dec_tok = sbt("dec_tok", (128, NPAIR * 32), F32)
        n.glast_bc = sbt("glast_bc", (128, NPAIR * 2 * 32), F32)
        n.bgr_tok = sbt("bgr_tok", (128, NPAIR * 32), F32)
        n.grq_tok = sbt("grq_tok", (128, NPAIR * 32), F32)
        n.rq_tok = sbt("rq_tok", (128, NPAIR * 16), F32)
        n.rk_tok = sbt("rk_tok", (128, NPAIR * 16), F32)
        n.brk_tok = sbt("brk_tok", (128, NPAIR * 32), F32)
        n.rsq_f = sbt("rsq_f", (16, 2 * T), F32)
        n.grow = sbt("grow", (1, 512), F32)
        n.lncol = sbt("lncol", (128, 16), F32)
        n.gamma_bc = sbt("gamma_bc", (128, D), BF16)
        n.ffnq = sbt("ffnq", (128, D), F32)
        n.xt_tok = None
        n.xq_tok = sbt("xq_tok", (128, D), BF16)
        n.identc = sbt("identc", (128, 128), BF16)
        n.identf = sbt("identf", (64, 64), F32)
        n.maskSc = sbt("maskSc", (128, 128), F32)
        n.onehc = sbt("onehc", (128, 16 * 16), BF16)
        n.ones1 = sbt("ones1", (1, 128), F32)
        n.onescol = sbt("onescol", (128, 1), F32)
        n.dtb = sbt("dtb", (64, 1), F32)
        n.nA = sbt("nA", (64, 1), F32)
        n.nw = sbt("nw", (128, 1), F32)
        n.mw_bc = sbt("mw_bc", (128, 1), F32)
        n.wsc = sbt("wsc", (128, 1), F32)
        n.xt_tok = n.xq_tok
        n.setup1 = n.ffnq
        n.setup2 = n.xq_tok
        n.wabs = sbt("wabs", (128, 4), F32)

        n.P = [pst("pb%d" % i) for i in range(8)]

        n.sW0, n.sW1, n.sG, n.sO = mks("sW0"), mks("sW1"), mks("sG"), mks("sO")
        n.sXa, n.sXb, n.sC = mks("sXa"), mks("sXb"), mks("sC")
        n.sT, n.sV, n.sE, n.sP = mks("sT"), mks("sV"), mks("sE"), mks("sP")
        n.sQ = mks("sQ")
        n.sD = mks("sD")
        n.ops = {e: [] for e in ("te", "ve", "se", "gp", "sy")}

        schedule(n)

        blk = est.enter_context(nc.Block())

        @blk.sync
        def _(eng):
            for f in n.ops["sy"]:
                f(eng)

        @blk.tensor
        def _(eng):
            for f in n.ops["te"]:
                f(eng)

        @blk.vector
        def _(eng):
            for f in n.ops["ve"]:
                f(eng)

        @blk.scalar
        def _(eng):
            for f in n.ops["se"]:
                f(eng)

        @blk.gpsimd
        def _(eng):
            for f in n.ops["gp"]:
                f(eng)

    return nc


def schedule(n):
    sT, sV, sE, sP = n.sT, n.sV, n.sE, n.sP
    sG, sO, sC = n.sG, n.sO, n.sC

    setup(n)
    for t in range(NT):
        tile_body(n, t)
    endw = [(sO, sO.n)]
    if n.dbg:
        endw.append((n.sD, n.sD.n))
    op(n, "gp", lambda e: e.engine_nop(), waits=endw)


def setup(n):
    sT, sV, sE, sP = n.sT, n.sV, n.sE, n.sP
    sG, sO, sC = n.sG, n.sO, n.sC
    sXa = n.sXa
    P = n.P

    def syd(out, in_, waits=()):
        return op(n, "sy", lambda e, o=out, i=in_: e.dma_start(out=o, in_=i),
                  waits=waits, inc=sC, k=16)

    def gpd(out, in_, waits=()):
        return op(n, "gp", lambda e, o=out, i=in_: e.dma_start(out=o, in_=i),
                  waits=waits, inc=sP, k=16, dma=True)

    syd(n.identc[:], n.ident_d)
    syd(n.maskSc[:], n.maskS_d)
    syd(n.onehc[:], n.oneh_d)
    syd(n.dtb[32:64, :], n.dtb_d)
    syd(n.nA[32:64, :], n.alog_d)
    syd(n.nw[:], n.nw_d)
    syd(n.setup1[0:1, 0:D], n.gam_d)
    v_cw = syd(n.convw[:], n.convw_d.rearrange("(c p) s -> p c s", p=128))
    n.v_convw = v_cw
    v_consts = sC.n
    v_gam = v_consts

    n.xload_v = {}
    v0 = op(n, "sy", lambda e: e.dma_start(
        out=n.x_sb[:, 0:(T // 128) * D],
        in_=n.x_d[0:T, :].rearrange("(a p) d -> p a d", p=128)),
        inc=n.sXa, k=16)
    n.xload_v[0] = (n.sXa, v0)

    op(n, "ve", lambda e: e.memset(n.ones1[:], 1.0), waits=[(sC, v_consts)])
    op(n, "ve", lambda e: e.memset(n.onescol[:], 1.0))
    op(n, "ve", lambda e: e.memset(n.onessm[:], 1.0))
    op(n, "ve", lambda e: e.memset(n.S_sb[:], 0.0))
    op(n, "ve", lambda e: e.memset(n.Sb_sb[:], 0.0))
    op(n, "ve", lambda e: e.memset(n.tails[:], 0.0))
    v_idf = op(n, "gp", lambda e: e.dma_start(out=n.identf[:],
                                              in_=n.ident_d[0:64, 0:64]),
               inc=sP, k=16, dma=True)
    n.v_idf = v_idf

    v_e = op(n, "se", lambda e: e.activation(n.nA[32:64, :], n.nA[32:64, :], AF.Exp),
             waits=[(sC, v_consts)], inc=sE)
    v_na = op(n, "ve", lambda e: e.tensor_scalar_mul(n.nA[32:64, :], n.nA[32:64, :], -1.0),
              waits=[(sE, v_e)], inc=sV)
    n.v_smallconst = v_na

    for i in range(4):
        tv = op(n, "te", lambda e, i=i: e.matmul(
            P[0][:, 0:512], n.ones1[0:1, :], n.setup1[0:1, i * 512:(i + 1) * 512],
            start=True, stop=True),
            waits=([(sC, v_gam), (sV, v_na)] if i == 0 else
                   [(sV, n.sV.n)]), inc=sT)
        op(n, "ve", lambda e, i=i: e.tensor_copy(
            n.gamma_bc[:, i * 512:(i + 1) * 512], P[0][:, 0:512]),
            waits=[(sT, tv)], inc=sV)
    n.v_gammm = sT.n

    for i in range(8):
        gpd(n.wqkv_b[i * 256:(i + 1) * 256, :], n.wqkv_d[i * 256:(i + 1) * 256, :])
    for i in range(4):
        gpd(n.wz_b[i * 512:(i + 1) * 512, :], n.wz_d[i * 512:(i + 1) * 512, :])
    gpd(n.wba_b[:, 0:HV], n.wb_d)
    gpd(n.wba_b[:, HV:2 * HV], n.wa_d)
    for i in range(16):
        gpd(n.wo_b[i * 256:(i + 1) * 256, :], n.wo_d[i * 256:(i + 1) * 256, :])
    n.v_wcast = sP.n

    v_z = op(n, "ve", lambda e: e.memset(n.wabs[:, 0:1], 0.0), inc=sV)
    prev_use = 0
    for i in range(16):
        lv = syd(n.setup1[:], n.wffn_d[i * 128:(i + 1) * 128, :],
                 waits=([(sV, prev_use)] if i else [(sT, n.v_gammm)]))
        op(n, "ve", lambda e: e.reduce_sum(n.wabs[:, 1:2], n.setup1[:], axis=AXL.X,
                                           apply_absolute_value=True),
           waits=[(sC, lv)])
        prev_use = op(n, "ve", lambda e: e.tensor_add(
            n.wabs[:, 0:1], n.wabs[:, 0:1], n.wabs[:, 1:2]), inc=sV, k=2)
    tv = op(n, "te", lambda e: e.matmul(P[1][0:1, 0:1], n.onescol[:],
                                        n.wabs[:, 0:1], start=True, stop=True),
            waits=[(sV, prev_use)], inc=sT)
    op(n, "ve", lambda e: e.tensor_scalar(n.wabs[0:1, 2:3], P[1][0:1, 0:1],
                                          1.0 / (D * D), QEPS,
                                          op0=ALU.mult, op1=ALU.add),
       waits=[(sT, tv)])
    v_r = op(n, "ve", lambda e: e.reciprocal(n.wabs[0:1, 3:4], n.wabs[0:1, 2:3]),
             inc=sV, k=2)
    tv = op(n, "te", lambda e: e.matmul(P[1][:, 4:5], n.ones1[0:1, :],
                                        n.wabs[0:1, 3:4], start=True, stop=True),
            waits=[(sV, v_r)])
    tv = op(n, "te", lambda e: e.matmul(P[1][:, 5:6], n.ones1[0:1, :],
                                        n.wabs[0:1, 2:3], start=True, stop=True),
            inc=sT)
    op(n, "ve", lambda e: e.tensor_copy(n.wsc[:], P[1][:, 4:5]), waits=[(sT, tv)])
    v_wsc = op(n, "ve", lambda e: e.tensor_copy(n.mw_bc[:], P[1][:, 5:6]), inc=sV)

    prev_gp = 0
    prev_ve = v_wsc
    for i in range(16):
        lv = syd(n.setup1[:], n.wffn_d[i * 128:(i + 1) * 128, :],
                 waits=[(sV, prev_ve)])
        op(n, "ve", lambda e: e.tensor_scalar_mul(n.ffnq[:], n.setup1[:],
                                                  n.wsc[:, 0:1]),
           waits=[(sC, lv), (sV, v_wsc)])
        op(n, "ve", lambda e: e.tensor_scalar(n.ffnq[:], n.ffnq[:], MAGIC, MAGIC,
                                              op0=ALU.add, op1=ALU.subtract))
        qv = op(n, "ve", lambda e: e.tensor_scalar(n.setup2[:], n.ffnq[:], 1.0, -1.0,
                                                   op0=ALU.min, op1=ALU.max),
                inc=sV, k=3)
        prev_ve = qv
        for k in range(16):
            tv = op(n, "te", lambda e, k=k: e.transpose(
                P[2][:].bitcast(BF16)[:, (k % 4) * 128:(k % 4) * 128 + 128],
                n.setup2[:, k * 128:(k + 1) * 128], n.identc[:]),
                waits=([(sV, qv), (sP, prev_gp)] if k == 0 else
                       ([(sV, sV.n)] if k % 4 == 0 else [])), inc=sT)
            if k % 4 == 3:
                prev_ve = op(n, "ve", lambda e, k=k: e.tensor_copy(
                    n.wbuf[:, (k - 3) * 128:(k + 1) * 128],
                    P[2][:].bitcast(BF16)[:, 0:512]),
                    waits=[(sT, tv)], inc=sV)
        prev_gp = gpd(n.wf_b[:, i * 128:(i + 1) * 128].rearrange(
            "(a p) c -> p a c", p=128), n.wbuf[:, 0:2048],
            waits=[(sV, prev_ve)])
    n.v_wffn = sP.n
    n.v_setupgp = sP.n
    n.v_setupT = sT.n
    n.setup_sC = sC.n
    n.mark = {}
    n.mark[("ffnq_free", -1)] = sV.n


class WStream:
    def __init__(self, n):
        self.n = n
        self.idx = 0
        self.consumed = []

    def load(self, dram_ap, waits=(), cols=2048):
        n = self.n
        half = self.idx % 2
        sem = n.sW0 if half == 0 else n.sW1
        wl = list(waits)
        if self.idx >= 2:
            wl.append((n.sT, self.consumed[self.idx - 2]))
        v = op(n, "sy", lambda e, o=n.wbuf[:, half * 2048:half * 2048 + cols],
               i=dram_ap: e.dma_start(out=o, in_=i), waits=wl, inc=sem, k=16)
        self.idx += 1
        return half, (sem, v)

    def done(self, sT_val):
        self.consumed.append(sT_val)


def tile_body(n, t):
    sT, sV, sE, sP = n.sT, n.sV, n.sE, n.sP
    sG, sO, sC = n.sG, n.sO, n.sC
    P = n.P
    par = t % 2
    xoff = par * (T // 128) * D
    if not hasattr(n, "ws"):
        n.ws = WStream(n)
        n.store_v = {}
    ws = n.ws
    dbg = n.dbg

    if t + 1 < NT:
        npar = (t + 1) % 2
        sem = n.sXa if npar == 0 else n.sXb
        wl = []
        if t >= 1:
            wl.append((sO, n.store_v[t - 1]))
        vx = op(
            n, "sy", lambda e, t=t, npar=npar: e.dma_start(
                out=n.x_sb[:, npar * (T // 128) * D:(npar + 1) * (T // 128) * D],
                in_=n.x_d[(t + 1) * T:(t + 2) * T, :].rearrange(
                    "(a p) d -> p a d", p=128)),
            waits=wl, inc=sem, k=16)
        n.xload_v[t + 1] = (sem, vx)

    xn_guard = n.mark.get(("projdone", t - 1), 0)
    for tc in range(T // 128):
        xs = n.x_sb[:, xoff + tc * D: xoff + (tc + 1) * D]
        wl = [n.xload_v[t]] + ([(sV, n.mark[("ffnq_free", t - 1)])]
                                     if ("ffnq_free", t - 1) in n.mark else [])
        op(n, "ve", lambda e, xs=xs: e.reduce_sum(n.lncol[:, 0:1], xs, axis=AXL.X),
           waits=wl)
        v_red = op(n, "ve", lambda e, xs=xs: e.tensor_scalar_mul(
            n.lncol[:, 2:3], n.lncol[:, 0:1], 1.0 / D), inc=sV)
        e_sq = op(n, "se", lambda e, xs=xs: e.activation(
            n.ffnq[:], xs, AF.Square, accum_out=n.lncol[:, 1:2]),
            waits=[n.xload_v[t]] + ([(sV, n.mark[("ffnq_free", t - 1)])]
                                          if ("ffnq_free", t - 1) in n.mark else []),
            inc=sE)
        op(n, "ve", lambda e: e.tensor_scalar_mul(n.lncol[:, 3:4], n.lncol[:, 1:2],
                                                  1.0 / D), waits=[(sE, e_sq)])
        op(n, "ve", lambda e: e.tensor_mul(n.lncol[:, 4:5], n.lncol[:, 2:3],
                                           n.lncol[:, 2:3]))
        op(n, "ve", lambda e: e.tensor_sub(n.lncol[:, 5:6], n.lncol[:, 3:4],
                                           n.lncol[:, 4:5]))
        op(n, "ve", lambda e: e.tensor_scalar_add(n.lncol[:, 6:7], n.lncol[:, 5:6],
                                                  LN_EPS))
        v_rc = op(n, "ve", lambda e: e.reciprocal(n.lncol[:, 7:8], n.lncol[:, 6:7]),
                  inc=sV)
        e_st = op(n, "se", lambda e: e.activation(n.lncol[:, 8:9], n.lncol[:, 7:8],
                                                  AF.Sqrt), waits=[(sV, v_rc)],
                  inc=sE)
        v_xn = op(n, "ve", lambda e, xs=xs: e.tensor_scalar(
            n.xt_tok[:], xs, n.lncol[:, 2:3], n.lncol[:, 8:9],
            op0=ALU.subtract, op1=ALU.mult),
            waits=[(sE, e_st),
                   (sT, n.mark.get(("xqT", t - 1), n.v_setupT if t == 0 else 0))],
            inc=sV)
        for kk in range(NKC):
            bank = P[kk % 2]
            if kk < 2:
                wl = [(sV, v_xn), (sV, n.mark.get(("xnev", kk - 2), 0)),
                      (sV, n.mark.get(("ffndone", t - 1), 0))]
            else:
                wl = [(sV, n.mark[("xnev", kk - 2)])]
            tv = op(n, "te", lambda e, kk=kk, bank=bank: e.transpose(
                bank[:].bitcast(BF16)[:, 0:128],
                n.xt_tok[:, kk * 128:(kk + 1) * 128], n.identc[:]),
                waits=wl, inc=sT)
            n.mark[("xnev", kk)] = op(
                n, "ve", lambda e, kk=kk, tc=tc, bank=bank: e.tensor_copy(
                    n.xn_f[:, kk * T + tc * 128: kk * T + tc * 128 + 128],
                    bank[:].bitcast(BF16)[:, 0:128]), waits=[(sT, tv)], inc=sV)
    v_xnf = sV.n
    if "xn" in dbg and t == 0:
        dbg_dump(n, "xn", n.xn_f[:], [(sV, v_xnf)])
    if "xn_b" in dbg and t == 0:
        dbg_dump(n, "xn_b", n.xn_f[:], [(sV, v_xnf)])
    if t == 0:
        if "lncol" in dbg:
            dbg_dump(n, "lncol", n.lncol[:], [(sV, v_xnf)])
        if "xt" in dbg:
            dbg_dump(n, "xt", n.xt_tok[:], [(sV, v_xnf)])
        if "xsb" in dbg:
            dbg_dump(n, "xsb", n.x_sb[:, 0:2 * D], [(sV, v_xnf)])
        if "idc" in dbg:
            dbg_dump(n, "idc", n.identc[:], [(sV, v_xnf)])
    if TRUNC == "ln1":
        return

    projection_phase(n, t, v_xnf, xn_guard)
    if TRUNC == "proj":
        return

    smalls_phase(n, t)
    if TRUNC == "smalls":
        return

    for p in range(NPAIR):
        delta_prep(n, t, p)
        for ci in range(2):
            delta_state(n, t, p, ci)
    n.mark[("gathers", t)] = n.sG.n
    if TRUNC == "delta":
        return

    wout_phase(n, t)
    if TRUNC == "wout":
        return

    ffn_phase(n, t)


def dbg_dump(n, name, src_ap, waits):
    op(n, "gp", lambda e, o=n.dbg_d[name], i=src_ap: e.dma_start(out=o, in_=i),
       waits=waits, inc=n.sD, k=16, dma=True)


def projection_phase(n, t, v_xnf, xn_guard):
    sT, sV, sE, sP = n.sT, n.sV, n.sE, n.sP
    sO = n.sO
    P = n.P
    ws = n.ws
    dbg = n.dbg

    vc_guard = n.mark.get(("statevec", t - 1), 0)

    n.qk_ready = {}
    sumsq_mm = 0
    for oc in range(64):
        half, wv = ws.load(
            n.wqkv_b[:, oc * 128:(oc + 1) * 128].rearrange("(k p) c -> p k c", p=128),
            waits=[(sP, n.v_setupgp)] if (t == 0 and oc == 0) else ())
        bank = P[oc % 4]
        tv = None
        for kk in range(NKC):
            wl = []
            if kk == 0:
                wl = [wv, (sV, v_xnf)]
                if oc < 4:
                    wl.append((sV, n.mark.get(("ffndone", t - 1), 0)))
                else:
                    wl.append((sV, n.mark[("convev", t, oc - 4)]))
            tv = op(n, "te", lambda e, oc=oc, kk=kk, half=half, bank=bank: e.matmul(
                bank[:, 0:256], n.wbuf[:, half * 2048 + kk * 128: half * 2048 + (kk + 1) * 128],
                n.xn_f[:, kk * T:(kk + 1) * T],
                start=(kk == 0), stop=(kk == NKC - 1)), waits=wl, inc=sT)
        ws.done(tv)
        conv_emit(n, t, oc, bank, tv, vc_guard)
    for oc in range(32):
        half, wv = ws.load(
            n.wz_b[:, oc * 128:(oc + 1) * 128].rearrange("(k p) c -> p k c", p=128))
        bank = P[4 + oc % 2]
        tv = None
        for kk in range(NKC):
            wl = []
            if kk == 0:
                if oc < 2:
                    wl = [wv, (sV, n.mark.get(("ffndone", t - 1), 0)),
                          (sV, vc_guard)]
                else:
                    wl = [wv, (sE, n.mark[("zev", t, oc - 2)])]
            tv = op(n, "te", lambda e, oc=oc, kk=kk, half=half, bank=bank: e.matmul(
                bank[:, 0:256], n.wbuf[:, half * 2048 + kk * 128: half * 2048 + (kk + 1) * 128],
                n.xn_f[:, kk * T:(kk + 1) * T],
                start=(kk == 0), stop=(kk == NKC - 1)), waits=wl, inc=sT)
        ws.done(tv)
        n.mark[("zev", t, oc)] = op(
            n, "se", lambda e, oc=oc, bank=bank: e.activation(
                n.z_sb[:, oc * T:(oc + 1) * T], bank[:, 0:256], AF.Silu),
            waits=[(sT, tv)], inc=sE)
    if "z" in dbg and t == 0:
        dbg_dump(n, "z", n.z_sb[:], [(sE, n.sE.n)])

    half, wv = ws.load(n.wba_b.rearrange("(k p) c -> p k c", p=128), cols=1024)
    bank = P[6]
    tv = None
    for kk in range(NKC):
        wl = ([wv, (sV, n.mark.get(("smalls", t - 1, NPAIR - 1), 0)),
               (sE, n.mark.get(("sigb", t - 1), 0))] if kk == 0 else [])
        tv = op(n, "te", lambda e, kk=kk, half=half: e.matmul(
            bank[0:64, 0:256], n.wbuf[:, half * 2048 + kk * 64: half * 2048 + (kk + 1) * 64],
            n.xn_f[:, kk * T:(kk + 1) * T],
            start=(kk == 0), stop=(kk == NKC - 1)), waits=wl, inc=sT)
    ws.done(tv)
    n.v_ba = op(n, "ve", lambda e, bank=bank: e.tensor_copy(
        n.ba_f[:], bank[0:64, 0:256]), waits=[(sT, tv)], inc=sV)
    n.mark[("projdone", t)] = sT.n

    l2_finish(n, t)


def conv_emit(n, t, oc, bank, tv_mm, vc_guard):
    sT, sV, sE, sP = n.sT, n.sV, n.sE, n.sP
    sQ = n.sQ
    P = n.P
    dbg = n.dbg
    w = n.convw
    gv = op(n, "gp", lambda e, oc=oc: e.tensor_copy(
        n.cbuf[:, 1:4], n.tails[:, oc * 4 + 1: oc * 4 + 4]),
        waits=[(sV, n.mark.get(("convdone", t, oc - 1),
                               n.mark.get(("convdone", t - 1, 63),
                                          n.v_smallconst)))], inc=sQ)
    ev = op(n, "ve", lambda e, bank=bank: e.tensor_copy(n.cbuf[:, 4:4 + T],
                                                        bank[:, 0:256]),
            waits=[(sT, tv_mm), (sQ, n.mark.get(("convgp", t, oc - 1), 0))],
            inc=sV)
    n.mark[("convev", t, oc)] = ev
    gv2 = op(n, "gp", lambda e, oc=oc: e.tensor_copy(
        n.tails[:, oc * 4 + 1: oc * 4 + 4], n.cbuf[:, 1 + T: 4 + T]),
        waits=[(sV, ev)], inc=sQ)
    op(n, "ve", lambda e, oc=oc: e.tensor_scalar_mul(
        n.ta_sb[:], n.cbuf[:, 4:4 + T], w[:, oc * 4 + 3: oc * 4 + 4]),
        waits=[(sQ, gv)])
    v1 = op(n, "ve", lambda e, oc=oc: e.tensor_scalar_mul(
        n.tb_sb[:], n.cbuf[:, 3:3 + T], w[:, oc * 4 + 2: oc * 4 + 3]), inc=sV)
    g1 = op(n, "gp", lambda e, oc=oc: e.tensor_scalar_mul(
        n.tc_sb[:], n.cbuf[:, 2:2 + T], w[:, oc * 4 + 1: oc * 4 + 2]),
        waits=[(sV, ev), (sQ, gv)], inc=sQ)
    g2 = op(n, "gp", lambda e, oc=oc: e.tensor_scalar_mul(
        n.td_sb[:], n.cbuf[:, 1:1 + T], w[:, oc * 4 + 0: oc * 4 + 1]), inc=sQ)
    v2 = op(n, "ve", lambda e: e.tensor_add(n.ta_sb[:], n.ta_sb[:], n.tb_sb[:]),
            inc=sV)
    g3 = op(n, "gp", lambda e: e.tensor_add(n.tc_sb[:], n.tc_sb[:], n.td_sb[:]),
            inc=sQ)
    n.mark[("convgp", t, oc)] = g3
    v3 = op(n, "ve", lambda e: e.tensor_add(n.cv_sb[:], n.ta_sb[:], n.tc_sb[:]),
            waits=[(sQ, g3),
                   (n.sE, n.mark.get(("silu", t, oc - 1),
                                     n.mark.get(("silu", t - 1, 63), 0)))],
            inc=sV)
    n.mark[("convdone", t, oc)] = v3
    if oc < 16:
        dst = n.q_sb[:, oc * T:(oc + 1) * T]
    elif oc < 32:
        dst = n.k_sb[:, (oc - 16) * T:(oc - 15) * T]
    else:
        dst = n.vc_sb[:, (oc - 32) * T:(oc - 31) * T]
    ev2 = op(n, "se", lambda e, dst=dst: e.activation(dst, n.cv_sb[:], AF.Silu),
             waits=[(sV, v3), (sT, n.mark.get(("woutmm", t - 1), 0))], inc=sE)
    n.mark[("silu", t, oc)] = ev2
    if oc < 32:
        v4 = op(n, "ve", lambda e, dst=dst: e.tensor_mul(n.qsq_sb[:], dst, dst),
                waits=[(sE, ev2)], inc=sV)
        h = oc % 16
        is_q = oc < 16
        colr = slice(0, 256) if is_q else slice(256, 512)
        wl5 = [(sV, v4)]
        if oc == 0:
            wl5.append((sE, n.mark.get(("rsqdone", t - 1), 0)))
        tv = op(n, "te", lambda e, h=h, colr=colr: e.matmul(
            n.P[7][0:16, colr], n.onehc[:, h * 16:(h + 1) * 16], n.qsq_sb[:],
            start=(oc == 0), stop=(oc == 31)), waits=wl5, inc=sT)
        n.mark[("l2mm", t, oc)] = tv


def l2_finish(n, t):
    sT, sV, sE = n.sT, n.sV, n.sE
    P = n.P
    dbg = n.dbg
    tv = n.mark[("l2mm", t, 31)]
    op(n, "ve", lambda e: e.tensor_scalar(n.rsq_f[:, 0:256], P[7][0:16, 0:256],
                                          float(DK), float(DK) * NRM_EPS,
                                          op0=ALU.mult, op1=ALU.add),
       waits=[(sT, tv), (sT, n.mark.get(("rsqT", t - 1), 0))])
    op(n, "ve", lambda e: e.tensor_scalar(n.rsq_f[:, 256:512], P[7][0:16, 256:512],
                                          1.0, NRM_EPS, op0=ALU.mult, op1=ALU.add))
    v1 = op(n, "ve", lambda e: e.reciprocal(n.rsq_f[:], n.rsq_f[:]), inc=sV, k=3)
    e1 = op(n, "se", lambda e: e.activation(n.rsq_f[:], n.rsq_f[:], AF.Sqrt),
            waits=[(sV, v1)], inc=sE)
    n.v_rsq = e1
    n.mark[("rsqdone", t)] = e1
    if "rsq" in dbg and t == 0:
        dbg_dump(n, "rsq", n.rsq_f[:], [(sE, e1)])


def smalls_phase(n, t):
    sT, sV, sE = n.sT, n.sV, n.sE
    P = n.P
    dbg = n.dbg
    v_ba = n.v_ba

    e_b = op(n, "se", lambda e: e.activation(n.beta_f[0:32, :], n.ba_f[0:32, :],
                                             AF.Sigmoid), waits=[(sV, v_ba)], inc=sE)
    n.mark[("sigb", t)] = e_b
    op(n, "ve", lambda e: e.tensor_scalar_add(n.sm1[32:64, :], n.ba_f[32:64, :],
                                              n.dtb[32:64, 0:1]), waits=[(sV, v_ba)])
    v_u = op(n, "ve", lambda e: e.tensor_scalar(n.sm2[32:64, :], n.sm1[32:64, :],
                                                0.0, 0.0, op0=ALU.max, op1=ALU.bypass),
             inc=sV, k=2)
    e_a = op(n, "se", lambda e: e.activation(n.sm3[32:64, :], n.sm1[32:64, :], AF.Abs),
             waits=[(sV, v_u)], inc=sE)
    e_e = op(n, "se", lambda e: e.activation(n.sm3[32:64, :], n.sm3[32:64, :],
                                             AF.Exp, scale=-1.0), inc=sE)
    e_l = op(n, "se", lambda e: e.activation(n.sm3[32:64, :], n.sm3[32:64, :],
                                             AF.Ln, bias=1.0), inc=sE)
    op(n, "ve", lambda e: e.tensor_add(n.sm2[32:64, :], n.sm2[32:64, :],
                                       n.sm3[32:64, :]), waits=[(sE, e_l)])
    v_g = op(n, "ve", lambda e: e.tensor_scalar_mul(n.g_f[32:64, :], n.sm2[32:64, :],
                                                    n.nA[32:64, 0:1]), inc=sV, k=2)
    v_gc = None
    for c in range(T // 64):
        wlc = [(sV, v_g)] if c == 0 else []
        if c == 0:
            wlc.append((n.sG, n.mark.get(("gathers", t - 1), 0)))
        v_gc = op(n, "ve", lambda e, c=c: e.tensor_tensor_scan(
            n.gc_f[32:64, c * 64:(c + 1) * 64], n.g_f[32:64, c * 64:(c + 1) * 64],
            n.g_f[32:64, c * 64:(c + 1) * 64], 0.0, op0=ALU.add, op1=ALU.bypass),
            waits=wlc, inc=sV)
    n.v_gc = v_gc
    if "gc" in dbg and t == 0:
        dbg_dump(n, "gc", n.gc_f[32:64, :], [(sV, v_gc)])
    if "beta" in dbg and t == 0:
        dbg_dump(n, "beta", n.beta_f[0:32, :], [(sE, e_b)])
    if "g" in dbg and t == 0:
        dbg_dump(n, "g", n.g_f[32:64, :], [(sV, v_g)])

    sv_prev = n.mark.get(("statevec", t - 1), 0)
    prev_t4 = 0
    prev_vgl = sv_prev
    prev_vgct = prev_egam = prev_vbt = prev_vdec = prev_vrk = 0
    for p in range(NPAIR):
        po = p * 32
        t1 = op(n, "te", lambda e, p=p: e.transpose(
            P[5][0:128, 0:32], n.gc_f[32:64, p * 128:(p + 1) * 128],
            n.identf[32:64, 32:64]),
            waits=[(sV, v_gc), (sV, sv_prev), (sV, prev_vgct),
                   (sE, prev_egam if p else n.mark.get(("egam", t - 1), 0))],
            inc=sT)
        op(n, "ve", lambda e, po=po: e.tensor_copy(n.gc_tok[:, po:po + 32],
                                                   P[5][0:128, 0:32]),
           waits=[(sT, t1)])
        v_gct = op(n, "ve", lambda e, po=po: e.tensor_copy(
            n.t32a[:, 0:32], P[5][0:128, 0:32]), inc=sV, k=2)
        e_gam = op(n, "se", lambda e, po=po: e.activation(
            n.gam_tok[:, po:po + 32], P[5][0:128, 0:32], AF.Exp),
            waits=[(sV, v_gct)], inc=sE)
        n.mark[("egam", t)] = e_gam
        prev_vgct, prev_egam = v_gct, e_gam
        t2 = op(n, "te", lambda e, p=p: e.transpose(
            P[5][0:128, 32:64], n.beta_f[0:32, p * 128:(p + 1) * 128],
            n.identf[0:32, 0:32]),
            waits=[(sE, e_b), (sE, e_gam), (sV, prev_vbt)], inc=sT)
        v_bt = op(n, "ve", lambda e, po=po: e.tensor_copy(
            n.beta_tok[:, po:po + 32], P[5][0:128, 32:64]), waits=[(sT, t2)], inc=sV)
        prev_vbt = v_bt
        for c in range(2):
            cc = p * 128 + c * 64 + 63
            op(n, "ve", lambda e, c=c, cc=cc: e.tensor_scalar_mul(
                n.sm1[32:64, c * 64:(c + 1) * 64], n.onessm[32:64, 0:64],
                n.gc_f[32:64, cc:cc + 1]), waits=[(sV, v_gc), (sT, prev_t4)])
        v_glb = op(n, "ve", lambda e, p=p: e.tensor_sub(
            n.sm2[32:64, 0:128], n.sm1[32:64, 0:128],
            n.gc_f[32:64, p * 128:(p + 1) * 128]), inc=sV, k=3)
        e_d = op(n, "se", lambda e: e.activation(n.sm3[32:64, 0:128],
                                                 n.sm2[32:64, 0:128], AF.Exp),
                 waits=[(sV, v_glb)], inc=sE)
        t3 = op(n, "te", lambda e: e.transpose(P[5][0:128, 64:96],
                                               n.sm3[32:64, 0:128],
                                               n.identf[32:64, 32:64]),
                waits=[(sE, e_d), (sV, prev_vdec)], inc=sT)
        v_dec = op(n, "ve", lambda e, po=po: e.tensor_copy(
            n.dec_tok[:, po:po + 32], P[5][0:128, 64:96]),
            waits=[(sT, t3)], inc=sV)
        prev_vdec = v_dec
        for c in range(2):
            cc = p * 128 + c * 64 + 63
            ts_gl = op(n, "ve", lambda e, cc=cc: e.tensor_scalar_mul(
                n.sm1[32:64, 0:128], n.onessm[32:64, 0:128],
                n.gc_f[32:64, cc:cc + 1]),
                waits=[(sV, v_gc), (sT, prev_t4)], inc=sV)
            e_g2 = op(n, "se", lambda e: e.activation(
                n.sm1[32:64, 0:128], n.sm1[32:64, 0:128], AF.Exp),
                waits=[(sV, ts_gl)], inc=sE)
            t4 = op(n, "te", lambda e: e.transpose(
                P[5][0:128, 96:128], n.sm1[32:64, 0:128],
                n.identf[32:64, 32:64]),
                waits=[(sE, e_g2), (sV, prev_vgl)], inc=sT)
            prev_t4 = t4
            v_gl = op(n, "ve", lambda e, po=po, c=c, p=p: e.tensor_copy(
                n.glast_bc[:, (p * 2 + c) * 32:(p * 2 + c) * 32 + 32],
                P[5][0:128, 96:128]), waits=[(sT, t4)], inc=sV)
            prev_vgl = v_gl
        t5 = op(n, "te", lambda e, p=p: e.transpose(
            P[6][0:128, 0:16], n.rsq_f[0:16, p * 128:(p + 1) * 128],
            n.identf[0:16, 0:16]),
            waits=[(sE, n.v_rsq), (sV, sv_prev), (sV, prev_vrk)], inc=sT)
        t6 = op(n, "te", lambda e, p=p: e.transpose(
            P[6][0:128, 16:32], n.rsq_f[0:16, 256 + p * 128: 256 + (p + 1) * 128],
            n.identf[0:16, 0:16]), inc=sT)
        n.mark[("rsqT", t)] = t6
        po16 = p * 16
        op(n, "ve", lambda e, po16=po16: e.tensor_copy(
            n.rq_tok[:, po16:po16 + 16], P[6][0:128, 0:16]), waits=[(sT, sT.n)])
        v_rk = op(n, "ve", lambda e, po16=po16: e.tensor_copy(
            n.rk_tok[:, po16:po16 + 16], P[6][0:128, 16:32]), inc=sV, k=2)
        prev_vrk = v_rk
        def rkx(pp):
            return n.rk_tok[:, pp * 16:(pp + 1) * 16].rearrange(
                "p (a b) -> p a b", b=1).broadcast_to([128, 16, 2])

        def v3d(ap):
            return ap.rearrange("p (a b) -> p a b", b=2)
        op(n, "ve", lambda e, po=po: e.tensor_mul(
            n.t32a[:, 0:32], n.beta_tok[:, po:po + 32], n.gam_tok[:, po:po + 32]),
            waits=[(sV, v_bt), (sV, v_rk), (sE, e_gam)])
        op(n, "ve", lambda e, po=po, p=p: e.tensor_tensor(
            v3d(n.bgr_tok[:, po:po + 32]), v3d(n.t32a[:, 0:32]), rkx(p),
            op=ALU.mult))
        op(n, "ve", lambda e, po=po, p=p: e.tensor_tensor(
            v3d(n.grq_tok[:, po:po + 32]), v3d(n.gam_tok[:, po:po + 32]),
            n.rq_tok[:, p * 16:(p + 1) * 16].rearrange(
                "p (a b) -> p a b", b=1).broadcast_to([128, 16, 2]),
            op=ALU.mult))
        op(n, "ve", lambda e, po=po, p=p: e.tensor_tensor(
            v3d(n.brk_tok[:, po:po + 32]), v3d(n.beta_tok[:, po:po + 32]), rkx(p),
            op=ALU.mult))
        v_der = op(n, "ve", lambda e, po=po, p=p: e.tensor_tensor(
            v3d(n.dec_tok[:, po:po + 32]), v3d(n.dec_tok[:, po:po + 32]), rkx(p),
            op=ALU.mult), waits=[(sV, v_dec)], inc=sV, k=5)
        n.mark[("smalls", t, p)] = v_der
    if "gctok" in dbg and t == 0:
        dbg_dump(n, "gctok", n.gc_tok[:], [(sV, n.sV.n)])


def delta_prep(n, t, p):
    sT, sV, sE, sG = n.sT, n.sV, n.sE, n.sG
    P = n.P
    dbg = n.dbg
    po = p * 32
    po16 = p * 16

    n.growv = {}
    for g in range(8):
        wl = [(n.sV, n.v_gc)]
        if g >= 1:
            wl.append((sT, n.mark[("rowmm", t, p, g - 1)]))
        elif p > 0:
            wl.append((sT, n.mark[("rowmm", t, p - 1, 7)]))
        elif ("rowmm", t - 1, NPAIR - 1, 7) in n.mark:
            wl.append((sT, n.mark[("rowmm", t - 1, NPAIR - 1, 7)]))
        n.growv[g] = op(n, "sy", lambda e, g=g, p=p: e.dma_start(
            out=n.grow[0:1, 0:512],
            in_=n.gc_f[32 + g * 4: 36 + g * 4, p * 128:(p + 1) * 128]),
            waits=wl, inc=sG, k=16)
        group_prep(n, t, p, g)

    for g in range(8):
        for j in range(4):
            h = g * 4 + j
            wl = [(n.sE, n.mark[("silu", t, 32 + h)])]
            if g == 0 and j == 0:
                wl.append((sV, n.mark[("smalls", t, p)]))
            if j == 0 and g > 0:
                wl.append((sV, n.mark[("vT", t, p, g - 1)]))
            tv = op(n, "te", lambda e, h=h, j=j, p=p: e.transpose(
                P[6][:].bitcast(BF16)[:, j * 128:(j + 1) * 128],
                n.vc_sb[:, h * T + p * 128: h * T + p * 128 + 128], n.identc[:]),
                waits=wl, inc=sT)
        vv = op(n, "ve", lambda e, g=g: e.tensor_copy(
            n.vT_sb[:, g * 512:(g + 1) * 512], P[6][:].bitcast(BF16)[:, 0:512]),
            waits=[(sT, tv)], inc=sV)
        n.mark[("vT", t, p, g)] = vv
    for g in range(4):
        for j in range(4):
            kv = g * 4 + j
            wl = [(n.sE, n.mark[("silu", t, 16 + kv)])]
            if j == 0:
                wl.append((sV, n.mark[("vT", t, p, 7)] if g == 0
                           else n.mark[("kT", t, p, g - 1)]))
            tv = op(n, "te", lambda e, kv=kv, j=j, p=p: e.transpose(
                P[6][:].bitcast(BF16)[:, j * 128:(j + 1) * 128],
                n.k_sb[:, kv * T + p * 128: kv * T + p * 128 + 128], n.identc[:]),
                waits=wl, inc=sT)
        vv = op(n, "ve", lambda e, g=g: e.tensor_copy(
            n.kT_sb[:, g * 512:(g + 1) * 512], P[6][:].bitcast(BF16)[:, 0:512]),
            waits=[(sT, tv)], inc=sV)
        n.mark[("kT", t, p, g)] = vv


def group_prep(n, t, p, g):
    sT, sV, sE, sG = n.sT, n.sV, n.sE, n.sG
    P = n.P
    po = p * 32
    po16 = p * 16
    h0 = 4 * g
    dbg = n.dbg
    bD, bS, bP_, bQ_, bV_ = P[0], P[1], P[2], P[3], P[4]

    def colAP(tensor, off, reps=128):
        return tensor[:, off:off + 4].rearrange("p (a b) -> p a b", b=1)\
            .broadcast_to([128, 4, reps])

    def maskAP():
        return n.maskSc[:].rearrange("p (a f) -> p a f", a=1)\
            .broadcast_to([128, 4, 128])

    def identAP():
        return n.identc[:].rearrange("p (a f) -> p a f", a=1)\
            .broadcast_to([128, 4, 128])

    def kvrepAP(tensor, base):
        return tensor[:, base:base + 256].rearrange("p (a f) -> p a f", a=2)\
            .rearrange("p (a r) f -> p a r f", r=1).broadcast_to([128, 2, 2, 128])

    wl = [(sG, n.growv[g]), (sV, n.mark.get(("argdone", t, p, g - 1),
                                            n.mark.get(("smalls", t, p), 0)))]
    if g == 0:
        wl.append((sV, n.mark[("smalls", t, p)]))
    rv = op(n, "te", lambda e: e.matmul(
        bD[:, 0:512], n.ones1[0:1, :], n.grow[0:1, 0:512],
        start=True, stop=True), waits=wl, inc=sT)
    n.mark[("rowmm", t, p, g)] = rv

    kv0 = 2 * g
    wl2 = [(sE, n.mark[("silu", t, 16 + kv0)]), (sE, n.mark[("silu", t, 17 + kv0)]),
           (sE, n.mark[("silu", t, kv0)]), (sE, n.mark[("silu", t, 1 + kv0)]),
           (sV, n.mark.get(("scoresdone", t, p, g - 1), 0))]
    for j, kv in enumerate((kv0, kv0 + 1)):
        ksl = n.k_sb[:, kv * T + p * 128: kv * T + p * 128 + 128]
        qsl = n.q_sb[:, kv * T + p * 128: kv * T + p * 128 + 128]
        op(n, "te", lambda e, j=j, ksl=ksl: e.matmul(
            bS[:, j * 128:(j + 1) * 128], ksl, ksl,
            start=(j == 0), stop=False), waits=(wl2 if j == 0 else ()))
        sv = op(n, "te", lambda e, j=j, ksl=ksl, qsl=qsl: e.matmul(
            bS[:, (2 + j) * 128:(3 + j) * 128], ksl, qsl,
            start=False, stop=(j == 1)), inc=sT)
    n.mark[("scores", t, p, g)] = sv

    op(n, "ve", lambda e, po=po, h0=h0: e.tensor_tensor(
        n.t32a[:].rearrange("p (a f) -> p a f", a=4),
        bD[:, 0:512].rearrange("p (a f) -> p a f", a=4),
        colAP(n.gc_tok, po + h0), op=ALU.subtract), waits=[(sT, rv)])
    v_arg = op(n, "ve", lambda e: e.tensor_tensor(
        n.arg_sb[:].rearrange("p (a f) -> p a f", a=4),
        n.t32a[:].rearrange("p (a f) -> p a f", a=4),
        maskAP(), op=ALU.add), inc=sV, k=2)
    n.mark[("argdone", t, p, g)] = v_arg
    e_es = op(n, "se", lambda e: e.activation(n.es_sb[:], n.arg_sb[:], AF.Exp),
              waits=[(sV, v_arg)], inc=sE)
    v_ei = op(n, "ve", lambda e: e.tensor_tensor(
        n.ei_sb[:].rearrange("p (a f) -> p a f", a=4),
        n.es_sb[:].rearrange("p (a f) -> p a f", a=4),
        identAP(), op=ALU.add), waits=[(sE, e_es)], inc=sV)
    op(n, "ve", lambda e, g=g: e.tensor_tensor(
        n.t32b[:].rearrange("p (a b f) -> p a b f", a=2, b=2),
        kvrepAP(bS, 256),
        n.ei_sb[:].rearrange("p (a b f) -> p a b f", a=2, b=2), op=ALU.mult),
        waits=[(sT, sv), (sV, v_ei)])
    v_A = op(n, "ve", lambda e, p=p, h0=h0, po16=po16: e.tensor_tensor(
        n.A_sb[:, h0 * 128:(h0 + 4) * 128].rearrange("p (a b f) -> p a b f", a=2, b=2),
        n.t32b[:].rearrange("p (a b f) -> p a b f", a=2, b=2),
        n.rk_tok[:, po16 + h0 // 2: po16 + h0 // 2 + 2].rearrange(
            "p (a b c) -> p a b c", b=1, c=1).broadcast_to([128, 2, 2, 128]),
        op=ALU.mult), inc=sV, k=2)
    n.mark[("A", t, p, g)] = v_A
    op(n, "ve", lambda e, g=g: e.tensor_tensor(
        n.t32b[:].rearrange("p (a b f) -> p a b f", a=2, b=2),
        kvrepAP(bS, 0),
        n.es_sb[:].rearrange("p (a b f) -> p a b f", a=2, b=2), op=ALU.mult))
    v_pt = op(n, "ve", lambda e, po16=po16, h0=h0: e.tensor_tensor(
        n.ptil_sb[:].rearrange("p (a b f) -> p a b f", a=2, b=2),
        n.t32b[:].rearrange("p (a b f) -> p a b f", a=2, b=2),
        n.rk_tok[:, po16 + h0 // 2: po16 + h0 // 2 + 2].rearrange(
            "p (a b c) -> p a b c", b=1, c=1).broadcast_to([128, 2, 2, 128]),
        op=ALU.mult), inc=sV, k=2)
    n.mark[("scoresdone", t, p, g)] = v_pt

    def pw(i):
        return n.pw_sb[:, i * 512:(i + 1) * 512]

    for j in range(4):
        wlq1 = [(sV, v_pt)] if j == 0 else []
        if j == 0:
            wlq1.append((sV, n.mark.get(("pwdone", t, p, g - 1),
                                        n.mark.get(("pwdone", t, p - 1, 7),
                                                   n.mark.get(("pwdoneT", t - 1), 0)))))
        tv = op(n, "te", lambda e, j=j: e.transpose(
            bQ_[:].bitcast(BF16)[:, j * 128:(j + 1) * 128],
            n.ptil_sb[:, j * 128:(j + 1) * 128],
            n.identc[:]), waits=wlq1, inc=sT)
    v_q1 = op(n, "ve", lambda e, po=po, h0=h0: e.tensor_tensor(
        pw(0).rearrange("p (a f) -> p a f", a=4),
        bQ_[:].bitcast(BF16)[:, 0:512].rearrange("p (a f) -> p a f", a=4),
        colAP(n.brk_tok, po + h0), op=ALU.mult), waits=[(sT, tv)], inc=sV)
    for j in range(4):
        tv = op(n, "te", lambda e, j=j: e.transpose(
            bP_[:].bitcast(BF16)[:, j * 128:(j + 1) * 128],
            pw(0)[:, j * 128:(j + 1) * 128],
            n.identc[:]), waits=([(sV, v_q1)] if j == 0 else ()), inc=sT)
    v_p1 = op(n, "ve", lambda e: e.tensor_copy(pw(1), bP_[:].bitcast(BF16)[:, 0:512]),
              waits=[(sT, tv)], inc=sV)
    lvl = [(0, 1, 2, 3), (2, 3, 4, 5), (4, 5, 6, 7), (6, 7, 8, 9), (8, 9, 10, None)]
    prev_ev = v_p1
    for (qi, pi, qo, po_) in lvl:
        for j in range(4):
            tv = op(n, "te", lambda e, j=j, qi=qi, pi=pi: e.matmul(
                bQ_[:, j * 128:(j + 1) * 128], pw(pi)[:, j * 128:(j + 1) * 128],
                pw(qi)[:, j * 128:(j + 1) * 128], start=(j == 0), stop=(j == 3)),
                waits=([(sV, prev_ev)] if j == 0 else ()), inc=sT)
        v_q = op(n, "ve", lambda e, qo=qo: e.tensor_copy(pw(qo), bQ_[:, 0:512]),
                 waits=[(sT, tv)], inc=sV)
        prev_ev = v_q
        if po_ is not None:
            for j in range(4):
                tv = op(n, "te", lambda e, j=j, qo=qo: e.transpose(
                    bP_[:].bitcast(BF16)[:, j * 128:(j + 1) * 128],
                    pw(qo)[:, j * 128:(j + 1) * 128],
                    n.identc[:]), waits=([(sV, v_q)] if j == 0 else ()), inc=sT)
            prev_ev = op(n, "ve", lambda e, po_=po_: e.tensor_copy(
                pw(po_), bP_[:].bitcast(BF16)[:, 0:512]), waits=[(sT, tv)], inc=sV)

    n.mark[("pwdone", t, p, g)] = prev_ev
    if (p, g) == (NPAIR - 1, 7):
        n.mark[("pwdoneT", t)] = prev_ev
    v_v = op(n, "ve", lambda e: e.tensor_tensor(
        n.zz_sb[:].rearrange("p (a f) -> p a f", a=4),
        pw(10).rearrange("p (a f) -> p a f", a=4), identAP(), op=ALU.add),
        waits=[(sV, prev_ev)], inc=sV)
    for qi, last in ((8, 0), (6, 0), (4, 0), (2, 0), (0, 1)):
        for j in range(4):
            tv = op(n, "te", lambda e, j=j, qi=qi: e.matmul(
                bV_[:, j * 128:(j + 1) * 128], pw(qi)[:, j * 128:(j + 1) * 128],
                n.zz_sb[:, j * 128:(j + 1) * 128], start=(j == 0), stop=(j == 3)),
                waits=([(sV, v_v)] if j == 0 else ()), inc=sT)
        if not last:
            v_v = op(n, "ve", lambda e: e.tensor_add(
                n.zz_sb[:], n.zz_sb[:], bV_[:, 0:512]), waits=[(sT, tv)], inc=sV)
        else:
            v_v = op(n, "ve", lambda e, h0=h0: e.tensor_sub(
                n.V_sb[:, h0 * 128:(h0 + 4) * 128], n.zz_sb[:], bV_[:, 0:512]),
                waits=[(sT, tv)], inc=sV)
    n.mark[("V", t, p, g)] = v_v
    if "V0" in dbg and (t, p, g) == (0, 0, 0):
        dbg_dump(n, "V0", n.V_sb[:, 0:128], [(sV, v_v)])
    if "A0" in dbg and (t, p, g) == (0, 0, 0):
        dbg_dump(n, "A0", n.A_sb[:, 0:128], [(sV, n.mark[("A", t, p, g)])])


def delta_state(n, t, p, ci):
    sT, sV, sE = n.sT, n.sV, n.sE
    P = n.P
    dbg = n.dbg
    po = p * 32
    po16 = p * 16
    cb = ci * 64
    bK, bO1, bO2 = P[0], P[1], P[2]

    def colAP4(tensor, off, reps=128):
        return tensor[cb:cb + 64, off:off + 4].rearrange("p (a b) -> p a b", b=1)\
            .broadcast_to([64, 4, reps])

    def colAPf(tensor, off, reps=128):
        return tensor[:, off:off + 4].rearrange("p (a b) -> p a b", b=1)\
            .broadcast_to([128, 4, reps])

    def rqAP(pn=64):
        pass

    for g in range(8):
        h0 = 4 * g
        kv0 = 2 * g
        wlk = [(sV, n.mark[("V", t, p, 7)]),
               (sV, n.mark.get(("Sbready", t, p, ci, g), n.v_smallconst)),
               (sV, n.mark.get(("uev", t, p, ci, g - 1), 0))]
        tv = None
        for j in range(4):
            h = h0 + j
            kv = kv0 + j // 2
            tv = op(n, "te", lambda e, j=j, h=h, kv=kv, p=p, cb=cb: e.matmul(
                bK[cb:cb + 64, j * 128:(j + 1) * 128],
                n.k_sb[:, kv * T + p * 128 + cb: kv * T + p * 128 + cb + 64],
                n.Sb_sb[:, h * 128:(h + 1) * 128],
                start=(j == 0), stop=(j == 3)), waits=(wlk if j == 0 else ()),
                inc=sT)
        op(n, "ve", lambda e, po=po, h0=h0, cb=cb: e.tensor_tensor(
            n.t32a[cb:cb + 64, :].rearrange("p (a f) -> p a f", a=4),
            bK[cb:cb + 64, 0:512].rearrange("p (a f) -> p a f", a=4),
            colAP4(n.bgr_tok, po + h0), op=ALU.mult), waits=[(sT, tv)])
        op(n, "ve", lambda e, po=po, h0=h0, g=g, cb=cb: e.tensor_tensor(
            n.t32b[cb:cb + 64, :].rearrange("p (a f) -> p a f", a=4),
            n.vT_sb[cb:cb + 64, g * 512:(g + 1) * 512].rearrange(
                "p (a f) -> p a f", a=4),
            colAP4(n.beta_tok, po + h0), op=ALU.mult),
            waits=[(sV, n.mark[("vT", t, p, g)])])
        v_rhs = op(n, "ve", lambda e, cb=cb: e.tensor_sub(
            n.rhs_sb[cb:cb + 64, :], n.t32b[cb:cb + 64, :],
            n.t32a[cb:cb + 64, :]), inc=sV, k=3)
        n.mark[("rhsdone", t, p, ci, g)] = v_rhs
        tv = None
        for j in range(4):
            h = h0 + j
            tv = op(n, "te", lambda e, j=j, h=h, cb=cb: e.matmul(
                bK[cb:cb + 64, j * 128:(j + 1) * 128],
                n.V_sb[cb:cb + 64, h * 128 + cb: h * 128 + cb + 64],
                n.rhs_sb[cb:cb + 64, j * 128:(j + 1) * 128],
                start=(j == 0), stop=(j == 3)),
                waits=([(sV, v_rhs)] if j == 0 else ()), inc=sT)
        n.mark[("ujmm", t, p, ci, g)] = tv
        v_u = op(n, "ve", lambda e, cb=cb: e.tensor_copy(
            n.u_sb[cb:cb + 64, :], bK[cb:cb + 64, 0:512]), waits=[(sT, tv)], inc=sV)
        n.mark[("uev", t, p, ci, g)] = v_u
        if "u0" in dbg and (t, p, ci, g) == (0, 0, 0, 0):
            dbg_dump(n, "u0", n.u_sb[0:64, :], [(sV, v_u)])
        wlq = [(sV, n.mark.get(("coreg", t, p, ci, g - 1), 0))]
        tvq = None
        for j in range(4):
            h = h0 + j
            kv = kv0 + j // 2
            tvq = op(n, "te", lambda e, j=j, h=h, kv=kv, p=p, cb=cb: e.matmul(
                bO1[cb:cb + 64, j * 128:(j + 1) * 128],
                n.q_sb[:, kv * T + p * 128 + cb: kv * T + p * 128 + cb + 64],
                n.Sb_sb[:, h * 128:(h + 1) * 128],
                start=(j == 0), stop=(j == 3)), waits=(wlq if j == 0 else ()),
                inc=sT)
        wla = [(sV, v_u), (sV, n.mark[("A", t, p, g)]),
               (sV, n.mark.get(("Sbready", t, p, ci + 1, g - 1), 0))]
        tva = None
        for j in range(4):
            h = h0 + j
            tva = op(n, "te", lambda e, j=j, h=h, cb=cb: e.matmul(
                bO2[cb:cb + 64, j * 128:(j + 1) * 128],
                n.A_sb[cb:cb + 64, h * 128 + cb: h * 128 + cb + 64],
                n.u_sb[cb:cb + 64, j * 128:(j + 1) * 128],
                start=(j == 0), stop=(j == 3)),
                waits=(wla if j == 0 else ()), inc=sT)
        v_o1 = op(n, "ve", lambda e, po=po, h0=h0, cb=cb: e.tensor_tensor(
            n.t32a[cb:cb + 64, :].rearrange("p (a f) -> p a f", a=4),
            bO1[cb:cb + 64, 0:512].rearrange("p (a f) -> p a f", a=4),
            colAP4(n.grq_tok, po + h0), op=ALU.mult), waits=[(sT, tvq)], inc=sV)
        n.mark[("oe1", t, p, ci, g)] = v_o1
        v_o2 = op(n, "ve", lambda e, po16=po16, h0=h0, cb=cb: e.tensor_tensor(
            n.t32b[cb:cb + 64, :].rearrange("p (a b f) -> p a b f", a=2, b=2),
            bO2[cb:cb + 64, 0:512].rearrange("p (a b f) -> p a b f", a=2, b=2),
            n.rq_tok[cb:cb + 64, po16 + h0 // 2: po16 + h0 // 2 + 2].rearrange(
                "p (a b c) -> p a b c", b=1, c=1).broadcast_to([64, 2, 2, 128]),
            op=ALU.mult), waits=[(sT, tva)], inc=sV)
        n.mark[("oe2", t, p, ci, g)] = v_o2
        v_os = op(n, "ve", lambda e, cb=cb: e.tensor_add(
            n.o1_sb[cb:cb + 64, :], n.t32a[cb:cb + 64, :], n.t32b[cb:cb + 64, :]),
            inc=sV)
        op(n, "ve", lambda e, cb=cb: e.tensor_mul(
            n.on_sb[cb:cb + 64, :], n.o1_sb[cb:cb + 64, :], n.o1_sb[cb:cb + 64, :]))
        op(n, "ve", lambda e, cb=cb: e.reduce_sum(
            n.lncol[cb:cb + 64, 8:12],
            n.on_sb[cb:cb + 64, :].rearrange("p (a f) -> p a f", a=4), axis=AXL.X))
        op(n, "ve", lambda e, cb=cb: e.tensor_scalar(
            n.lncol[cb:cb + 64, 8:12], n.lncol[cb:cb + 64, 8:12],
            1.0 / DV, RMS_EPS, op0=ALU.mult, op1=ALU.add))
        v_rr = op(n, "ve", lambda e, cb=cb: e.reciprocal(
            n.lncol[cb:cb + 64, 8:12], n.lncol[cb:cb + 64, 8:12]), inc=sV, k=4)
        e_sq = op(n, "se", lambda e, cb=cb: e.activation(
            n.lncol[cb:cb + 64, 12:16], n.lncol[cb:cb + 64, 8:12], AF.Sqrt),
            waits=[(sV, v_rr)], inc=sE)
        v_on = op(n, "ve", lambda e, cb=cb: e.tensor_tensor(
            n.on_sb[cb:cb + 64, :].rearrange("p (a f) -> p a f", a=4),
            n.o1_sb[cb:cb + 64, :].rearrange("p (a f) -> p a f", a=4),
            n.lncol[cb:cb + 64, 12:16].rearrange("p (a b) -> p a b", b=1)
                .broadcast_to([64, 4, 128]), op=ALU.mult),
            waits=[(sE, e_sq)], inc=sV)
        tvc = None
        for j in range(4):
            tvc = op(n, "te", lambda e, j=j, cb=cb: e.transpose(
                bO1[:].bitcast(BF16)[:, j * 64:(j + 1) * 64],
                n.on_sb[cb:cb + 64, j * 128:(j + 1) * 128],
                n.identc[cb:cb + 64, cb:cb + 64]),
                waits=([(sV, v_on)] if j == 0 else ()), inc=sT)
        zoff = p * 128 + cb

        def vc_view(e_, h0=h0, zoff=zoff):
            return n.vc_sb[:, h0 * T:(h0 + 4) * T].rearrange(
                "p (a f) -> p a f", a=4)[:, :, zoff:zoff + 64]

        def z_view(e_, h0=h0, zoff=zoff):
            return n.z_sb[:, h0 * T:(h0 + 4) * T].rearrange(
                "p (a f) -> p a f", a=4)[:, :, zoff:zoff + 64]

        v_cg = op(n, "ve", lambda e, h0=h0, zoff=zoff: e.scalar_tensor_tensor(
            vc_view(e, h0, zoff),
            bO1[:].bitcast(BF16)[:, 0:256].rearrange("p (a f) -> p a f", a=4),
            n.nw[:, 0:1], z_view(e, h0, zoff),
            op0=ALU.mult, op1=ALU.mult),
            waits=[(sT, tvc), (sE, n.mark[("zev", t, 31)])],
            inc=sV)
        n.mark[("coreg", t, p, ci, g)] = v_cg
        v_kh = op(n, "ve", lambda e, g=g, po=po, h0=h0, cb=cb: e.tensor_tensor(
            n.kh_sb[cb:cb + 64, :].rearrange("p (a b f) -> p a b f", a=2, b=2),
            n.kT_sb[cb:cb + 64, g * 256:(g + 1) * 256].rearrange(
                "p (a r f) -> p a r f", a=2, r=1).broadcast_to([64, 2, 2, 128]),
            n.dec_tok[cb:cb + 64, po + h0: po + h0 + 4].rearrange(
                "p (a b c) -> p a b c", b=2, c=1).broadcast_to([64, 2, 2, 128]),
            op=ALU.mult), waits=[(sV, n.mark[("kT", t, p, g // 2)])], inc=sV)
        tvd = None
        for j in range(4):
            h = h0 + j
            tvd = op(n, "te", lambda e, j=j, cb=cb: e.matmul(
                bO2[:, j * 128:(j + 1) * 128],
                n.kh_sb[cb:cb + 64, j * 128:(j + 1) * 128],
                n.u_sb[cb:cb + 64, j * 128:(j + 1) * 128],
                start=(j == 0), stop=(j == 3)),
                waits=([(sV, v_kh), (sV, v_o2)] if j == 0 else ()), inc=sT)
        gl_off = (p * 2 + ci) * 32 + h0
        op(n, "ve", lambda e, gl_off=gl_off, h0=h0: e.tensor_tensor(
            n.t32a[:].rearrange("p (a f) -> p a f", a=4),
            n.S_sb[:, h0 * 128:(h0 + 4) * 128].rearrange("p (a f) -> p a f", a=4),
            colAPf(n.glast_bc, gl_off), op=ALU.mult), waits=[(sT, tvd)])
        op(n, "ve", lambda e, h0=h0: e.tensor_add(
            n.S_sb[:, h0 * 128:(h0 + 4) * 128], n.t32a[:], bO2[:, 0:512]))
        v_sb = op(n, "ve", lambda e, h0=h0: e.tensor_copy(
            n.Sb_sb[:, h0 * 128:(h0 + 4) * 128],
            n.S_sb[:, h0 * 128:(h0 + 4) * 128]), inc=sV, k=3)
        n.mark[("Sbready", t, p, ci + 1, g)] = v_sb
        if ci == 1:
            if p == NPAIR - 1:
                n.mark[("Sbready", t + 1, 0, 0, g)] = v_sb
            else:
                n.mark[("Sbready", t, p + 1, 0, g)] = v_sb
    n.mark[("SbT", t, p, ci + 1)] = sV.n
    n.mark[("statevec", t)] = sV.n
    if "S0" in dbg and (t, p, ci) == (0, NPAIR - 1, 1):
        dbg_dump(n, "S0", n.S_sb[:], [(sV, n.sV.n)])
    if "core" in dbg and (t, p, ci) == (0, NPAIR - 1, 1):
        dbg_dump(n, "core", n.vc_sb[:], [(sV, n.sV.n)])


def wout_phase(n, t):
    sT, sV, sE = n.sT, n.sV, n.sE
    P = n.P
    ws = n.ws
    par = t % 2
    xoff = par * (T // 128) * D
    v_core = n.mark[("statevec", t)]

    tv = None
    for h in range(HV):
        half, wv = ws.load(n.wo_b[h * 128:(h + 1) * 128, :],
                           waits=([(sV, v_core)] if h == 0 else ()))
        for tc in range(T // 128):
            for nb in range(4):
                bank = P[tc * 4 + nb]
                wl = []
                if h == 0 and nb == 0:
                    wl = [wv, (sV, v_core),
                          (sV, n.mark.get(("woutev", t - 1), 0))]
                elif nb == 0 and tc == 0:
                    wl = [wv]
                tv = op(n, "te", lambda e, h=h, tc=tc, nb=nb, half=half, bank=bank:
                        e.matmul(
                            bank[:, 0:512],
                            n.vc_sb[:, h * T + tc * 128: h * T + tc * 128 + 128],
                            n.wbuf[:, half * 2048 + nb * 512:
                                   half * 2048 + (nb + 1) * 512],
                            start=(h == 0), stop=(h == HV - 1)),
                        waits=wl, inc=sT)
        ws.done(tv)
    n.mark[("woutmm", t)] = tv
    v = None
    for tc in range(T // 128):
        for nb in range(4):
            bank = P[tc * 4 + nb]
            xs = n.x_sb[:, xoff + tc * D + nb * 512: xoff + tc * D + (nb + 1) * 512]
            v = op(n, "ve", lambda e, bank=bank, xs=xs: e.tensor_add(
                xs, xs, bank[:, 0:512]), waits=([(sT, tv)] if (tc, nb) == (0, 0)
                                                else ()), inc=sV)
    n.mark[("woutev", t)] = v
    if "x1" in n.dbg and t == 0:
        dbg_dump(n, "x1", n.x_sb[:, xoff:xoff + (T // 128) * D], [(n.sV, v)])


def ffn_phase(n, t):
    sT, sV, sE, sO = n.sT, n.sV, n.sE, n.sO
    P = n.P
    ws = n.ws
    par = t % 2
    xoff = par * (T // 128) * D
    v_res = n.mark[("woutev", t)]

    store_vals = []
    prev_vxq = v_res
    for tc in range(T // 128):
        xs = n.x_sb[:, xoff + tc * D: xoff + (tc + 1) * D]
        op(n, "ve", lambda e, xs=xs: e.reduce_sum(n.lncol[:, 0:1], xs, axis=AXL.X),
           waits=[(sV, v_res)])
        op(n, "ve", lambda e: e.tensor_scalar_mul(n.lncol[:, 2:3], n.lncol[:, 0:1],
                                                  1.0 / D))
        v_m = op(n, "ve", lambda e: e.tensor_mul(
            n.lncol[:, 4:5], n.lncol[:, 2:3], n.lncol[:, 2:3]), inc=sV, k=3)
        e_sq = op(n, "se", lambda e, xs=xs: e.activation(
            n.ffnq[:], xs, AF.Square, accum_out=n.lncol[:, 1:2]),
            waits=[(sV, prev_vxq)], inc=sE)
        op(n, "ve", lambda e: e.tensor_scalar_mul(n.lncol[:, 3:4], n.lncol[:, 1:2],
                                                  1.0 / D), waits=[(sE, e_sq)])
        op(n, "ve", lambda e: e.tensor_sub(n.lncol[:, 5:6], n.lncol[:, 3:4],
                                           n.lncol[:, 4:5]))
        op(n, "ve", lambda e: e.tensor_scalar_add(n.lncol[:, 6:7], n.lncol[:, 5:6],
                                                  LN_EPS))
        v_rc = op(n, "ve", lambda e: e.reciprocal(n.lncol[:, 7:8], n.lncol[:, 6:7]),
                  inc=sV, k=3)
        e_st = op(n, "se", lambda e: e.activation(n.lncol[:, 8:9], n.lncol[:, 7:8],
                                                  AF.Sqrt), waits=[(sV, v_rc)],
                  inc=sE)
        op(n, "ve", lambda e: e.tensor_mul(n.lncol[:, 9:10], n.lncol[:, 5:6],
                                           n.lncol[:, 7:8]), waits=[(sE, e_st)])
        op(n, "ve", lambda e: e.tensor_scalar_add(n.lncol[:, 9:10],
                                                  n.lncol[:, 9:10], NRM_EPS))
        v_r2 = op(n, "ve", lambda e: e.reciprocal(n.lncol[:, 10:11],
                                                  n.lncol[:, 9:10]), inc=sV, k=2)
        e_r2 = op(n, "se", lambda e: e.activation(n.lncol[:, 11:12],
                                                  n.lncol[:, 10:11], AF.Sqrt),
                  waits=[(sV, v_r2)], inc=sE)
        op(n, "ve", lambda e, xs=xs: e.tensor_scalar(
            n.ffnq[:], xs, n.lncol[:, 2:3], n.lncol[:, 8:9],
            op0=ALU.subtract, op1=ALU.mult), waits=[(sE, e_r2)])
        v_u2 = op(n, "ve", lambda e: e.tensor_mul(n.ffnq[:], n.ffnq[:],
                                                  n.gamma_bc[:]), inc=sV, k=2)
        op(n, "ve", lambda e: e.reduce_max(n.lncol[:, 12:13], n.ffnq[:], axis=AXL.X,
                                           apply_absolute_value=True))
        op(n, "ve", lambda e: e.tensor_mul(n.lncol[:, 12:13], n.lncol[:, 12:13],
                                           n.lncol[:, 11:12]))
        op(n, "ve", lambda e: e.tensor_scalar_add(n.lncol[:, 13:14],
                                                  n.lncol[:, 12:13], QEPS))
        v_am = op(n, "ve", lambda e: e.reciprocal(n.lncol[:, 14:15],
                                                  n.lncol[:, 13:14]), inc=sV, k=4)
        op(n, "ve", lambda e: e.tensor_scalar_mul(n.lncol[:, 14:15],
                                                  n.lncol[:, 14:15], 127.0),
           waits=[(sV, v_am)])
        v_qs = op(n, "ve", lambda e: e.tensor_mul(n.lncol[:, 14:15],
                                                  n.lncol[:, 14:15],
                                                  n.lncol[:, 11:12]), inc=sV, k=2)
        op(n, "ve", lambda e: e.tensor_scalar_mul(n.ffnq[:], n.ffnq[:],
                                                  n.lncol[:, 14:15]),
           waits=[(sV, v_qs)])
        op(n, "ve", lambda e: e.tensor_scalar(n.ffnq[:], n.ffnq[:], MAGIC, MAGIC,
                                              op0=ALU.add, op1=ALU.subtract))
        v_xq = op(n, "ve", lambda e: e.tensor_scalar(
            n.xq_tok[:], n.ffnq[:], 127.0, -128.0, op0=ALU.min, op1=ALU.max),
            inc=sV, k=3)
        prev_vxq = v_xq
        op(n, "ve", lambda e: e.tensor_mul(n.lncol[:, 15:16], n.lncol[:, 13:14],
                                           n.mw_bc[:, 0:1]))
        v_osc = op(n, "ve", lambda e: e.tensor_scalar_mul(
            n.lncol[:, 15:16], n.lncol[:, 15:16], 1.0 / 127.0), inc=sV, k=2)
        for kk in range(NKC):
            bank = P[kk % 2]
            if kk < 2:
                wl = [(sV, v_xq), (sV, n.mark.get(("xqev", kk - 2 + NKC), 0))]
            else:
                wl = [(sV, n.mark[("xqev", kk - 2)])]
            tv = op(n, "te", lambda e, kk=kk, bank=bank: e.transpose(
                bank[:].bitcast(BF16)[:, 0:128],
                n.xq_tok[:, kk * 128:(kk + 1) * 128], n.identc[:]),
                waits=wl, inc=sT)
            n.mark[("xqT", t)] = tv
            n.mark[("xqev", kk)] = op(
                n, "ve", lambda e, kk=kk, tc=tc, bank=bank: e.tensor_copy(
                    n.xn_f[:, kk * T + tc * 128: kk * T + tc * 128 + 128],
                    bank[:].bitcast(BF16)[:, 0:128]), waits=[(sT, tv)], inc=sV)
            if kk >= NKC - 2:
                n.mark[("xqev", kk + NKC)] = n.mark[("xqev", kk)]
        v_xqf = n.sV.n
        for ob in range(4):
            bank = P[4 + ob % 2]
            tv = None
            for kq in range(4):
                half, wv = ws.load(
                    n.wf_b[kq * 512:(kq + 1) * 512, ob * 512:(ob + 1) * 512]
                    .rearrange("(k p) c -> p k c", p=128))
                for ki in range(4):
                    kk = kq * 4 + ki
                    wl = []
                    if ki == 0:
                        wl = [wv]
                        if kq == 0:
                            wl.append((sV, v_xqf))
                            wl.append((sV, n.mark.get(("ffnev", t, tc, ob - 2),
                                      n.mark.get(("ffnev", t, tc - 1, 2 + ob), 0))))
                    tv = op(n, "te", lambda e, kk=kk, ki=ki, kq=kq, tc=tc,
                            half=half, bank=bank: e.matmul(
                                bank[:, 0:512],
                                n.xn_f[:, kk * T + tc * 128: kk * T + tc * 128 + 128],
                                n.wbuf[:, half * 2048 + ki * 512:
                                       half * 2048 + (ki + 1) * 512],
                                start=(kk == 0), stop=(kk == NKC - 1)),
                            waits=wl, inc=sT)
                ws.done(tv)
            xs2 = n.x_sb[:, xoff + tc * D + ob * 512: xoff + tc * D + (ob + 1) * 512]
            n.mark[("ffnev", t, tc, ob)] = op(
                n, "ve", lambda e, bank=bank, xs2=xs2: e.scalar_tensor_tensor(
                    xs2, bank[:, 0:512], n.lncol[:, 15:16], xs2,
                    op0=ALU.mult, op1=ALU.add),
                waits=[(sT, tv), (sV, v_osc)], inc=sV)
        v_ffn = n.sV.n
        sv = op(n, "sy", lambda e, t=t, tc=tc, xs=xs: e.dma_start(
            out=n.out_d[t * T + tc * 128: t * T + (tc + 1) * 128, :], in_=xs),
            waits=[(sV, v_ffn)], inc=sO, k=16)
        store_vals.append(sv)
    n.mark[("ffnq_free", t)] = n.sV.n
    n.mark[("ffndone", t)] = n.sV.n
    n.store_v[t] = store_vals[-1]


def make_consts():
    ident = np.eye(128, dtype=BF)
    jj = np.arange(128)[:, None]
    tt = np.arange(128)[None, :]
    same = (jj // 64) == (tt // 64)
    strict = same & ((tt % 64) > (jj % 64))
    maskS = np.where(strict, 0.0, NEG).astype(np.float32)
    onehot = np.zeros((128, 16 * 16), dtype=BF)
    for h in range(16):
        onehot[:, h * 16 + h] = 1.0
    return ident, maskS, onehot


def kernel(**inputs):
    x = np.ascontiguousarray(inputs["L0_attn_hidden_states"])
    ident, maskS, onehot = make_consts()
    base = {
        "w_qkv": np.ascontiguousarray(inputs["L0_attn_w_qkv"]),
        "w_z": np.ascontiguousarray(inputs["L0_attn_w_z"]),
        "w_b": np.ascontiguousarray(inputs["L0_attn_w_b"]),
        "w_a": np.ascontiguousarray(inputs["L0_attn_w_a"]),
        "w_out": np.ascontiguousarray(inputs["L0_attn_w_out"]),
        "conv_w": np.ascontiguousarray(inputs["L0_attn_conv_w"]),
        "dt_bias": np.ascontiguousarray(inputs["L0_attn_dt_bias"]).reshape(HV, 1),
        "A_log": np.ascontiguousarray(inputs["L0_attn_A_log"]).reshape(HV, 1),
        "norm_w": np.ascontiguousarray(inputs["L0_attn_norm_weight"]).reshape(DV, 1),
        "gamma": np.ascontiguousarray(inputs["L0_ffn_gamma"]).reshape(1, D),
        "W_ffn": np.ascontiguousarray(inputs["L0_ffn_W"]),
        "ident": ident, "maskS": maskS, "onehot": onehot,
    }
    nc = build_program()
    in_maps = [dict(base, x=np.ascontiguousarray(x[i])) for i in range(8)]
    res = run_bass_kernel_spmd(nc, in_maps, core_ids=list(range(8)))
    return np.stack([res.results[i]["out"] for i in range(8)], axis=0)


# version 03
# speedup vs baseline: 1.2282x; 1.2282x over previous
import contextlib
import numpy as np
import ml_dtypes
import concourse.bass as bass
import concourse.mybir as mybir
from concourse.bass_utils import run_bass_kernel_spmd

F32 = mybir.dt.float32
BF16 = mybir.dt.bfloat16
AF = mybir.ActivationFunctionType
ALU = mybir.AluOpType
AXL = mybir.AxisListType
BF = ml_dtypes.bfloat16

S, D = 4096, 2048
HK, HV, DK, DV = 16, 32, 128, 128
KDIM, VDIM = HK * DK, HV * DV
CONV_DIM = 2 * KDIM + VDIM
KSIZE = 4
T = 256
NT = S // T
NPAIR = T // 128
NKC = D // 128
LN_EPS, NRM_EPS, RMS_EPS, QEPS = 1e-5, 1e-6, 1e-6, 1e-5
MAGIC = float(np.float32(1.5 * 2 ** 23))
NEG = -1e30

DEBUG_SHAPES = {
    "xn": (128, NKC * T), "q": (128, HK * T), "k": (128, HK * T),
    "v": (128, HV * T), "z": (128, HV * T), "g": (32, T), "gc": (32, T),
    "beta": (32, T), "rsq": (16, 2 * T), "A0": (128, 128), "V0": (128, 128),
    "core": (128, HV * T), "x1": (128, (T // 128) * D), "S0": (128, HV * DV),
    "gctok": (128, NPAIR * 32), "u0": (64, 512),
    "lncol": (128, 16), "xt": (128, 2048), "xsb": (128, 4096),
    "idc": (128, 128), "xn_b": (128, NKC * T),
}


class Sem:
    def __init__(self, handle):
        self.h = handle
        self.n = 0

    def inc(self, k=1):
        self.n += k
        return self.n


class NS:
    pass


ORDER_EDGES = False
TRUNC = None
AUTOCHAIN = ("ve", "se", "gp")


def op(n, eng, emit, waits=(), inc=None, k=1, dma=None):
    val = inc.inc(k) if inc is not None else None
    wl = tuple((s.h, v) for (s, v) in waits if v > 0)
    ih = inc.h if inc is not None else None
    oe = None
    is_dma = (eng == "sy") if dma is None else dma
    if ORDER_EDGES or (eng in AUTOCHAIN and not is_dma):
        key = (eng, "d" if is_dma else "c")
        owl = []
        if n.olast.get(key):
            owl.append(n.olast[key])
        if is_dma and eng == "gp" and n.olast.get(("gp", "c")):
            owl.append(n.olast[("gp", "c")])
        if ih is not None:
            n.olast[key] = (ih, val)
            oe = (tuple(owl), None, 0)
        else:
            os_ = n.osem[key]
            step = 16 if is_dma else 1
            os_.inc(step)
            n.olast[key] = (os_.h, os_.n)
            oe = (tuple(owl), os_.h, step)

    def f(e, emit=emit, wl=wl, ih=ih, k=k, oe=oe):
        if oe is not None:
            for h, v in oe[0]:
                e.wait_ge(h, v)
        for h, v in wl:
            e.wait_ge(h, v)
        ins = emit(e)
        if ih is not None:
            ins.then_inc(ih, k)
        if oe is not None and oe[1] is not None:
            ins.then_inc(oe[1], oe[2])
    n.ops[eng].append(f)
    return val


def build_program(debug=()):
    nc = bass.Bass("TRN2", target_bir_lowering=False,
                   detect_race_conditions=False, debug=bool(debug))
    dbg = set(debug)
    n = NS()
    n.nc = nc
    n.dbg = dbg

    def dram(name, shape, dt, kind):
        return nc.dram_tensor(name, list(shape), dt, kind=kind).ap()

    n.x_d = dram("x", (S, D), F32, "ExternalInput")
    n.wqkv_d = dram("w_qkv", (D, CONV_DIM), F32, "ExternalInput")
    n.wz_d = dram("w_z", (D, VDIM), F32, "ExternalInput")
    n.wb_d = dram("w_b", (D, HV), F32, "ExternalInput")
    n.wa_d = dram("w_a", (D, HV), F32, "ExternalInput")
    n.wo_d = dram("w_out", (VDIM, D), F32, "ExternalInput")
    n.convw_d = dram("conv_w", (CONV_DIM, KSIZE), F32, "ExternalInput")
    n.dtb_d = dram("dt_bias", (HV, 1), F32, "ExternalInput")
    n.alog_d = dram("A_log", (HV, 1), F32, "ExternalInput")
    n.nw_d = dram("norm_w", (DV, 1), F32, "ExternalInput")
    n.gam_d = dram("gamma", (1, D), F32, "ExternalInput")
    n.wffn_d = dram("W_ffn", (D, D), F32, "ExternalInput")
    n.ident_d = dram("ident", (128, 128), BF16, "ExternalInput")
    n.maskS_d = dram("maskS", (128, 128), F32, "ExternalInput")
    n.oneh_d = dram("onehot", (128, 16 * 16), BF16, "ExternalInput")
    n.out_d = dram("out", (S, D), F32, "ExternalOutput")
    n.wqkv_b = dram("wqkv_b", (D, CONV_DIM), BF16, "Internal")
    n.wz_b = dram("wz_b", (D, VDIM), BF16, "Internal")
    n.wba_b = dram("wba_b", (D, 2 * HV), BF16, "Internal")
    n.wo_b = dram("wo_b", (VDIM, D), BF16, "Internal")
    n.wf_b = dram("wf_b", (D, D), BF16, "Internal")
    n.dbg_d = {nm: dram("dbg_" + nm, DEBUG_SHAPES[nm],
                        BF16 if nm.endswith("_b") else F32, "ExternalOutput")
               for nm in dbg}

    est = contextlib.ExitStack()
    sbt = lambda nm, shp, dt: est.enter_context(nc.sbuf_tensor(nm, list(shp), dt))
    pst = lambda nm: est.enter_context(nc.psum_tensor(nm, [128, 512], F32))
    mks = lambda nm: Sem(est.enter_context(nc.semaphore(nm)))
    n.osem = {}
    n.olast = {}
    okeys = [("ve", "c"), ("se", "c"), ("gp", "c")]
    if ORDER_EDGES:
        okeys += [("sy", "d"), ("gp", "d"), ("te", "c")]
    for ek, ck in okeys:
        n.osem[(ek, ck)] = mks(f"ord_{ek}_{ck}")

    with est:
        n.x_sb = sbt("x_sb", (128, 2 * (T // 128) * D), F32)
        n.xn_f = sbt("xn_f", (128, NKC * T), BF16)
        n.q_sb = sbt("q_sb", (128, HK * T), BF16)
        n.k_sb = sbt("k_sb", (128, HK * T), BF16)
        n.vc_sb = sbt("vc_sb", (128, HV * T), BF16)
        n.z_sb = sbt("z_sb", (128, HV * T), BF16)
        n.A_sb = sbt("A_sb", (128, HV * 128), BF16)
        n.V_sb = sbt("V_sb", (128, HV * 128), BF16)
        n.vT_sb = sbt("vT_sb", (128, HV * 128), BF16)
        n.kT_sb = sbt("kT_sb", (128, HK * 128), BF16)
        n.S_sb = sbt("S_sb", (128, HV * DV), F32)
        n.Sb_sb = sbt("Sb_sb", (128, HV * DV), BF16)
        n.pw_sb = sbt("pw_sb", (128, 11 * 512), BF16)
        n.ptil_sb = sbt("ptil_sb", (128, 512), BF16)
        n.zz_sb = sbt("zz_sb", (128, 512), BF16)
        n.u_sb = sbt("u_sb", (128, 512), BF16)
        n.rhs_sb = sbt("rhs_sb", (128, 512), BF16)
        n.kh_sb = sbt("kh_sb", (128, 512), BF16)
        n.es_sb = sbt("es_sb", (128, 512), BF16)
        n.ei_sb = sbt("ei_sb", (128, 512), BF16)
        n.o1_sb = sbt("o1_sb", (128, 512), BF16)
        n.on_sb = sbt("on_sb", (128, 512), BF16)
        n.t32a = sbt("t32a", (128, 512), F32)
        n.t32b = sbt("t32b", (128, 512), F32)
        n.arg_sb = sbt("arg_sb", (128, 512), F32)
        n.wbuf = sbt("wbuf", (128, 2 * 2048), BF16)
        n.cbuf = sbt("cbuf", (128, T + 4), BF16)
        n.ta_sb = sbt("ta_sb", (128, T), BF16)
        n.tb_sb = sbt("tb_sb", (128, T), BF16)
        n.tc_sb = sbt("tc_sb", (128, T), BF16)
        n.td_sb = sbt("td_sb", (128, T), BF16)
        n.cv_sb = sbt("cv_sb", (128, T), BF16)
        n.qsq_sb = sbt("qsq_sb", (128, T), BF16)
        n.tails = sbt("tails", (128, 64 * 4), BF16)
        n.convw = sbt("convw", (128, 64 * KSIZE), F32)
        n.g_f = sbt("g_f", (64, T), F32)
        n.gc_f = sbt("gc_f", (64, T), F32)
        n.sm1 = None
        n.sm2 = None
        n.sm3 = None
        n.beta_f = sbt("beta_f", (64, T), F32)
        n.sm1 = n.t32a[0:64, 0:256]
        n.sm2 = n.t32a[0:64, 256:512]
        n.sm3 = n.t32b[0:64, 0:256]
        n.ba_f = n.t32b[0:64, 256:512]
        n.onessm = sbt("onessm", (64, 128), F32)
        n.gc_tok = sbt("gc_tok", (128, NPAIR * 32), F32)
        n.beta_tok = sbt("beta_tok", (128, NPAIR * 32), F32)
        n.gam_tok = sbt("gam_tok", (128, NPAIR * 32), F32)
        n.dec_tok = sbt("dec_tok", (128, NPAIR * 32), F32)
        n.glast_bc = sbt("glast_bc", (128, NPAIR * 2 * 32), F32)
        n.bgr_tok = sbt("bgr_tok", (128, NPAIR * 32), F32)
        n.grq_tok = sbt("grq_tok", (128, NPAIR * 32), F32)
        n.rq_tok = sbt("rq_tok", (128, NPAIR * 16), F32)
        n.rk_tok = sbt("rk_tok", (128, NPAIR * 16), F32)
        n.brk_tok = sbt("brk_tok", (128, NPAIR * 32), F32)
        n.rsq_f = sbt("rsq_f", (16, 2 * T), F32)
        n.grow = sbt("grow", (1, 512), F32)
        n.lncol = sbt("lncol", (128, 16), F32)
        n.gamma_bc = sbt("gamma_bc", (128, D), BF16)
        n.ffnq = sbt("ffnq", (128, D), F32)
        n.xt_tok = None
        n.xq_tok = sbt("xq_tok", (128, D), BF16)
        n.identc = sbt("identc", (128, 128), BF16)
        n.identf = sbt("identf", (64, 64), F32)
        n.maskSc = sbt("maskSc", (128, 128), F32)
        n.onehc = sbt("onehc", (128, 16 * 16), BF16)
        n.ones1 = sbt("ones1", (1, 128), F32)
        n.onescol = sbt("onescol", (128, 1), F32)
        n.dtb = sbt("dtb", (64, 1), F32)
        n.nA = sbt("nA", (64, 1), F32)
        n.nw = sbt("nw", (128, 1), F32)
        n.mw_bc = sbt("mw_bc", (128, 1), F32)
        n.wsc = sbt("wsc", (128, 1), F32)
        n.xt_tok = n.xq_tok
        n.setup1 = n.ffnq
        n.setup2 = n.xq_tok
        n.wabs = sbt("wabs", (128, 4), F32)

        n.P = [pst("pb%d" % i) for i in range(8)]

        n.sW0, n.sW1, n.sG, n.sO = mks("sW0"), mks("sW1"), mks("sG"), mks("sO")
        n.sXa, n.sXb, n.sC = mks("sXa"), mks("sXb"), mks("sC")
        n.sT, n.sV, n.sE, n.sP = mks("sT"), mks("sV"), mks("sE"), mks("sP")
        n.sQ = mks("sQ")
        n.sD = mks("sD")
        n.ops = {e: [] for e in ("te", "ve", "se", "gp", "sy")}

        schedule(n)

        blk = est.enter_context(nc.Block())

        @blk.sync
        def _(eng):
            for f in n.ops["sy"]:
                f(eng)

        @blk.tensor
        def _(eng):
            for f in n.ops["te"]:
                f(eng)

        @blk.vector
        def _(eng):
            for f in n.ops["ve"]:
                f(eng)

        @blk.scalar
        def _(eng):
            for f in n.ops["se"]:
                f(eng)

        @blk.gpsimd
        def _(eng):
            for f in n.ops["gp"]:
                f(eng)

    return nc


def schedule(n):
    sT, sV, sE, sP = n.sT, n.sV, n.sE, n.sP
    sG, sO, sC = n.sG, n.sO, n.sC

    setup(n)
    for t in range(NT):
        tile_body(n, t)
    endw = [(sO, sO.n)]
    if n.dbg:
        endw.append((n.sD, n.sD.n))
    op(n, "gp", lambda e: e.engine_nop(), waits=endw)


def setup(n):
    sT, sV, sE, sP = n.sT, n.sV, n.sE, n.sP
    sG, sO, sC = n.sG, n.sO, n.sC
    sXa = n.sXa
    P = n.P

    def syd(out, in_, waits=()):
        return op(n, "sy", lambda e, o=out, i=in_: e.dma_start(out=o, in_=i),
                  waits=waits, inc=sC, k=16)

    def gpd(out, in_, waits=()):
        return op(n, "gp", lambda e, o=out, i=in_: e.dma_start(out=o, in_=i),
                  waits=waits, inc=sP, k=16, dma=True)

    syd(n.identc[:], n.ident_d)
    syd(n.maskSc[:], n.maskS_d)
    syd(n.onehc[:], n.oneh_d)
    syd(n.dtb[32:64, :], n.dtb_d)
    syd(n.nA[32:64, :], n.alog_d)
    syd(n.nw[:], n.nw_d)
    syd(n.setup1[0:1, 0:D], n.gam_d)
    v_cw = syd(n.convw[:], n.convw_d.rearrange("(c p) s -> p c s", p=128))
    n.v_convw = v_cw
    v_consts = sC.n
    v_gam = v_consts

    n.xload_v = {}
    v0 = op(n, "sy", lambda e: e.dma_start(
        out=n.x_sb[:, 0:(T // 128) * D],
        in_=n.x_d[0:T, :].rearrange("(a p) d -> p a d", p=128)),
        inc=n.sXa, k=16)
    n.xload_v[0] = (n.sXa, v0)

    op(n, "ve", lambda e: e.memset(n.ones1[:], 1.0), waits=[(sC, v_consts)])
    op(n, "ve", lambda e: e.memset(n.onescol[:], 1.0))
    op(n, "ve", lambda e: e.memset(n.onessm[:], 1.0))
    op(n, "ve", lambda e: e.memset(n.S_sb[:], 0.0))
    op(n, "ve", lambda e: e.memset(n.Sb_sb[:], 0.0))
    op(n, "ve", lambda e: e.memset(n.tails[:], 0.0))
    v_idf = op(n, "gp", lambda e: e.dma_start(out=n.identf[:],
                                              in_=n.ident_d[0:64, 0:64]),
               inc=sP, k=16, dma=True)
    n.v_idf = v_idf

    v_e = op(n, "se", lambda e: e.activation(n.nA[32:64, :], n.nA[32:64, :], AF.Exp),
             waits=[(sC, v_consts)], inc=sE)
    v_na = op(n, "ve", lambda e: e.tensor_scalar_mul(n.nA[32:64, :], n.nA[32:64, :], -1.0),
              waits=[(sE, v_e)], inc=sV)
    n.v_smallconst = v_na

    for i in range(4):
        tv = op(n, "te", lambda e, i=i: e.matmul(
            P[0][:, 0:512], n.ones1[0:1, :], n.setup1[0:1, i * 512:(i + 1) * 512],
            start=True, stop=True),
            waits=([(sC, v_gam), (sV, v_na)] if i == 0 else
                   [(sV, n.sV.n)]), inc=sT)
        op(n, "ve", lambda e, i=i: e.tensor_copy(
            n.gamma_bc[:, i * 512:(i + 1) * 512], P[0][:, 0:512]),
            waits=[(sT, tv)], inc=sV)
    n.v_gammm = sT.n

    for i in range(8):
        gpd(n.wqkv_b[i * 256:(i + 1) * 256, :], n.wqkv_d[i * 256:(i + 1) * 256, :])
    for i in range(4):
        gpd(n.wz_b[i * 512:(i + 1) * 512, :], n.wz_d[i * 512:(i + 1) * 512, :])
    gpd(n.wba_b[:, 0:HV], n.wb_d)
    gpd(n.wba_b[:, HV:2 * HV], n.wa_d)
    for i in range(16):
        gpd(n.wo_b[i * 256:(i + 1) * 256, :], n.wo_d[i * 256:(i + 1) * 256, :])
    n.v_wcast = sP.n

    v_z = op(n, "ve", lambda e: e.memset(n.wabs[:, 0:1], 0.0), inc=sV)
    prev_use = 0
    for i in range(16):
        lv = syd(n.setup1[:], n.wffn_d[i * 128:(i + 1) * 128, :],
                 waits=([(sV, prev_use)] if i else [(sT, n.v_gammm)]))
        op(n, "ve", lambda e: e.reduce_sum(n.wabs[:, 1:2], n.setup1[:], axis=AXL.X,
                                           apply_absolute_value=True),
           waits=[(sC, lv)])
        prev_use = op(n, "ve", lambda e: e.tensor_add(
            n.wabs[:, 0:1], n.wabs[:, 0:1], n.wabs[:, 1:2]), inc=sV, k=2)
    tv = op(n, "te", lambda e: e.matmul(P[1][0:1, 0:1], n.onescol[:],
                                        n.wabs[:, 0:1], start=True, stop=True),
            waits=[(sV, prev_use)], inc=sT)
    op(n, "ve", lambda e: e.tensor_scalar(n.wabs[0:1, 2:3], P[1][0:1, 0:1],
                                          1.0 / (D * D), QEPS,
                                          op0=ALU.mult, op1=ALU.add),
       waits=[(sT, tv)])
    v_r = op(n, "ve", lambda e: e.reciprocal(n.wabs[0:1, 3:4], n.wabs[0:1, 2:3]),
             inc=sV, k=2)
    tv = op(n, "te", lambda e: e.matmul(P[1][:, 4:5], n.ones1[0:1, :],
                                        n.wabs[0:1, 3:4], start=True, stop=True),
            waits=[(sV, v_r)])
    tv = op(n, "te", lambda e: e.matmul(P[1][:, 5:6], n.ones1[0:1, :],
                                        n.wabs[0:1, 2:3], start=True, stop=True),
            inc=sT)
    op(n, "ve", lambda e: e.tensor_copy(n.wsc[:], P[1][:, 4:5]), waits=[(sT, tv)])
    v_wsc = op(n, "ve", lambda e: e.tensor_copy(n.mw_bc[:], P[1][:, 5:6]), inc=sV)

    prev_gp = 0
    prev_ve = v_wsc
    for i in range(16):
        lv = syd(n.setup1[:], n.wffn_d[i * 128:(i + 1) * 128, :],
                 waits=[(sV, prev_ve)])
        op(n, "ve", lambda e: e.tensor_scalar_mul(n.ffnq[:], n.setup1[:],
                                                  n.wsc[:, 0:1]),
           waits=[(sC, lv), (sV, v_wsc)])
        op(n, "ve", lambda e: e.tensor_scalar(n.ffnq[:], n.ffnq[:], MAGIC, MAGIC,
                                              op0=ALU.add, op1=ALU.subtract))
        qv = op(n, "ve", lambda e: e.tensor_scalar(n.setup2[:], n.ffnq[:], 1.0, -1.0,
                                                   op0=ALU.min, op1=ALU.max),
                inc=sV, k=3)
        prev_ve = qv
        for k in range(16):
            tv = op(n, "te", lambda e, k=k: e.transpose(
                P[2][:].bitcast(BF16)[:, (k % 4) * 128:(k % 4) * 128 + 128],
                n.setup2[:, k * 128:(k + 1) * 128], n.identc[:]),
                waits=([(sV, qv), (sP, prev_gp)] if k == 0 else
                       ([(sV, sV.n)] if k % 4 == 0 else [])), inc=sT)
            if k % 4 == 3:
                prev_ve = op(n, "ve", lambda e, k=k: e.tensor_copy(
                    n.wbuf[:, (k - 3) * 128:(k + 1) * 128],
                    P[2][:].bitcast(BF16)[:, 0:512]),
                    waits=[(sT, tv)], inc=sV)
        prev_gp = gpd(n.wf_b[:, i * 128:(i + 1) * 128].rearrange(
            "(a p) c -> p a c", p=128), n.wbuf[:, 0:2048],
            waits=[(sV, prev_ve)])
    n.v_wffn = sP.n
    n.v_setupgp = sP.n
    n.v_setupT = sT.n
    n.setup_sC = sC.n
    n.mark = {}
    n.mark[("ffnq_free", -1)] = sV.n


class WStream:
    def __init__(self, n):
        self.n = n
        self.idx = 0
        self.consumed = []

    def load(self, dram_ap, waits=(), cols=2048):
        n = self.n
        half = self.idx % 2
        sem = n.sW0 if half == 0 else n.sW1
        wl = list(waits)
        if self.idx >= 2:
            wl.append((n.sT, self.consumed[self.idx - 2]))
        v = op(n, "sy", lambda e, o=n.wbuf[:, half * 2048:half * 2048 + cols],
               i=dram_ap: e.dma_start(out=o, in_=i), waits=wl, inc=sem, k=16)
        self.idx += 1
        return half, (sem, v)

    def done(self, sT_val):
        self.consumed.append(sT_val)


def tile_body(n, t):
    sT, sV, sE, sP = n.sT, n.sV, n.sE, n.sP
    sG, sO, sC = n.sG, n.sO, n.sC
    P = n.P
    par = t % 2
    xoff = par * (T // 128) * D
    if not hasattr(n, "ws"):
        n.ws = WStream(n)
        n.store_v = {}
    ws = n.ws
    dbg = n.dbg

    if t + 1 < NT:
        npar = (t + 1) % 2
        sem = n.sXa if npar == 0 else n.sXb
        wl = []
        if t >= 1:
            wl.append((sO, n.store_v[t - 1]))
        vx = op(
            n, "sy", lambda e, t=t, npar=npar: e.dma_start(
                out=n.x_sb[:, npar * (T // 128) * D:(npar + 1) * (T // 128) * D],
                in_=n.x_d[(t + 1) * T:(t + 2) * T, :].rearrange(
                    "(a p) d -> p a d", p=128)),
            waits=wl, inc=sem, k=16)
        n.xload_v[t + 1] = (sem, vx)

    xn_guard = n.mark.get(("projdone", t - 1), 0)
    for tc in range(T // 128):
        xs = n.x_sb[:, xoff + tc * D: xoff + (tc + 1) * D]
        wl = [n.xload_v[t]] + ([(sV, n.mark[("ffnq_free", t - 1)])]
                                     if ("ffnq_free", t - 1) in n.mark else [])
        op(n, "ve", lambda e, xs=xs: e.reduce_sum(n.lncol[:, 0:1], xs, axis=AXL.X),
           waits=wl)
        v_red = op(n, "ve", lambda e, xs=xs: e.tensor_scalar_mul(
            n.lncol[:, 2:3], n.lncol[:, 0:1], 1.0 / D), inc=sV)
        e_sq = op(n, "se", lambda e, xs=xs: e.activation(
            n.ffnq[:], xs, AF.Square, accum_out=n.lncol[:, 1:2]),
            waits=[n.xload_v[t]] + ([(sV, n.mark[("ffnq_free", t - 1)])]
                                          if ("ffnq_free", t - 1) in n.mark else []),
            inc=sE)
        op(n, "ve", lambda e: e.tensor_scalar_mul(n.lncol[:, 3:4], n.lncol[:, 1:2],
                                                  1.0 / D), waits=[(sE, e_sq)])
        op(n, "ve", lambda e: e.tensor_mul(n.lncol[:, 4:5], n.lncol[:, 2:3],
                                           n.lncol[:, 2:3]))
        op(n, "ve", lambda e: e.tensor_sub(n.lncol[:, 5:6], n.lncol[:, 3:4],
                                           n.lncol[:, 4:5]))
        op(n, "ve", lambda e: e.tensor_scalar_add(n.lncol[:, 6:7], n.lncol[:, 5:6],
                                                  LN_EPS))
        v_rc = op(n, "ve", lambda e: e.reciprocal(n.lncol[:, 7:8], n.lncol[:, 6:7]),
                  inc=sV)
        e_st = op(n, "se", lambda e: e.activation(n.lncol[:, 8:9], n.lncol[:, 7:8],
                                                  AF.Sqrt), waits=[(sV, v_rc)],
                  inc=sE)
        v_xn = op(n, "ve", lambda e, xs=xs: e.tensor_scalar(
            n.xt_tok[:], xs, n.lncol[:, 2:3], n.lncol[:, 8:9],
            op0=ALU.subtract, op1=ALU.mult),
            waits=[(sE, e_st),
                   (sT, n.mark.get(("xqT", t - 1), n.v_setupT if t == 0 else 0))],
            inc=sV)
        for kk in range(NKC):
            bank = P[kk % 2]
            if kk < 2:
                wl = [(sV, v_xn), (sV, n.mark.get(("xnev", kk - 2), 0)),
                      (sV, n.mark.get(("ffndone", t - 1), 0))]
            else:
                wl = [(sV, n.mark[("xnev", kk - 2)])]
            tv = op(n, "te", lambda e, kk=kk, bank=bank: e.transpose(
                bank[:].bitcast(BF16)[:, 0:128],
                n.xt_tok[:, kk * 128:(kk + 1) * 128], n.identc[:]),
                waits=wl, inc=sT)
            n.mark[("xnev", kk)] = op(
                n, "ve", lambda e, kk=kk, tc=tc, bank=bank: e.tensor_copy(
                    n.xn_f[:, kk * T + tc * 128: kk * T + tc * 128 + 128],
                    bank[:].bitcast(BF16)[:, 0:128]), waits=[(sT, tv)], inc=sV)
    v_xnf = sV.n
    if "xn" in dbg and t == 0:
        dbg_dump(n, "xn", n.xn_f[:], [(sV, v_xnf)])
    if "xn_b" in dbg and t == 0:
        dbg_dump(n, "xn_b", n.xn_f[:], [(sV, v_xnf)])
    if t == 0:
        if "lncol" in dbg:
            dbg_dump(n, "lncol", n.lncol[:], [(sV, v_xnf)])
        if "xt" in dbg:
            dbg_dump(n, "xt", n.xt_tok[:], [(sV, v_xnf)])
        if "xsb" in dbg:
            dbg_dump(n, "xsb", n.x_sb[:, 0:2 * D], [(sV, v_xnf)])
        if "idc" in dbg:
            dbg_dump(n, "idc", n.identc[:], [(sV, v_xnf)])
    if TRUNC == "ln1":
        return

    projection_phase(n, t, v_xnf, xn_guard)
    if TRUNC == "proj":
        return

    smalls_phase(n, t)
    if TRUNC == "smalls":
        return

    for p in range(NPAIR):
        delta_prep(n, t, p)
        for ci in range(2):
            delta_state(n, t, p, ci)
    n.mark[("gathers", t)] = n.sG.n
    if TRUNC == "delta":
        return

    wout_phase(n, t)
    if TRUNC == "wout":
        return

    ffn_phase(n, t)


def dbg_dump(n, name, src_ap, waits):
    op(n, "gp", lambda e, o=n.dbg_d[name], i=src_ap: e.dma_start(out=o, in_=i),
       waits=waits, inc=n.sD, k=16, dma=True)


def projection_phase(n, t, v_xnf, xn_guard):
    sT, sV, sE, sP = n.sT, n.sV, n.sE, n.sP
    sO = n.sO
    P = n.P
    ws = n.ws
    dbg = n.dbg

    vc_guard = n.mark.get(("statevec", t - 1), 0)

    n.qk_ready = {}
    sumsq_mm = 0
    for oc in range(64):
        half, wv = ws.load(
            n.wqkv_b[:, oc * 128:(oc + 1) * 128].rearrange("(k p) c -> p k c", p=128),
            waits=[(sP, n.v_setupgp)] if (t == 0 and oc == 0) else ())
        bank = P[oc % 4]
        tv = None
        for kk in range(NKC):
            wl = []
            if kk == 0:
                wl = [wv, (sV, v_xnf)]
                if oc < 4:
                    wl.append((sV, n.mark.get(("ffndone", t - 1), 0)))
                else:
                    wl.append((sV, n.mark[("convev", t, oc - 4)]))
            tv = op(n, "te", lambda e, oc=oc, kk=kk, half=half, bank=bank: e.matmul(
                bank[:, 0:256], n.wbuf[:, half * 2048 + kk * 128: half * 2048 + (kk + 1) * 128],
                n.xn_f[:, kk * T:(kk + 1) * T],
                start=(kk == 0), stop=(kk == NKC - 1)), waits=wl, inc=sT)
        ws.done(tv)
        conv_emit(n, t, oc, bank, tv, vc_guard)
    for oc in range(32):
        half, wv = ws.load(
            n.wz_b[:, oc * 128:(oc + 1) * 128].rearrange("(k p) c -> p k c", p=128))
        bank = P[4 + oc % 2]
        tv = None
        for kk in range(NKC):
            wl = []
            if kk == 0:
                if oc < 2:
                    wl = [wv, (sV, n.mark.get(("ffndone", t - 1), 0)),
                          (sV, vc_guard)]
                else:
                    wl = [wv, (sE, n.mark[("zev", t, oc - 2)])]
            tv = op(n, "te", lambda e, oc=oc, kk=kk, half=half, bank=bank: e.matmul(
                bank[:, 0:256], n.wbuf[:, half * 2048 + kk * 128: half * 2048 + (kk + 1) * 128],
                n.xn_f[:, kk * T:(kk + 1) * T],
                start=(kk == 0), stop=(kk == NKC - 1)), waits=wl, inc=sT)
        ws.done(tv)
        n.mark[("zev", t, oc)] = op(
            n, "se", lambda e, oc=oc, bank=bank: e.activation(
                n.z_sb[:, oc * T:(oc + 1) * T], bank[:, 0:256], AF.Silu),
            waits=[(sT, tv)], inc=sE)
    if "z" in dbg and t == 0:
        dbg_dump(n, "z", n.z_sb[:], [(sE, n.sE.n)])

    half, wv = ws.load(n.wba_b.rearrange("(k p) c -> p k c", p=128), cols=1024)
    bank = P[6]
    tv = None
    for kk in range(NKC):
        wl = ([wv, (sV, n.mark.get(("smalls", t - 1, NPAIR - 1), 0)),
               (sE, n.mark.get(("sigb", t - 1), 0))] if kk == 0 else [])
        tv = op(n, "te", lambda e, kk=kk, half=half: e.matmul(
            bank[0:64, 0:256], n.wbuf[:, half * 2048 + kk * 64: half * 2048 + (kk + 1) * 64],
            n.xn_f[:, kk * T:(kk + 1) * T],
            start=(kk == 0), stop=(kk == NKC - 1)), waits=wl, inc=sT)
    ws.done(tv)
    n.v_ba = op(n, "ve", lambda e, bank=bank: e.tensor_copy(
        n.ba_f[:], bank[0:64, 0:256]), waits=[(sT, tv)], inc=sV)
    n.mark[("projdone", t)] = sT.n

    l2_finish(n, t)


def conv_emit(n, t, oc, bank, tv_mm, vc_guard):
    sT, sV, sE, sP = n.sT, n.sV, n.sE, n.sP
    sQ = n.sQ
    P = n.P
    dbg = n.dbg
    w = n.convw
    gv = op(n, "gp", lambda e, oc=oc: e.tensor_copy(
        n.cbuf[:, 1:4], n.tails[:, oc * 4 + 1: oc * 4 + 4]),
        waits=[(sV, n.mark.get(("convdone", t, oc - 1),
                               n.mark.get(("convdone", t - 1, 63),
                                          n.v_smallconst)))], inc=sQ)
    ev = op(n, "ve", lambda e, bank=bank: e.tensor_copy(n.cbuf[:, 4:4 + T],
                                                        bank[:, 0:256]),
            waits=[(sT, tv_mm),
                   (sQ, n.mark.get(("convgp", t, oc - 1),
                                   n.mark.get(("convgp", t - 1, 63), 0)))],
            inc=sV)
    n.mark[("convev", t, oc)] = ev
    gv2 = op(n, "gp", lambda e, oc=oc: e.tensor_copy(
        n.tails[:, oc * 4 + 1: oc * 4 + 4], n.cbuf[:, 1 + T: 4 + T]),
        waits=[(sV, ev)], inc=sQ)
    n.mark[("convgp", t, oc)] = gv2
    op(n, "ve", lambda e, oc=oc: e.tensor_scalar_mul(
        n.cv_sb[:], n.cbuf[:, 4:4 + T], w[:, oc * 4 + 3: oc * 4 + 4]),
        waits=[(sQ, gv),
               (n.sE, n.mark.get(("silu", t, oc - 1),
                                 n.mark.get(("silu", t - 1, 63), 0)))])
    v3 = None
    for s in (2, 1, 0):
        v3 = op(n, "ve", lambda e, oc=oc, s=s: e.scalar_tensor_tensor(
            n.cv_sb[:], n.cbuf[:, 1 + s: 1 + s + T],
            w[:, oc * 4 + s: oc * 4 + s + 1], n.cv_sb[:],
            op0=ALU.mult, op1=ALU.add), inc=(sV if s == 0 else None))
    n.mark[("convdone", t, oc)] = v3
    if oc < 16:
        dst = n.q_sb[:, oc * T:(oc + 1) * T]
    elif oc < 32:
        dst = n.k_sb[:, (oc - 16) * T:(oc - 15) * T]
    else:
        dst = n.vc_sb[:, (oc - 32) * T:(oc - 31) * T]
    ev2 = op(n, "se", lambda e, dst=dst: e.activation(dst, n.cv_sb[:], AF.Silu),
             waits=[(sV, v3), (sT, n.mark.get(("woutmm", t - 1), 0))], inc=sE)
    n.mark[("silu", t, oc)] = ev2
    if oc < 32:
        v4 = op(n, "ve", lambda e, dst=dst: e.tensor_mul(n.qsq_sb[:], dst, dst),
                waits=[(sE, ev2)], inc=sV)
        h = oc % 16
        is_q = oc < 16
        colr = slice(0, 256) if is_q else slice(256, 512)
        wl5 = [(sV, v4)]
        if oc == 0:
            wl5.append((sE, n.mark.get(("rsqdone", t - 1), 0)))
        tv = op(n, "te", lambda e, h=h, colr=colr: e.matmul(
            n.P[7][0:16, colr], n.onehc[:, h * 16:(h + 1) * 16], n.qsq_sb[:],
            start=(oc == 0), stop=(oc == 31)), waits=wl5, inc=sT)
        n.mark[("l2mm", t, oc)] = tv


def l2_finish(n, t):
    sT, sV, sE = n.sT, n.sV, n.sE
    P = n.P
    dbg = n.dbg
    tv = n.mark[("l2mm", t, 31)]
    op(n, "ve", lambda e: e.tensor_scalar(n.rsq_f[:, 0:256], P[7][0:16, 0:256],
                                          float(DK), float(DK) * NRM_EPS,
                                          op0=ALU.mult, op1=ALU.add),
       waits=[(sT, tv), (sT, n.mark.get(("rsqT", t - 1), 0))])
    op(n, "ve", lambda e: e.tensor_scalar(n.rsq_f[:, 256:512], P[7][0:16, 256:512],
                                          1.0, NRM_EPS, op0=ALU.mult, op1=ALU.add))
    v1 = op(n, "ve", lambda e: e.reciprocal(n.rsq_f[:], n.rsq_f[:]), inc=sV, k=3)
    e1 = op(n, "se", lambda e: e.activation(n.rsq_f[:], n.rsq_f[:], AF.Sqrt),
            waits=[(sV, v1)], inc=sE)
    n.v_rsq = e1
    n.mark[("rsqdone", t)] = e1
    if "rsq" in dbg and t == 0:
        dbg_dump(n, "rsq", n.rsq_f[:], [(sE, e1)])


def smalls_phase(n, t):
    sT, sV, sE = n.sT, n.sV, n.sE
    P = n.P
    dbg = n.dbg
    v_ba = n.v_ba

    e_b = op(n, "se", lambda e: e.activation(n.beta_f[0:32, :], n.ba_f[0:32, :],
                                             AF.Sigmoid), waits=[(sV, v_ba)], inc=sE)
    n.mark[("sigb", t)] = e_b
    op(n, "ve", lambda e: e.tensor_scalar_add(n.sm1[32:64, :], n.ba_f[32:64, :],
                                              n.dtb[32:64, 0:1]), waits=[(sV, v_ba)])
    v_u = op(n, "ve", lambda e: e.tensor_scalar(n.sm2[32:64, :], n.sm1[32:64, :],
                                                0.0, 0.0, op0=ALU.max, op1=ALU.bypass),
             inc=sV, k=2)
    e_a = op(n, "se", lambda e: e.activation(n.sm3[32:64, :], n.sm1[32:64, :], AF.Abs),
             waits=[(sV, v_u)], inc=sE)
    e_e = op(n, "se", lambda e: e.activation(n.sm3[32:64, :], n.sm3[32:64, :],
                                             AF.Exp, scale=-1.0), inc=sE)
    e_l = op(n, "se", lambda e: e.activation(n.sm3[32:64, :], n.sm3[32:64, :],
                                             AF.Ln, bias=1.0), inc=sE)
    op(n, "ve", lambda e: e.tensor_add(n.sm2[32:64, :], n.sm2[32:64, :],
                                       n.sm3[32:64, :]), waits=[(sE, e_l)])
    v_g = op(n, "ve", lambda e: e.tensor_scalar_mul(n.g_f[32:64, :], n.sm2[32:64, :],
                                                    n.nA[32:64, 0:1]), inc=sV, k=2)
    v_gc = None
    for c in range(T // 64):
        wlc = [(sV, v_g)] if c == 0 else []
        if c == 0:
            wlc.append((n.sG, n.mark.get(("gathers", t - 1), 0)))
        v_gc = op(n, "ve", lambda e, c=c: e.tensor_tensor_scan(
            n.gc_f[32:64, c * 64:(c + 1) * 64], n.g_f[32:64, c * 64:(c + 1) * 64],
            n.g_f[32:64, c * 64:(c + 1) * 64], 0.0, op0=ALU.add, op1=ALU.bypass),
            waits=wlc, inc=sV)
    n.v_gc = v_gc
    if "gc" in dbg and t == 0:
        dbg_dump(n, "gc", n.gc_f[32:64, :], [(sV, v_gc)])
    if "beta" in dbg and t == 0:
        dbg_dump(n, "beta", n.beta_f[0:32, :], [(sE, e_b)])
    if "g" in dbg and t == 0:
        dbg_dump(n, "g", n.g_f[32:64, :], [(sV, v_g)])

    sv_prev = n.mark.get(("statevec", t - 1), 0)
    prev_t4 = 0
    prev_vgl = sv_prev
    prev_vgct = prev_egam = prev_vbt = prev_vdec = prev_vrk = 0
    for p in range(NPAIR):
        po = p * 32
        t1 = op(n, "te", lambda e, p=p: e.transpose(
            P[5][0:128, 0:32], n.gc_f[32:64, p * 128:(p + 1) * 128],
            n.identf[32:64, 32:64]),
            waits=[(sV, v_gc), (sV, sv_prev), (sV, prev_vgct),
                   (sE, prev_egam if p else n.mark.get(("egam", t - 1), 0))],
            inc=sT)
        op(n, "ve", lambda e, po=po: e.tensor_copy(n.gc_tok[:, po:po + 32],
                                                   P[5][0:128, 0:32]),
           waits=[(sT, t1)])
        v_gct = op(n, "ve", lambda e, po=po: e.tensor_copy(
            n.t32a[:, 0:32], P[5][0:128, 0:32]), inc=sV, k=2)
        e_gam = op(n, "se", lambda e, po=po: e.activation(
            n.gam_tok[:, po:po + 32], P[5][0:128, 0:32], AF.Exp),
            waits=[(sV, v_gct)], inc=sE)
        n.mark[("egam", t)] = e_gam
        prev_vgct, prev_egam = v_gct, e_gam
        t2 = op(n, "te", lambda e, p=p: e.transpose(
            P[5][0:128, 32:64], n.beta_f[0:32, p * 128:(p + 1) * 128],
            n.identf[0:32, 0:32]),
            waits=[(sE, e_b), (sE, e_gam), (sV, prev_vbt)], inc=sT)
        v_bt = op(n, "ve", lambda e, po=po: e.tensor_copy(
            n.beta_tok[:, po:po + 32], P[5][0:128, 32:64]), waits=[(sT, t2)], inc=sV)
        prev_vbt = v_bt
        for c in range(2):
            cc = p * 128 + c * 64 + 63
            op(n, "ve", lambda e, c=c, cc=cc: e.tensor_scalar_mul(
                n.sm1[32:64, c * 64:(c + 1) * 64], n.onessm[32:64, 0:64],
                n.gc_f[32:64, cc:cc + 1]), waits=[(sV, v_gc), (sT, prev_t4)])
        v_glb = op(n, "ve", lambda e, p=p: e.tensor_sub(
            n.sm2[32:64, 0:128], n.sm1[32:64, 0:128],
            n.gc_f[32:64, p * 128:(p + 1) * 128]), inc=sV, k=3)
        e_d = op(n, "se", lambda e: e.activation(n.sm3[32:64, 0:128],
                                                 n.sm2[32:64, 0:128], AF.Exp),
                 waits=[(sV, v_glb)], inc=sE)
        t3 = op(n, "te", lambda e: e.transpose(P[5][0:128, 64:96],
                                               n.sm3[32:64, 0:128],
                                               n.identf[32:64, 32:64]),
                waits=[(sE, e_d), (sV, prev_vdec)], inc=sT)
        v_dec = op(n, "ve", lambda e, po=po: e.tensor_copy(
            n.dec_tok[:, po:po + 32], P[5][0:128, 64:96]),
            waits=[(sT, t3)], inc=sV)
        prev_vdec = v_dec
        for c in range(2):
            cc = p * 128 + c * 64 + 63
            ts_gl = op(n, "ve", lambda e, cc=cc: e.tensor_scalar_mul(
                n.sm1[32:64, 0:128], n.onessm[32:64, 0:128],
                n.gc_f[32:64, cc:cc + 1]),
                waits=[(sV, v_gc), (sT, prev_t4)], inc=sV)
            e_g2 = op(n, "se", lambda e: e.activation(
                n.sm1[32:64, 0:128], n.sm1[32:64, 0:128], AF.Exp),
                waits=[(sV, ts_gl)], inc=sE)
            t4 = op(n, "te", lambda e: e.transpose(
                P[5][0:128, 96:128], n.sm1[32:64, 0:128],
                n.identf[32:64, 32:64]),
                waits=[(sE, e_g2), (sV, prev_vgl)], inc=sT)
            prev_t4 = t4
            v_gl = op(n, "ve", lambda e, po=po, c=c, p=p: e.tensor_copy(
                n.glast_bc[:, (p * 2 + c) * 32:(p * 2 + c) * 32 + 32],
                P[5][0:128, 96:128]), waits=[(sT, t4)], inc=sV)
            prev_vgl = v_gl
        t5 = op(n, "te", lambda e, p=p: e.transpose(
            P[6][0:128, 0:16], n.rsq_f[0:16, p * 128:(p + 1) * 128],
            n.identf[0:16, 0:16]),
            waits=[(sE, n.v_rsq), (sV, sv_prev), (sV, prev_vrk)], inc=sT)
        t6 = op(n, "te", lambda e, p=p: e.transpose(
            P[6][0:128, 16:32], n.rsq_f[0:16, 256 + p * 128: 256 + (p + 1) * 128],
            n.identf[0:16, 0:16]), inc=sT)
        n.mark[("rsqT", t)] = t6
        po16 = p * 16
        op(n, "ve", lambda e, po16=po16: e.tensor_copy(
            n.rq_tok[:, po16:po16 + 16], P[6][0:128, 0:16]), waits=[(sT, sT.n)])
        v_rk = op(n, "ve", lambda e, po16=po16: e.tensor_copy(
            n.rk_tok[:, po16:po16 + 16], P[6][0:128, 16:32]), inc=sV, k=2)
        prev_vrk = v_rk
        def rkx(pp):
            return n.rk_tok[:, pp * 16:(pp + 1) * 16].rearrange(
                "p (a b) -> p a b", b=1).broadcast_to([128, 16, 2])

        def v3d(ap):
            return ap.rearrange("p (a b) -> p a b", b=2)
        op(n, "ve", lambda e, po=po: e.tensor_mul(
            n.t32a[:, 0:32], n.beta_tok[:, po:po + 32], n.gam_tok[:, po:po + 32]),
            waits=[(sV, v_bt), (sV, v_rk), (sE, e_gam)])
        op(n, "ve", lambda e, po=po, p=p: e.tensor_tensor(
            v3d(n.bgr_tok[:, po:po + 32]), v3d(n.t32a[:, 0:32]), rkx(p),
            op=ALU.mult))
        op(n, "ve", lambda e, po=po, p=p: e.tensor_tensor(
            v3d(n.grq_tok[:, po:po + 32]), v3d(n.gam_tok[:, po:po + 32]),
            n.rq_tok[:, p * 16:(p + 1) * 16].rearrange(
                "p (a b) -> p a b", b=1).broadcast_to([128, 16, 2]),
            op=ALU.mult))
        op(n, "ve", lambda e, po=po, p=p: e.tensor_tensor(
            v3d(n.brk_tok[:, po:po + 32]), v3d(n.beta_tok[:, po:po + 32]), rkx(p),
            op=ALU.mult))
        v_der = op(n, "ve", lambda e, po=po, p=p: e.tensor_tensor(
            v3d(n.dec_tok[:, po:po + 32]), v3d(n.dec_tok[:, po:po + 32]), rkx(p),
            op=ALU.mult), waits=[(sV, v_dec)], inc=sV, k=5)
        n.mark[("smalls", t, p)] = v_der
    if "gctok" in dbg and t == 0:
        dbg_dump(n, "gctok", n.gc_tok[:], [(sV, n.sV.n)])


def delta_prep(n, t, p):
    sT, sV, sE, sG = n.sT, n.sV, n.sE, n.sG
    P = n.P
    dbg = n.dbg
    po = p * 32
    po16 = p * 16

    n.growv = {}
    for g in range(8):
        wl = [(n.sV, n.v_gc)]
        if g >= 1:
            wl.append((sT, n.mark[("rowmm", t, p, g - 1)]))
        elif p > 0:
            wl.append((sT, n.mark[("rowmm", t, p - 1, 7)]))
        elif ("rowmm", t - 1, NPAIR - 1, 7) in n.mark:
            wl.append((sT, n.mark[("rowmm", t - 1, NPAIR - 1, 7)]))
        n.growv[g] = op(n, "sy", lambda e, g=g, p=p: e.dma_start(
            out=n.grow[0:1, 0:512],
            in_=n.gc_f[32 + g * 4: 36 + g * 4, p * 128:(p + 1) * 128]),
            waits=wl, inc=sG, k=16)
        group_prep(n, t, p, g)

    for g in range(8):
        for j in range(4):
            h = g * 4 + j
            wl = [(n.sE, n.mark[("silu", t, 32 + h)])]
            if g == 0 and j == 0:
                wl.append((sV, n.mark[("smalls", t, p)]))
            if j == 0 and g > 0:
                wl.append((sV, n.mark[("vT", t, p, g - 1)]))
            tv = op(n, "te", lambda e, h=h, j=j, p=p: e.transpose(
                P[6][:].bitcast(BF16)[:, j * 128:(j + 1) * 128],
                n.vc_sb[:, h * T + p * 128: h * T + p * 128 + 128], n.identc[:]),
                waits=wl, inc=sT)
        vv = op(n, "ve", lambda e, g=g: e.tensor_copy(
            n.vT_sb[:, g * 512:(g + 1) * 512], P[6][:].bitcast(BF16)[:, 0:512]),
            waits=[(sT, tv)], inc=sV)
        n.mark[("vT", t, p, g)] = vv
    for g in range(4):
        for j in range(4):
            kv = g * 4 + j
            wl = [(n.sE, n.mark[("silu", t, 16 + kv)])]
            if j == 0:
                wl.append((sV, n.mark[("vT", t, p, 7)] if g == 0
                           else n.mark[("kT", t, p, g - 1)]))
            tv = op(n, "te", lambda e, kv=kv, j=j, p=p: e.transpose(
                P[6][:].bitcast(BF16)[:, j * 128:(j + 1) * 128],
                n.k_sb[:, kv * T + p * 128: kv * T + p * 128 + 128], n.identc[:]),
                waits=wl, inc=sT)
        vv = op(n, "ve", lambda e, g=g: e.tensor_copy(
            n.kT_sb[:, g * 512:(g + 1) * 512], P[6][:].bitcast(BF16)[:, 0:512]),
            waits=[(sT, tv)], inc=sV)
        n.mark[("kT", t, p, g)] = vv


def group_prep(n, t, p, g):
    sT, sV, sE, sG = n.sT, n.sV, n.sE, n.sG
    P = n.P
    po = p * 32
    po16 = p * 16
    h0 = 4 * g
    dbg = n.dbg
    bD, bS, bP_, bQ_, bV_ = P[0], P[1], P[2], P[3], P[4]

    def colAP(tensor, off, reps=128):
        return tensor[:, off:off + 4].rearrange("p (a b) -> p a b", b=1)\
            .broadcast_to([128, 4, reps])

    def maskAP():
        return n.maskSc[:].rearrange("p (a f) -> p a f", a=1)\
            .broadcast_to([128, 4, 128])

    def identAP():
        return n.identc[:].rearrange("p (a f) -> p a f", a=1)\
            .broadcast_to([128, 4, 128])

    def kvrepAP(tensor, base):
        return tensor[:, base:base + 256].rearrange("p (a f) -> p a f", a=2)\
            .rearrange("p (a r) f -> p a r f", r=1).broadcast_to([128, 2, 2, 128])

    wl = [(sG, n.growv[g]), (sV, n.mark.get(("argdone", t, p, g - 1),
                                            n.mark.get(("smalls", t, p), 0)))]
    if g == 0:
        wl.append((sV, n.mark[("smalls", t, p)]))
    rv = op(n, "te", lambda e: e.matmul(
        bD[:, 0:512], n.ones1[0:1, :], n.grow[0:1, 0:512],
        start=True, stop=True), waits=wl, inc=sT)
    n.mark[("rowmm", t, p, g)] = rv

    kv0 = 2 * g
    wl2 = [(sE, n.mark[("silu", t, 16 + kv0)]), (sE, n.mark[("silu", t, 17 + kv0)]),
           (sE, n.mark[("silu", t, kv0)]), (sE, n.mark[("silu", t, 1 + kv0)]),
           (sV, n.mark.get(("scoresdone", t, p, g - 1), 0))]
    for j, kv in enumerate((kv0, kv0 + 1)):
        ksl = n.k_sb[:, kv * T + p * 128: kv * T + p * 128 + 128]
        qsl = n.q_sb[:, kv * T + p * 128: kv * T + p * 128 + 128]
        op(n, "te", lambda e, j=j, ksl=ksl: e.matmul(
            bS[:, j * 128:(j + 1) * 128], ksl, ksl,
            start=(j == 0), stop=False), waits=(wl2 if j == 0 else ()))
        sv = op(n, "te", lambda e, j=j, ksl=ksl, qsl=qsl: e.matmul(
            bS[:, (2 + j) * 128:(3 + j) * 128], ksl, qsl,
            start=False, stop=(j == 1)), inc=sT)
    n.mark[("scores", t, p, g)] = sv

    op(n, "ve", lambda e, po=po, h0=h0: e.tensor_tensor(
        n.t32a[:].rearrange("p (a f) -> p a f", a=4),
        bD[:, 0:512].rearrange("p (a f) -> p a f", a=4),
        colAP(n.gc_tok, po + h0), op=ALU.subtract), waits=[(sT, rv)])
    v_arg = op(n, "ve", lambda e: e.tensor_tensor(
        n.arg_sb[:].rearrange("p (a f) -> p a f", a=4),
        n.t32a[:].rearrange("p (a f) -> p a f", a=4),
        maskAP(), op=ALU.add), inc=sV, k=2)
    n.mark[("argdone", t, p, g)] = v_arg
    e_es = op(n, "se", lambda e: e.activation(n.es_sb[:], n.arg_sb[:], AF.Exp),
              waits=[(sV, v_arg)], inc=sE)
    v_ei = op(n, "ve", lambda e: e.tensor_tensor(
        n.ei_sb[:].rearrange("p (a f) -> p a f", a=4),
        n.es_sb[:].rearrange("p (a f) -> p a f", a=4),
        identAP(), op=ALU.add), waits=[(sE, e_es)], inc=sV)
    op(n, "ve", lambda e, g=g: e.tensor_tensor(
        n.t32b[:].rearrange("p (a b f) -> p a b f", a=2, b=2),
        kvrepAP(bS, 256),
        n.ei_sb[:].rearrange("p (a b f) -> p a b f", a=2, b=2), op=ALU.mult),
        waits=[(sT, sv), (sV, v_ei)])
    v_A = op(n, "ve", lambda e, p=p, h0=h0, po16=po16: e.tensor_tensor(
        n.A_sb[:, h0 * 128:(h0 + 4) * 128].rearrange("p (a b f) -> p a b f", a=2, b=2),
        n.t32b[:].rearrange("p (a b f) -> p a b f", a=2, b=2),
        n.rk_tok[:, po16 + h0 // 2: po16 + h0 // 2 + 2].rearrange(
            "p (a b c) -> p a b c", b=1, c=1).broadcast_to([128, 2, 2, 128]),
        op=ALU.mult), inc=sV, k=2)
    n.mark[("A", t, p, g)] = v_A
    op(n, "ve", lambda e, g=g: e.tensor_tensor(
        n.t32b[:].rearrange("p (a b f) -> p a b f", a=2, b=2),
        kvrepAP(bS, 0),
        n.es_sb[:].rearrange("p (a b f) -> p a b f", a=2, b=2), op=ALU.mult))
    v_pt = op(n, "ve", lambda e, po16=po16, h0=h0: e.tensor_tensor(
        n.ptil_sb[:].rearrange("p (a b f) -> p a b f", a=2, b=2),
        n.t32b[:].rearrange("p (a b f) -> p a b f", a=2, b=2),
        n.rk_tok[:, po16 + h0 // 2: po16 + h0 // 2 + 2].rearrange(
            "p (a b c) -> p a b c", b=1, c=1).broadcast_to([128, 2, 2, 128]),
        op=ALU.mult), inc=sV, k=2)
    n.mark[("scoresdone", t, p, g)] = v_pt

    def pw(i):
        return n.pw_sb[:, i * 512:(i + 1) * 512]

    for j in range(4):
        wlq1 = [(sV, v_pt)] if j == 0 else []
        if j == 0:
            wlq1.append((sV, n.mark.get(("pwdone", t, p, g - 1),
                                        n.mark.get(("pwdone", t, p - 1, 7),
                                                   n.mark.get(("pwdoneT", t - 1), 0)))))
        tv = op(n, "te", lambda e, j=j: e.transpose(
            bQ_[:].bitcast(BF16)[:, j * 128:(j + 1) * 128],
            n.ptil_sb[:, j * 128:(j + 1) * 128],
            n.identc[:]), waits=wlq1, inc=sT)
    v_q1 = op(n, "ve", lambda e, po=po, h0=h0: e.tensor_tensor(
        pw(0).rearrange("p (a f) -> p a f", a=4),
        bQ_[:].bitcast(BF16)[:, 0:512].rearrange("p (a f) -> p a f", a=4),
        colAP(n.brk_tok, po + h0), op=ALU.mult), waits=[(sT, tv)], inc=sV)
    for j in range(4):
        tv = op(n, "te", lambda e, j=j: e.transpose(
            bP_[:].bitcast(BF16)[:, j * 128:(j + 1) * 128],
            pw(0)[:, j * 128:(j + 1) * 128],
            n.identc[:]), waits=([(sV, v_q1)] if j == 0 else ()), inc=sT)
    v_p1 = op(n, "ve", lambda e: e.tensor_copy(pw(1), bP_[:].bitcast(BF16)[:, 0:512]),
              waits=[(sT, tv)], inc=sV)
    lvl = [(0, 1, 2, 3), (2, 3, 4, 5), (4, 5, 6, 7), (6, 7, 8, 9), (8, 9, 10, None)]
    prev_ev = v_p1
    for (qi, pi, qo, po_) in lvl:
        for j in range(4):
            tv = op(n, "te", lambda e, j=j, qi=qi, pi=pi: e.matmul(
                bQ_[:, j * 128:(j + 1) * 128], pw(pi)[:, j * 128:(j + 1) * 128],
                pw(qi)[:, j * 128:(j + 1) * 128], start=(j == 0), stop=(j == 3)),
                waits=([(sV, prev_ev)] if j == 0 else ()), inc=sT)
        v_q = op(n, "ve", lambda e, qo=qo: e.tensor_copy(pw(qo), bQ_[:, 0:512]),
                 waits=[(sT, tv)], inc=sV)
        prev_ev = v_q
        if po_ is not None:
            for j in range(4):
                tv = op(n, "te", lambda e, j=j, qo=qo: e.transpose(
                    bP_[:].bitcast(BF16)[:, j * 128:(j + 1) * 128],
                    pw(qo)[:, j * 128:(j + 1) * 128],
                    n.identc[:]), waits=([(sV, v_q)] if j == 0 else ()), inc=sT)
            prev_ev = op(n, "ve", lambda e, po_=po_: e.tensor_copy(
                pw(po_), bP_[:].bitcast(BF16)[:, 0:512]), waits=[(sT, tv)], inc=sV)

    n.mark[("pwdone", t, p, g)] = prev_ev
    if (p, g) == (NPAIR - 1, 7):
        n.mark[("pwdoneT", t)] = prev_ev
    v_v = op(n, "ve", lambda e: e.tensor_tensor(
        n.zz_sb[:].rearrange("p (a f) -> p a f", a=4),
        pw(10).rearrange("p (a f) -> p a f", a=4), identAP(), op=ALU.add),
        waits=[(sV, prev_ev)], inc=sV)
    for qi, last in ((8, 0), (6, 0), (4, 0), (2, 0), (0, 1)):
        for j in range(4):
            tv = op(n, "te", lambda e, j=j, qi=qi: e.matmul(
                bV_[:, j * 128:(j + 1) * 128], pw(qi)[:, j * 128:(j + 1) * 128],
                n.zz_sb[:, j * 128:(j + 1) * 128], start=(j == 0), stop=(j == 3)),
                waits=([(sV, v_v)] if j == 0 else ()), inc=sT)
        if not last:
            v_v = op(n, "ve", lambda e: e.tensor_add(
                n.zz_sb[:], n.zz_sb[:], bV_[:, 0:512]), waits=[(sT, tv)], inc=sV)
        else:
            v_v = op(n, "ve", lambda e, h0=h0: e.tensor_sub(
                n.V_sb[:, h0 * 128:(h0 + 4) * 128], n.zz_sb[:], bV_[:, 0:512]),
                waits=[(sT, tv)], inc=sV)
    n.mark[("V", t, p, g)] = v_v
    if "V0" in dbg and (t, p, g) == (0, 0, 0):
        dbg_dump(n, "V0", n.V_sb[:, 0:128], [(sV, v_v)])
    if "A0" in dbg and (t, p, g) == (0, 0, 0):
        dbg_dump(n, "A0", n.A_sb[:, 0:128], [(sV, n.mark[("A", t, p, g)])])


def delta_state(n, t, p, ci):
    sT, sV, sE = n.sT, n.sV, n.sE
    P = n.P
    dbg = n.dbg
    po = p * 32
    po16 = p * 16
    cb = ci * 64
    bK, bO1, bO2 = P[0], P[1], P[2]

    def colAP4(tensor, off, reps=128):
        return tensor[cb:cb + 64, off:off + 4].rearrange("p (a b) -> p a b", b=1)\
            .broadcast_to([64, 4, reps])

    def colAPf(tensor, off, reps=128):
        return tensor[:, off:off + 4].rearrange("p (a b) -> p a b", b=1)\
            .broadcast_to([128, 4, reps])

    def rqAP(pn=64):
        pass

    for g in range(8):
        h0 = 4 * g
        kv0 = 2 * g
        wlk = [(sV, n.mark[("V", t, p, 7)]),
               (sV, n.mark.get(("Sbready", t, p, ci, g), n.v_smallconst)),
               (sV, n.mark.get(("uev", t, p, ci, g - 1), 0))]
        tv = None
        for j in range(4):
            h = h0 + j
            kv = kv0 + j // 2
            tv = op(n, "te", lambda e, j=j, h=h, kv=kv, p=p, cb=cb: e.matmul(
                bK[cb:cb + 64, j * 128:(j + 1) * 128],
                n.k_sb[:, kv * T + p * 128 + cb: kv * T + p * 128 + cb + 64],
                n.Sb_sb[:, h * 128:(h + 1) * 128],
                start=(j == 0), stop=(j == 3)), waits=(wlk if j == 0 else ()),
                inc=sT)
        op(n, "ve", lambda e, po=po, h0=h0, cb=cb: e.tensor_tensor(
            n.t32a[cb:cb + 64, :].rearrange("p (a f) -> p a f", a=4),
            bK[cb:cb + 64, 0:512].rearrange("p (a f) -> p a f", a=4),
            colAP4(n.bgr_tok, po + h0), op=ALU.mult), waits=[(sT, tv)])
        op(n, "ve", lambda e, po=po, h0=h0, g=g, cb=cb: e.tensor_tensor(
            n.t32b[cb:cb + 64, :].rearrange("p (a f) -> p a f", a=4),
            n.vT_sb[cb:cb + 64, g * 512:(g + 1) * 512].rearrange(
                "p (a f) -> p a f", a=4),
            colAP4(n.beta_tok, po + h0), op=ALU.mult),
            waits=[(sV, n.mark[("vT", t, p, g)])])
        v_rhs = op(n, "ve", lambda e, cb=cb: e.tensor_sub(
            n.rhs_sb[cb:cb + 64, :], n.t32b[cb:cb + 64, :],
            n.t32a[cb:cb + 64, :]), inc=sV, k=3)
        n.mark[("rhsdone", t, p, ci, g)] = v_rhs
        tv = None
        for j in range(4):
            h = h0 + j
            tv = op(n, "te", lambda e, j=j, h=h, cb=cb: e.matmul(
                bK[cb:cb + 64, j * 128:(j + 1) * 128],
                n.V_sb[cb:cb + 64, h * 128 + cb: h * 128 + cb + 64],
                n.rhs_sb[cb:cb + 64, j * 128:(j + 1) * 128],
                start=(j == 0), stop=(j == 3)),
                waits=([(sV, v_rhs)] if j == 0 else ()), inc=sT)
        n.mark[("ujmm", t, p, ci, g)] = tv
        v_u = op(n, "ve", lambda e, cb=cb: e.tensor_copy(
            n.u_sb[cb:cb + 64, :], bK[cb:cb + 64, 0:512]), waits=[(sT, tv)], inc=sV)
        n.mark[("uev", t, p, ci, g)] = v_u
        if "u0" in dbg and (t, p, ci, g) == (0, 0, 0, 0):
            dbg_dump(n, "u0", n.u_sb[0:64, :], [(sV, v_u)])
        wlq = [(sV, n.mark.get(("coreg", t, p, ci, g - 1), 0))]
        tvq = None
        for j in range(4):
            h = h0 + j
            kv = kv0 + j // 2
            tvq = op(n, "te", lambda e, j=j, h=h, kv=kv, p=p, cb=cb: e.matmul(
                bO1[cb:cb + 64, j * 128:(j + 1) * 128],
                n.q_sb[:, kv * T + p * 128 + cb: kv * T + p * 128 + cb + 64],
                n.Sb_sb[:, h * 128:(h + 1) * 128],
                start=(j == 0), stop=(j == 3)), waits=(wlq if j == 0 else ()),
                inc=sT)
        wla = [(sV, v_u), (sV, n.mark[("A", t, p, g)]),
               (sV, n.mark.get(("Sbready", t, p, ci + 1, g - 1), 0))]
        tva = None
        for j in range(4):
            h = h0 + j
            tva = op(n, "te", lambda e, j=j, h=h, cb=cb: e.matmul(
                bO2[cb:cb + 64, j * 128:(j + 1) * 128],
                n.A_sb[cb:cb + 64, h * 128 + cb: h * 128 + cb + 64],
                n.u_sb[cb:cb + 64, j * 128:(j + 1) * 128],
                start=(j == 0), stop=(j == 3)),
                waits=(wla if j == 0 else ()), inc=sT)
        v_o1 = op(n, "ve", lambda e, po=po, h0=h0, cb=cb: e.tensor_tensor(
            n.t32a[cb:cb + 64, :].rearrange("p (a f) -> p a f", a=4),
            bO1[cb:cb + 64, 0:512].rearrange("p (a f) -> p a f", a=4),
            colAP4(n.grq_tok, po + h0), op=ALU.mult), waits=[(sT, tvq)], inc=sV)
        n.mark[("oe1", t, p, ci, g)] = v_o1
        v_o2 = op(n, "ve", lambda e, po16=po16, h0=h0, cb=cb: e.tensor_tensor(
            n.t32b[cb:cb + 64, :].rearrange("p (a b f) -> p a b f", a=2, b=2),
            bO2[cb:cb + 64, 0:512].rearrange("p (a b f) -> p a b f", a=2, b=2),
            n.rq_tok[cb:cb + 64, po16 + h0 // 2: po16 + h0 // 2 + 2].rearrange(
                "p (a b c) -> p a b c", b=1, c=1).broadcast_to([64, 2, 2, 128]),
            op=ALU.mult), waits=[(sT, tva)], inc=sV)
        n.mark[("oe2", t, p, ci, g)] = v_o2
        v_os = op(n, "ve", lambda e, cb=cb: e.tensor_add(
            n.o1_sb[cb:cb + 64, :], n.t32a[cb:cb + 64, :], n.t32b[cb:cb + 64, :]),
            inc=sV)
        op(n, "ve", lambda e, cb=cb: e.tensor_mul(
            n.on_sb[cb:cb + 64, :], n.o1_sb[cb:cb + 64, :], n.o1_sb[cb:cb + 64, :]))
        op(n, "ve", lambda e, cb=cb: e.reduce_sum(
            n.lncol[cb:cb + 64, 8:12],
            n.on_sb[cb:cb + 64, :].rearrange("p (a f) -> p a f", a=4), axis=AXL.X))
        op(n, "ve", lambda e, cb=cb: e.tensor_scalar(
            n.lncol[cb:cb + 64, 8:12], n.lncol[cb:cb + 64, 8:12],
            1.0 / DV, RMS_EPS, op0=ALU.mult, op1=ALU.add))
        v_rr = op(n, "ve", lambda e, cb=cb: e.reciprocal(
            n.lncol[cb:cb + 64, 8:12], n.lncol[cb:cb + 64, 8:12]), inc=sV, k=4)
        e_sq = op(n, "se", lambda e, cb=cb: e.activation(
            n.lncol[cb:cb + 64, 12:16], n.lncol[cb:cb + 64, 8:12], AF.Sqrt),
            waits=[(sV, v_rr)], inc=sE)
        v_on = op(n, "ve", lambda e, cb=cb: e.tensor_tensor(
            n.on_sb[cb:cb + 64, :].rearrange("p (a f) -> p a f", a=4),
            n.o1_sb[cb:cb + 64, :].rearrange("p (a f) -> p a f", a=4),
            n.lncol[cb:cb + 64, 12:16].rearrange("p (a b) -> p a b", b=1)
                .broadcast_to([64, 4, 128]), op=ALU.mult),
            waits=[(sE, e_sq)], inc=sV)
        tvc = None
        for j in range(4):
            tvc = op(n, "te", lambda e, j=j, cb=cb: e.transpose(
                bO1[:].bitcast(BF16)[:, j * 64:(j + 1) * 64],
                n.on_sb[cb:cb + 64, j * 128:(j + 1) * 128],
                n.identc[cb:cb + 64, cb:cb + 64]),
                waits=([(sV, v_on)] if j == 0 else ()), inc=sT)
        zoff = p * 128 + cb

        def vc_view(e_, h0=h0, zoff=zoff):
            return n.vc_sb[:, h0 * T:(h0 + 4) * T].rearrange(
                "p (a f) -> p a f", a=4)[:, :, zoff:zoff + 64]

        def z_view(e_, h0=h0, zoff=zoff):
            return n.z_sb[:, h0 * T:(h0 + 4) * T].rearrange(
                "p (a f) -> p a f", a=4)[:, :, zoff:zoff + 64]

        v_cg = op(n, "ve", lambda e, h0=h0, zoff=zoff: e.scalar_tensor_tensor(
            vc_view(e, h0, zoff),
            bO1[:].bitcast(BF16)[:, 0:256].rearrange("p (a f) -> p a f", a=4),
            n.nw[:, 0:1], z_view(e, h0, zoff),
            op0=ALU.mult, op1=ALU.mult),
            waits=[(sT, tvc), (sE, n.mark[("zev", t, 31)])],
            inc=sV)
        n.mark[("coreg", t, p, ci, g)] = v_cg
        v_kh = op(n, "ve", lambda e, g=g, po=po, h0=h0, cb=cb: e.tensor_tensor(
            n.kh_sb[cb:cb + 64, :].rearrange("p (a b f) -> p a b f", a=2, b=2),
            n.kT_sb[cb:cb + 64, g * 256:(g + 1) * 256].rearrange(
                "p (a r f) -> p a r f", a=2, r=1).broadcast_to([64, 2, 2, 128]),
            n.dec_tok[cb:cb + 64, po + h0: po + h0 + 4].rearrange(
                "p (a b c) -> p a b c", b=2, c=1).broadcast_to([64, 2, 2, 128]),
            op=ALU.mult), waits=[(sV, n.mark[("kT", t, p, g // 2)])], inc=sV)
        tvd = None
        for j in range(4):
            h = h0 + j
            tvd = op(n, "te", lambda e, j=j, cb=cb: e.matmul(
                bO2[:, j * 128:(j + 1) * 128],
                n.kh_sb[cb:cb + 64, j * 128:(j + 1) * 128],
                n.u_sb[cb:cb + 64, j * 128:(j + 1) * 128],
                start=(j == 0), stop=(j == 3)),
                waits=([(sV, v_kh), (sV, v_o2)] if j == 0 else ()), inc=sT)
        gl_off = (p * 2 + ci) * 32 + h0
        op(n, "ve", lambda e, gl_off=gl_off, h0=h0: e.tensor_tensor(
            n.t32a[:].rearrange("p (a f) -> p a f", a=4),
            n.S_sb[:, h0 * 128:(h0 + 4) * 128].rearrange("p (a f) -> p a f", a=4),
            colAPf(n.glast_bc, gl_off), op=ALU.mult), waits=[(sT, tvd)])
        op(n, "ve", lambda e, h0=h0: e.tensor_add(
            n.S_sb[:, h0 * 128:(h0 + 4) * 128], n.t32a[:], bO2[:, 0:512]))
        v_sb = op(n, "ve", lambda e, h0=h0: e.tensor_copy(
            n.Sb_sb[:, h0 * 128:(h0 + 4) * 128],
            n.S_sb[:, h0 * 128:(h0 + 4) * 128]), inc=sV, k=3)
        n.mark[("Sbready", t, p, ci + 1, g)] = v_sb
        if ci == 1:
            if p == NPAIR - 1:
                n.mark[("Sbready", t + 1, 0, 0, g)] = v_sb
            else:
                n.mark[("Sbready", t, p + 1, 0, g)] = v_sb
    n.mark[("SbT", t, p, ci + 1)] = sV.n
    n.mark[("statevec", t)] = sV.n
    if "S0" in dbg and (t, p, ci) == (0, NPAIR - 1, 1):
        dbg_dump(n, "S0", n.S_sb[:], [(sV, n.sV.n)])
    if "core" in dbg and (t, p, ci) == (0, NPAIR - 1, 1):
        dbg_dump(n, "core", n.vc_sb[:], [(sV, n.sV.n)])


def wout_phase(n, t):
    sT, sV, sE = n.sT, n.sV, n.sE
    P = n.P
    ws = n.ws
    par = t % 2
    xoff = par * (T // 128) * D
    v_core = n.mark[("statevec", t)]

    tv = None
    for h in range(HV):
        half, wv = ws.load(n.wo_b[h * 128:(h + 1) * 128, :],
                           waits=([(sV, v_core)] if h == 0 else ()))
        for tc in range(T // 128):
            for nb in range(4):
                bank = P[tc * 4 + nb]
                wl = []
                if h == 0 and nb == 0:
                    wl = [wv, (sV, v_core),
                          (sV, n.mark.get(("woutev", t - 1), 0))]
                elif nb == 0 and tc == 0:
                    wl = [wv]
                tv = op(n, "te", lambda e, h=h, tc=tc, nb=nb, half=half, bank=bank:
                        e.matmul(
                            bank[:, 0:512],
                            n.vc_sb[:, h * T + tc * 128: h * T + tc * 128 + 128],
                            n.wbuf[:, half * 2048 + nb * 512:
                                   half * 2048 + (nb + 1) * 512],
                            start=(h == 0), stop=(h == HV - 1)),
                        waits=wl, inc=sT)
        ws.done(tv)
    n.mark[("woutmm", t)] = tv
    v = None
    for tc in range(T // 128):
        for nb in range(4):
            bank = P[tc * 4 + nb]
            xs = n.x_sb[:, xoff + tc * D + nb * 512: xoff + tc * D + (nb + 1) * 512]
            v = op(n, "ve", lambda e, bank=bank, xs=xs: e.tensor_add(
                xs, xs, bank[:, 0:512]), waits=([(sT, tv)] if (tc, nb) == (0, 0)
                                                else ()), inc=sV)
    n.mark[("woutev", t)] = v
    if "x1" in n.dbg and t == 0:
        dbg_dump(n, "x1", n.x_sb[:, xoff:xoff + (T // 128) * D], [(n.sV, v)])


def ffn_phase(n, t):
    sT, sV, sE, sO = n.sT, n.sV, n.sE, n.sO
    P = n.P
    ws = n.ws
    par = t % 2
    xoff = par * (T // 128) * D
    v_res = n.mark[("woutev", t)]

    store_vals = []
    prev_vxq = v_res
    for tc in range(T // 128):
        xs = n.x_sb[:, xoff + tc * D: xoff + (tc + 1) * D]
        op(n, "ve", lambda e, xs=xs: e.reduce_sum(n.lncol[:, 0:1], xs, axis=AXL.X),
           waits=[(sV, v_res)])
        op(n, "ve", lambda e: e.tensor_scalar_mul(n.lncol[:, 2:3], n.lncol[:, 0:1],
                                                  1.0 / D))
        v_m = op(n, "ve", lambda e: e.tensor_mul(
            n.lncol[:, 4:5], n.lncol[:, 2:3], n.lncol[:, 2:3]), inc=sV, k=3)
        e_sq = op(n, "se", lambda e, xs=xs: e.activation(
            n.ffnq[:], xs, AF.Square, accum_out=n.lncol[:, 1:2]),
            waits=[(sV, prev_vxq)], inc=sE)
        op(n, "ve", lambda e: e.tensor_scalar_mul(n.lncol[:, 3:4], n.lncol[:, 1:2],
                                                  1.0 / D), waits=[(sE, e_sq)])
        op(n, "ve", lambda e: e.tensor_sub(n.lncol[:, 5:6], n.lncol[:, 3:4],
                                           n.lncol[:, 4:5]))
        op(n, "ve", lambda e: e.tensor_scalar_add(n.lncol[:, 6:7], n.lncol[:, 5:6],
                                                  LN_EPS))
        v_rc = op(n, "ve", lambda e: e.reciprocal(n.lncol[:, 7:8], n.lncol[:, 6:7]),
                  inc=sV, k=3)
        e_st = op(n, "se", lambda e: e.activation(n.lncol[:, 8:9], n.lncol[:, 7:8],
                                                  AF.Sqrt), waits=[(sV, v_rc)],
                  inc=sE)
        op(n, "ve", lambda e: e.tensor_mul(n.lncol[:, 9:10], n.lncol[:, 5:6],
                                           n.lncol[:, 7:8]), waits=[(sE, e_st)])
        op(n, "ve", lambda e: e.tensor_scalar_add(n.lncol[:, 9:10],
                                                  n.lncol[:, 9:10], NRM_EPS))
        v_r2 = op(n, "ve", lambda e: e.reciprocal(n.lncol[:, 10:11],
                                                  n.lncol[:, 9:10]), inc=sV, k=2)
        e_r2 = op(n, "se", lambda e: e.activation(n.lncol[:, 11:12],
                                                  n.lncol[:, 10:11], AF.Sqrt),
                  waits=[(sV, v_r2)], inc=sE)
        op(n, "ve", lambda e, xs=xs: e.tensor_scalar(
            n.ffnq[:], xs, n.lncol[:, 2:3], n.lncol[:, 8:9],
            op0=ALU.subtract, op1=ALU.mult), waits=[(sE, e_r2)])
        v_u2 = op(n, "ve", lambda e: e.tensor_mul(n.ffnq[:], n.ffnq[:],
                                                  n.gamma_bc[:]), inc=sV, k=2)
        op(n, "ve", lambda e: e.reduce_max(n.lncol[:, 12:13], n.ffnq[:], axis=AXL.X,
                                           apply_absolute_value=True))
        op(n, "ve", lambda e: e.tensor_mul(n.lncol[:, 12:13], n.lncol[:, 12:13],
                                           n.lncol[:, 11:12]))
        op(n, "ve", lambda e: e.tensor_scalar_add(n.lncol[:, 13:14],
                                                  n.lncol[:, 12:13], QEPS))
        v_am = op(n, "ve", lambda e: e.reciprocal(n.lncol[:, 14:15],
                                                  n.lncol[:, 13:14]), inc=sV, k=4)
        op(n, "ve", lambda e: e.tensor_scalar_mul(n.lncol[:, 14:15],
                                                  n.lncol[:, 14:15], 127.0),
           waits=[(sV, v_am)])
        v_qs = op(n, "ve", lambda e: e.tensor_mul(n.lncol[:, 14:15],
                                                  n.lncol[:, 14:15],
                                                  n.lncol[:, 11:12]), inc=sV, k=2)
        op(n, "ve", lambda e: e.tensor_scalar_mul(n.ffnq[:], n.ffnq[:],
                                                  n.lncol[:, 14:15]),
           waits=[(sV, v_qs)])
        op(n, "ve", lambda e: e.tensor_scalar(n.ffnq[:], n.ffnq[:], MAGIC, MAGIC,
                                              op0=ALU.add, op1=ALU.subtract))
        v_xq = op(n, "ve", lambda e: e.tensor_scalar(
            n.xq_tok[:], n.ffnq[:], 127.0, -128.0, op0=ALU.min, op1=ALU.max),
            inc=sV, k=3)
        prev_vxq = v_xq
        op(n, "ve", lambda e: e.tensor_mul(n.lncol[:, 15:16], n.lncol[:, 13:14],
                                           n.mw_bc[:, 0:1]))
        v_osc = op(n, "ve", lambda e: e.tensor_scalar_mul(
            n.lncol[:, 15:16], n.lncol[:, 15:16], 1.0 / 127.0), inc=sV, k=2)
        for kk in range(NKC):
            bank = P[kk % 2]
            if kk < 2:
                wl = [(sV, v_xq), (sV, n.mark.get(("xqev", kk - 2 + NKC), 0))]
            else:
                wl = [(sV, n.mark[("xqev", kk - 2)])]
            tv = op(n, "te", lambda e, kk=kk, bank=bank: e.transpose(
                bank[:].bitcast(BF16)[:, 0:128],
                n.xq_tok[:, kk * 128:(kk + 1) * 128], n.identc[:]),
                waits=wl, inc=sT)
            n.mark[("xqT", t)] = tv
            n.mark[("xqev", kk)] = op(
                n, "ve", lambda e, kk=kk, tc=tc, bank=bank: e.tensor_copy(
                    n.xn_f[:, kk * T + tc * 128: kk * T + tc * 128 + 128],
                    bank[:].bitcast(BF16)[:, 0:128]), waits=[(sT, tv)], inc=sV)
            if kk >= NKC - 2:
                n.mark[("xqev", kk + NKC)] = n.mark[("xqev", kk)]
        v_xqf = n.sV.n
        for ob in range(4):
            bank = P[4 + ob % 2]
            tv = None
            for kq in range(4):
                half, wv = ws.load(
                    n.wf_b[kq * 512:(kq + 1) * 512, ob * 512:(ob + 1) * 512]
                    .rearrange("(k p) c -> p k c", p=128))
                for ki in range(4):
                    kk = kq * 4 + ki
                    wl = []
                    if ki == 0:
                        wl = [wv]
                        if kq == 0:
                            wl.append((sV, v_xqf))
                            wl.append((sV, n.mark.get(("ffnev", t, tc, ob - 2),
                                      n.mark.get(("ffnev", t, tc - 1, 2 + ob), 0))))
                    tv = op(n, "te", lambda e, kk=kk, ki=ki, kq=kq, tc=tc,
                            half=half, bank=bank: e.matmul(
                                bank[:, 0:512],
                                n.xn_f[:, kk * T + tc * 128: kk * T + tc * 128 + 128],
                                n.wbuf[:, half * 2048 + ki * 512:
                                       half * 2048 + (ki + 1) * 512],
                                start=(kk == 0), stop=(kk == NKC - 1)),
                            waits=wl, inc=sT)
                ws.done(tv)
            xs2 = n.x_sb[:, xoff + tc * D + ob * 512: xoff + tc * D + (ob + 1) * 512]
            n.mark[("ffnev", t, tc, ob)] = op(
                n, "ve", lambda e, bank=bank, xs2=xs2: e.scalar_tensor_tensor(
                    xs2, bank[:, 0:512], n.lncol[:, 15:16], xs2,
                    op0=ALU.mult, op1=ALU.add),
                waits=[(sT, tv), (sV, v_osc)], inc=sV)
        v_ffn = n.sV.n
        sv = op(n, "sy", lambda e, t=t, tc=tc, xs=xs: e.dma_start(
            out=n.out_d[t * T + tc * 128: t * T + (tc + 1) * 128, :], in_=xs),
            waits=[(sV, v_ffn)], inc=sO, k=16)
        store_vals.append(sv)
    n.mark[("ffnq_free", t)] = n.sV.n
    n.mark[("ffndone", t)] = n.sV.n
    n.store_v[t] = store_vals[-1]


def make_consts():
    ident = np.eye(128, dtype=BF)
    jj = np.arange(128)[:, None]
    tt = np.arange(128)[None, :]
    same = (jj // 64) == (tt // 64)
    strict = same & ((tt % 64) > (jj % 64))
    maskS = np.where(strict, 0.0, NEG).astype(np.float32)
    onehot = np.zeros((128, 16 * 16), dtype=BF)
    for h in range(16):
        onehot[:, h * 16 + h] = 1.0
    return ident, maskS, onehot


def kernel(**inputs):
    x = np.ascontiguousarray(inputs["L0_attn_hidden_states"])
    ident, maskS, onehot = make_consts()
    base = {
        "w_qkv": np.ascontiguousarray(inputs["L0_attn_w_qkv"]),
        "w_z": np.ascontiguousarray(inputs["L0_attn_w_z"]),
        "w_b": np.ascontiguousarray(inputs["L0_attn_w_b"]),
        "w_a": np.ascontiguousarray(inputs["L0_attn_w_a"]),
        "w_out": np.ascontiguousarray(inputs["L0_attn_w_out"]),
        "conv_w": np.ascontiguousarray(inputs["L0_attn_conv_w"]),
        "dt_bias": np.ascontiguousarray(inputs["L0_attn_dt_bias"]).reshape(HV, 1),
        "A_log": np.ascontiguousarray(inputs["L0_attn_A_log"]).reshape(HV, 1),
        "norm_w": np.ascontiguousarray(inputs["L0_attn_norm_weight"]).reshape(DV, 1),
        "gamma": np.ascontiguousarray(inputs["L0_ffn_gamma"]).reshape(1, D),
        "W_ffn": np.ascontiguousarray(inputs["L0_ffn_W"]),
        "ident": ident, "maskS": maskS, "onehot": onehot,
    }
    nc = build_program()
    in_maps = [dict(base, x=np.ascontiguousarray(x[i])) for i in range(8)]
    res = run_bass_kernel_spmd(nc, in_maps, core_ids=list(range(8)))
    return np.stack([res.results[i]["out"] for i in range(8)], axis=0)


# version 04
# speedup vs baseline: 1.2540x; 1.0210x over previous
import contextlib
import numpy as np
import ml_dtypes
import concourse.bass as bass
import concourse.mybir as mybir
from concourse.bass_utils import run_bass_kernel_spmd

F32 = mybir.dt.float32
BF16 = mybir.dt.bfloat16
AF = mybir.ActivationFunctionType
ALU = mybir.AluOpType
AXL = mybir.AxisListType
BF = ml_dtypes.bfloat16

S, D = 4096, 2048
HK, HV, DK, DV = 16, 32, 128, 128
KDIM, VDIM = HK * DK, HV * DV
CONV_DIM = 2 * KDIM + VDIM
KSIZE = 4
T = 256
NT = S // T
NPAIR = T // 128
NKC = D // 128
LN_EPS, NRM_EPS, RMS_EPS, QEPS = 1e-5, 1e-6, 1e-6, 1e-5
MAGIC = float(np.float32(1.5 * 2 ** 23))
NEG = -1e30

DEBUG_SHAPES = {
    "xn": (128, NKC * T), "q": (128, HK * T), "k": (128, HK * T),
    "v": (128, HV * T), "z": (128, HV * T), "g": (32, T), "gc": (32, T),
    "beta": (32, T), "rsq": (16, 2 * T), "A0": (128, 128), "V0": (128, 128),
    "core": (128, HV * T), "x1": (128, (T // 128) * D), "S0": (128, HV * DV),
    "gctok": (128, NPAIR * 32), "u0": (64, 512),
    "lncol": (128, 16), "xt": (128, 2048), "xsb": (128, 4096),
    "idc": (128, 128), "xn_b": (128, NKC * T),
}


class Sem:
    def __init__(self, handle):
        self.h = handle
        self.n = 0

    def inc(self, k=1):
        self.n += k
        return self.n


class NS:
    pass


ORDER_EDGES = False
TRUNC = None
AUTOCHAIN = ("ve", "se", "gp")


def op(n, eng, emit, waits=(), inc=None, k=1, dma=None, wide=False):
    val = inc.inc(k) if inc is not None else None
    wl = tuple((s.h, v) for (s, v) in waits if v > 0)
    ih = inc.h if inc is not None else None
    oe = None
    is_dma = (eng == "sy") if dma is None else dma
    if ORDER_EDGES or (eng in AUTOCHAIN and not is_dma):
        key = (eng, "d" if is_dma else "c")
        owl = []
        if n.olast.get(key) and not (n.owide.get(key) and eng == "ve"
                                     and not ORDER_EDGES):
            owl.append(n.olast[key])
        n.owide[key] = wide
        if is_dma and eng == "gp" and n.olast.get(("gp", "c")):
            owl.append(n.olast[("gp", "c")])
        if ih is not None:
            n.olast[key] = (ih, val)
            oe = (tuple(owl), None, 0)
        else:
            os_ = n.osem[key]
            step = 16 if is_dma else 1
            os_.inc(step)
            n.olast[key] = (os_.h, os_.n)
            oe = (tuple(owl), os_.h, step)

    def f(e, emit=emit, wl=wl, ih=ih, k=k, oe=oe):
        if oe is not None:
            for h, v in oe[0]:
                e.wait_ge(h, v)
        for h, v in wl:
            e.wait_ge(h, v)
        ins = emit(e)
        if ih is not None:
            ins.then_inc(ih, k)
        if oe is not None and oe[1] is not None:
            ins.then_inc(oe[1], oe[2])
    n.ops[eng].append(f)
    return val


def build_program(debug=()):
    nc = bass.Bass("TRN2", target_bir_lowering=False,
                   detect_race_conditions=False, debug=bool(debug))
    dbg = set(debug)
    n = NS()
    n.nc = nc
    n.dbg = dbg

    def dram(name, shape, dt, kind):
        return nc.dram_tensor(name, list(shape), dt, kind=kind).ap()

    n.x_d = dram("x", (S, D), F32, "ExternalInput")
    n.wqkv_d = dram("w_qkv", (D, CONV_DIM), F32, "ExternalInput")
    n.wz_d = dram("w_z", (D, VDIM), F32, "ExternalInput")
    n.wb_d = dram("w_b", (D, HV), F32, "ExternalInput")
    n.wa_d = dram("w_a", (D, HV), F32, "ExternalInput")
    n.wo_d = dram("w_out", (VDIM, D), F32, "ExternalInput")
    n.convw_d = dram("conv_w", (CONV_DIM, KSIZE), F32, "ExternalInput")
    n.dtb_d = dram("dt_bias", (HV, 1), F32, "ExternalInput")
    n.alog_d = dram("A_log", (HV, 1), F32, "ExternalInput")
    n.nw_d = dram("norm_w", (DV, 1), F32, "ExternalInput")
    n.gam_d = dram("gamma", (1, D), F32, "ExternalInput")
    n.wffn_d = dram("W_ffn", (D, D), F32, "ExternalInput")
    n.ident_d = dram("ident", (128, 128), BF16, "ExternalInput")
    n.maskS_d = dram("maskS", (128, 128), F32, "ExternalInput")
    n.oneh_d = dram("onehot", (128, 16 * 16), BF16, "ExternalInput")
    n.out_d = dram("out", (S, D), F32, "ExternalOutput")
    n.wqkv_b = dram("wqkv_b", (D, CONV_DIM), BF16, "Internal")
    n.wz_b = dram("wz_b", (D, VDIM), BF16, "Internal")
    n.wba_b = dram("wba_b", (D, 2 * HV), BF16, "Internal")
    n.wo_b = dram("wo_b", (VDIM, D), BF16, "Internal")
    n.wf_b = dram("wf_b", (D, D), BF16, "Internal")
    n.dbg_d = {nm: dram("dbg_" + nm, DEBUG_SHAPES[nm],
                        BF16 if nm.endswith("_b") else F32, "ExternalOutput")
               for nm in dbg}

    est = contextlib.ExitStack()
    sbt = lambda nm, shp, dt: est.enter_context(nc.sbuf_tensor(nm, list(shp), dt))
    pst = lambda nm: est.enter_context(nc.psum_tensor(nm, [128, 512], F32))
    mks = lambda nm: Sem(est.enter_context(nc.semaphore(nm)))
    n.osem = {}
    n.olast = {}
    n.owide = {}
    okeys = [("ve", "c"), ("se", "c"), ("gp", "c")]
    if ORDER_EDGES:
        okeys += [("sy", "d"), ("gp", "d"), ("te", "c")]
    for ek, ck in okeys:
        n.osem[(ek, ck)] = mks(f"ord_{ek}_{ck}")

    with est:
        n.x_sb = sbt("x_sb", (128, 2 * (T // 128) * D), F32)
        n.xn_f = sbt("xn_f", (128, NKC * T), BF16)
        n.q_sb = sbt("q_sb", (128, HK * T), BF16)
        n.k_sb = sbt("k_sb", (128, HK * T), BF16)
        n.vc_sb = sbt("vc_sb", (128, HV * T), BF16)
        n.z_sb = sbt("z_sb", (128, HV * T), BF16)
        n.A_sb = sbt("A_sb", (128, HV * 128), BF16)
        n.V_sb = sbt("V_sb", (128, HV * 128), BF16)
        n.vT_sb = sbt("vT_sb", (128, HV * 128), BF16)
        n.kT_sb = sbt("kT_sb", (128, HK * 128), BF16)
        n.S_sb = sbt("S_sb", (128, HV * DV), F32)
        n.Sb_sb = sbt("Sb_sb", (128, HV * DV), BF16)
        n.pw_sb = sbt("pw_sb", (128, 11 * 512), BF16)
        n.ptil_sb = sbt("ptil_sb", (128, 512), BF16)
        n.zz_sb = sbt("zz_sb", (128, 512), BF16)
        n.u_sb = sbt("u_sb", (128, 512), BF16)
        n.rhs_sb = sbt("rhs_sb", (128, 512), BF16)
        n.kh_sb = sbt("kh_sb", (128, 512), BF16)
        n.es_sb = sbt("es_sb", (128, 512), BF16)
        n.ei_sb = sbt("ei_sb", (128, 512), BF16)
        n.o1_sb = sbt("o1_sb", (128, 512), BF16)
        n.on_sb = sbt("on_sb", (128, 512), BF16)
        n.t32a = sbt("t32a", (128, 512), F32)
        n.t32b = sbt("t32b", (128, 512), F32)
        n.arg_sb = sbt("arg_sb", (128, 512), F32)
        n.wbuf = sbt("wbuf", (128, 2 * 2048), BF16)
        n.cbuf = sbt("cbuf", (128, T + 4), BF16)
        n.ta_sb = sbt("ta_sb", (128, T), BF16)
        n.tb_sb = sbt("tb_sb", (128, T), BF16)
        n.tc_sb = sbt("tc_sb", (128, T), BF16)
        n.td_sb = sbt("td_sb", (128, T), BF16)
        n.cv_sb = sbt("cv_sb", (128, T), BF16)
        n.qsq_sb = sbt("qsq_sb", (128, T), BF16)
        n.tails = sbt("tails", (128, 64 * 4), BF16)
        n.convw = sbt("convw", (128, 64 * KSIZE), F32)
        n.g_f = sbt("g_f", (64, T), F32)
        n.gc_f = sbt("gc_f", (64, T), F32)
        n.sm1 = None
        n.sm2 = None
        n.sm3 = None
        n.beta_f = sbt("beta_f", (64, T), F32)
        n.sm1 = n.t32a[0:64, 0:256]
        n.sm2 = n.t32a[0:64, 256:512]
        n.sm3 = n.t32b[0:64, 0:256]
        n.ba_f = n.t32b[0:64, 256:512]
        n.onessm = sbt("onessm", (64, 128), F32)
        n.gc_tok = sbt("gc_tok", (128, NPAIR * 32), F32)
        n.beta_tok = sbt("beta_tok", (128, NPAIR * 32), F32)
        n.gam_tok = sbt("gam_tok", (128, NPAIR * 32), F32)
        n.dec_tok = sbt("dec_tok", (128, NPAIR * 32), F32)
        n.glast_bc = sbt("glast_bc", (128, NPAIR * 2 * 32), F32)
        n.bgr_tok = sbt("bgr_tok", (128, NPAIR * 32), F32)
        n.grq_tok = sbt("grq_tok", (128, NPAIR * 32), F32)
        n.rq_tok = sbt("rq_tok", (128, NPAIR * 16), F32)
        n.rk_tok = sbt("rk_tok", (128, NPAIR * 16), F32)
        n.brk_tok = sbt("brk_tok", (128, NPAIR * 32), F32)
        n.rsq_f = sbt("rsq_f", (16, 2 * T), F32)
        n.grow = sbt("grow", (1, 512), F32)
        n.lncol = sbt("lncol", (128, 16), F32)
        n.gamma_bc = sbt("gamma_bc", (128, D), BF16)
        n.ffnq = sbt("ffnq", (128, D), F32)
        n.xt_tok = None
        n.xq_tok = sbt("xq_tok", (128, D), BF16)
        n.identc = sbt("identc", (128, 128), BF16)
        n.identf = sbt("identf", (64, 64), F32)
        n.maskSc = sbt("maskSc", (128, 128), F32)
        n.onehc = sbt("onehc", (128, 16 * 16), BF16)
        n.ones1 = sbt("ones1", (1, 128), F32)
        n.onescol = sbt("onescol", (128, 1), F32)
        n.dtb = sbt("dtb", (64, 1), F32)
        n.nA = sbt("nA", (64, 1), F32)
        n.nw = sbt("nw", (128, 1), F32)
        n.mw_bc = sbt("mw_bc", (128, 1), F32)
        n.wsc = sbt("wsc", (128, 1), F32)
        n.xt_tok = n.xq_tok
        n.setup1 = n.ffnq
        n.setup2 = n.xq_tok
        n.wabs = sbt("wabs", (128, 4), F32)

        n.P = [pst("pb%d" % i) for i in range(8)]

        n.sW0, n.sW1, n.sG, n.sO = mks("sW0"), mks("sW1"), mks("sG"), mks("sO")
        n.sXa, n.sXb, n.sC = mks("sXa"), mks("sXb"), mks("sC")
        n.sT, n.sV, n.sE, n.sP = mks("sT"), mks("sV"), mks("sE"), mks("sP")
        n.sQ = mks("sQ")
        n.sD = mks("sD")
        n.ops = {e: [] for e in ("te", "ve", "se", "gp", "sy")}

        schedule(n)

        blk = est.enter_context(nc.Block())

        @blk.sync
        def _(eng):
            for f in n.ops["sy"]:
                f(eng)

        @blk.tensor
        def _(eng):
            for f in n.ops["te"]:
                f(eng)

        @blk.vector
        def _(eng):
            for f in n.ops["ve"]:
                f(eng)

        @blk.scalar
        def _(eng):
            for f in n.ops["se"]:
                f(eng)

        @blk.gpsimd
        def _(eng):
            for f in n.ops["gp"]:
                f(eng)

    return nc


def schedule(n):
    sT, sV, sE, sP = n.sT, n.sV, n.sE, n.sP
    sG, sO, sC = n.sG, n.sO, n.sC

    setup(n)
    for t in range(NT):
        tile_body(n, t)
    endw = [(sO, sO.n)]
    if n.dbg:
        endw.append((n.sD, n.sD.n))
    op(n, "gp", lambda e: e.engine_nop(), waits=endw)


def setup(n):
    sT, sV, sE, sP = n.sT, n.sV, n.sE, n.sP
    sG, sO, sC = n.sG, n.sO, n.sC
    sXa = n.sXa
    P = n.P

    def syd(out, in_, waits=()):
        return op(n, "sy", lambda e, o=out, i=in_: e.dma_start(out=o, in_=i),
                  waits=waits, inc=sC, k=16)

    def gpd(out, in_, waits=()):
        return op(n, "gp", lambda e, o=out, i=in_: e.dma_start(out=o, in_=i),
                  waits=waits, inc=sP, k=16, dma=True)

    syd(n.identc[:], n.ident_d)
    syd(n.maskSc[:], n.maskS_d)
    syd(n.onehc[:], n.oneh_d)
    syd(n.dtb[32:64, :], n.dtb_d)
    syd(n.nA[32:64, :], n.alog_d)
    syd(n.nw[:], n.nw_d)
    syd(n.setup1[0:1, 0:D], n.gam_d)
    v_cw = syd(n.convw[:], n.convw_d.rearrange("(c p) s -> p c s", p=128))
    n.v_convw = v_cw
    v_consts = sC.n
    v_gam = v_consts

    n.xload_v = {}
    v0 = op(n, "sy", lambda e: e.dma_start(
        out=n.x_sb[:, 0:(T // 128) * D],
        in_=n.x_d[0:T, :].rearrange("(a p) d -> p a d", p=128)),
        inc=n.sXa, k=16)
    n.xload_v[0] = (n.sXa, v0)

    op(n, "ve", lambda e: e.memset(n.ones1[:], 1.0), waits=[(sC, v_consts)])
    op(n, "ve", lambda e: e.memset(n.onescol[:], 1.0))
    op(n, "ve", lambda e: e.memset(n.onessm[:], 1.0))
    op(n, "ve", lambda e: e.memset(n.S_sb[:], 0.0))
    op(n, "ve", lambda e: e.memset(n.Sb_sb[:], 0.0))
    op(n, "ve", lambda e: e.memset(n.tails[:], 0.0))
    v_idf = op(n, "gp", lambda e: e.dma_start(out=n.identf[:],
                                              in_=n.ident_d[0:64, 0:64]),
               inc=sP, k=16, dma=True)
    n.v_idf = v_idf

    v_e = op(n, "se", lambda e: e.activation(n.nA[32:64, :], n.nA[32:64, :], AF.Exp),
             waits=[(sC, v_consts)], inc=sE)
    v_na = op(n, "ve", lambda e: e.tensor_scalar_mul(n.nA[32:64, :], n.nA[32:64, :], -1.0),
              waits=[(sE, v_e)], inc=sV)
    n.v_smallconst = v_na

    for i in range(4):
        tv = op(n, "te", lambda e, i=i: e.matmul(
            P[0][:, 0:512], n.ones1[0:1, :], n.setup1[0:1, i * 512:(i + 1) * 512],
            start=True, stop=True),
            waits=([(sC, v_gam), (sV, v_na)] if i == 0 else
                   [(sV, n.sV.n)]), inc=sT)
        op(n, "ve", lambda e, i=i: e.tensor_copy(
            n.gamma_bc[:, i * 512:(i + 1) * 512], P[0][:, 0:512]),
            waits=[(sT, tv)], inc=sV)
    n.v_gammm = sT.n

    for i in range(8):
        gpd(n.wqkv_b[i * 256:(i + 1) * 256, :], n.wqkv_d[i * 256:(i + 1) * 256, :])
    for i in range(4):
        gpd(n.wz_b[i * 512:(i + 1) * 512, :], n.wz_d[i * 512:(i + 1) * 512, :])
    gpd(n.wba_b[:, 0:HV], n.wb_d)
    gpd(n.wba_b[:, HV:2 * HV], n.wa_d)
    for i in range(16):
        gpd(n.wo_b[i * 256:(i + 1) * 256, :], n.wo_d[i * 256:(i + 1) * 256, :])
    n.v_wcast = sP.n

    v_z = op(n, "ve", lambda e: e.memset(n.wabs[:, 0:1], 0.0), inc=sV)
    prev_use = 0
    for i in range(16):
        lv = syd(n.setup1[:], n.wffn_d[i * 128:(i + 1) * 128, :],
                 waits=([(sV, prev_use)] if i else [(sT, n.v_gammm)]))
        op(n, "ve", lambda e: e.reduce_sum(n.wabs[:, 1:2], n.setup1[:], axis=AXL.X,
                                           apply_absolute_value=True),
           waits=[(sC, lv)])
        prev_use = op(n, "ve", lambda e: e.tensor_add(
            n.wabs[:, 0:1], n.wabs[:, 0:1], n.wabs[:, 1:2]), inc=sV, k=2)
    tv = op(n, "te", lambda e: e.matmul(P[1][0:1, 0:1], n.onescol[:],
                                        n.wabs[:, 0:1], start=True, stop=True),
            waits=[(sV, prev_use)], inc=sT)
    op(n, "ve", lambda e: e.tensor_scalar(n.wabs[0:1, 2:3], P[1][0:1, 0:1],
                                          1.0 / (D * D), QEPS,
                                          op0=ALU.mult, op1=ALU.add),
       waits=[(sT, tv)])
    v_r = op(n, "ve", lambda e: e.reciprocal(n.wabs[0:1, 3:4], n.wabs[0:1, 2:3]),
             inc=sV, k=2)
    tv = op(n, "te", lambda e: e.matmul(P[1][:, 4:5], n.ones1[0:1, :],
                                        n.wabs[0:1, 3:4], start=True, stop=True),
            waits=[(sV, v_r)])
    tv = op(n, "te", lambda e: e.matmul(P[1][:, 5:6], n.ones1[0:1, :],
                                        n.wabs[0:1, 2:3], start=True, stop=True),
            inc=sT)
    op(n, "ve", lambda e: e.tensor_copy(n.wsc[:], P[1][:, 4:5]), waits=[(sT, tv)])
    v_wsc = op(n, "ve", lambda e: e.tensor_copy(n.mw_bc[:], P[1][:, 5:6]), inc=sV)

    prev_gp = 0
    prev_ve = v_wsc
    for i in range(16):
        lv = syd(n.setup1[:], n.wffn_d[i * 128:(i + 1) * 128, :],
                 waits=[(sV, prev_ve)])
        op(n, "ve", lambda e: e.tensor_scalar_mul(n.ffnq[:], n.setup1[:],
                                                  n.wsc[:, 0:1]),
           waits=[(sC, lv), (sV, v_wsc)])
        op(n, "ve", lambda e: e.tensor_scalar(n.ffnq[:], n.ffnq[:], MAGIC, MAGIC,
                                              op0=ALU.add, op1=ALU.subtract))
        qv = op(n, "ve", lambda e: e.tensor_scalar(n.setup2[:], n.ffnq[:], 1.0, -1.0,
                                                   op0=ALU.min, op1=ALU.max),
                inc=sV, k=3)
        prev_ve = qv
        for k in range(16):
            tv = op(n, "te", lambda e, k=k: e.transpose(
                P[2][:].bitcast(BF16)[:, (k % 4) * 128:(k % 4) * 128 + 128],
                n.setup2[:, k * 128:(k + 1) * 128], n.identc[:]),
                waits=([(sV, qv), (sP, prev_gp)] if k == 0 else
                       ([(sV, sV.n)] if k % 4 == 0 else [])), inc=sT)
            if k % 4 == 3:
                prev_ve = op(n, "ve", lambda e, k=k: e.tensor_copy(
                    n.wbuf[:, (k - 3) * 128:(k + 1) * 128],
                    P[2][:].bitcast(BF16)[:, 0:512]),
                    waits=[(sT, tv)], inc=sV)
        prev_gp = gpd(n.wf_b[:, i * 128:(i + 1) * 128].rearrange(
            "(a p) c -> p a c", p=128), n.wbuf[:, 0:2048],
            waits=[(sV, prev_ve)])
    n.v_wffn = sP.n
    n.v_setupgp = sP.n
    n.v_setupT = sT.n
    n.setup_sC = sC.n
    n.mark = {}
    n.mark[("ffnq_free", -1)] = sV.n


class WStream:
    def __init__(self, n):
        self.n = n
        self.idx = 0
        self.consumed = []

    def load(self, dram_ap, waits=(), cols=2048):
        n = self.n
        half = self.idx % 2
        sem = n.sW0 if half == 0 else n.sW1
        wl = list(waits)
        if self.idx >= 2:
            wl.append((n.sT, self.consumed[self.idx - 2]))
        v = op(n, "sy", lambda e, o=n.wbuf[:, half * 2048:half * 2048 + cols],
               i=dram_ap: e.dma_start(out=o, in_=i), waits=wl, inc=sem, k=16)
        self.idx += 1
        return half, (sem, v)

    def done(self, sT_val):
        self.consumed.append(sT_val)


def tile_body(n, t):
    sT, sV, sE, sP = n.sT, n.sV, n.sE, n.sP
    sG, sO, sC = n.sG, n.sO, n.sC
    P = n.P
    par = t % 2
    xoff = par * (T // 128) * D
    if not hasattr(n, "ws"):
        n.ws = WStream(n)
        n.store_v = {}
    ws = n.ws
    dbg = n.dbg

    if t + 1 < NT:
        npar = (t + 1) % 2
        sem = n.sXa if npar == 0 else n.sXb
        wl = []
        if t >= 1:
            wl.append((sO, n.store_v[t - 1]))
        vx = op(
            n, "sy", lambda e, t=t, npar=npar: e.dma_start(
                out=n.x_sb[:, npar * (T // 128) * D:(npar + 1) * (T // 128) * D],
                in_=n.x_d[(t + 1) * T:(t + 2) * T, :].rearrange(
                    "(a p) d -> p a d", p=128)),
            waits=wl, inc=sem, k=16)
        n.xload_v[t + 1] = (sem, vx)

    xn_guard = n.mark.get(("projdone", t - 1), 0)
    for tc in range(T // 128):
        xs = n.x_sb[:, xoff + tc * D: xoff + (tc + 1) * D]
        wl = [n.xload_v[t]] + ([(sV, n.mark[("ffnq_free", t - 1)])]
                                     if ("ffnq_free", t - 1) in n.mark else [])
        op(n, "ve", lambda e, xs=xs: e.reduce_sum(n.lncol[:, 0:1], xs, axis=AXL.X),
           waits=wl)
        v_red = op(n, "ve", lambda e, xs=xs: e.tensor_scalar_mul(
            n.lncol[:, 2:3], n.lncol[:, 0:1], 1.0 / D), inc=sV)
        e_sq = op(n, "se", lambda e, xs=xs: e.activation(
            n.ffnq[:], xs, AF.Square, accum_out=n.lncol[:, 1:2]),
            waits=[n.xload_v[t]] + ([(sV, n.mark[("ffnq_free", t - 1)])]
                                          if ("ffnq_free", t - 1) in n.mark else []),
            inc=sE)
        op(n, "ve", lambda e: e.tensor_scalar_mul(n.lncol[:, 3:4], n.lncol[:, 1:2],
                                                  1.0 / D), waits=[(sE, e_sq)])
        op(n, "ve", lambda e: e.tensor_mul(n.lncol[:, 4:5], n.lncol[:, 2:3],
                                           n.lncol[:, 2:3]))
        op(n, "ve", lambda e: e.tensor_sub(n.lncol[:, 5:6], n.lncol[:, 3:4],
                                           n.lncol[:, 4:5]))
        op(n, "ve", lambda e: e.tensor_scalar_add(n.lncol[:, 6:7], n.lncol[:, 5:6],
                                                  LN_EPS))
        v_rc = op(n, "ve", lambda e: e.reciprocal(n.lncol[:, 7:8], n.lncol[:, 6:7]),
                  inc=sV)
        e_st = op(n, "se", lambda e: e.activation(n.lncol[:, 8:9], n.lncol[:, 7:8],
                                                  AF.Sqrt), waits=[(sV, v_rc)],
                  inc=sE)
        v_xn = op(n, "ve", lambda e, xs=xs: e.tensor_scalar(
            n.xt_tok[:], xs, n.lncol[:, 2:3], n.lncol[:, 8:9],
            op0=ALU.subtract, op1=ALU.mult),
            waits=[(sE, e_st),
                   (sT, n.mark.get(("xqT", t - 1), n.v_setupT if t == 0 else 0))],
            inc=sV)
        for kk in range(NKC):
            bank = P[kk % 2]
            if kk < 2:
                wl = [(sV, v_xn), (sV, n.mark.get(("xnev", kk - 2), 0)),
                      (sV, n.mark.get(("ffndone", t - 1), 0))]
            else:
                wl = [(sV, n.mark[("xnev", kk - 2)])]
            tv = op(n, "te", lambda e, kk=kk, bank=bank: e.transpose(
                bank[:].bitcast(BF16)[:, 0:128],
                n.xt_tok[:, kk * 128:(kk + 1) * 128], n.identc[:]),
                waits=wl, inc=sT)
            n.mark[("xnev", kk)] = op(
                n, "ve", lambda e, kk=kk, tc=tc, bank=bank: e.tensor_copy(
                    n.xn_f[:, kk * T + tc * 128: kk * T + tc * 128 + 128],
                    bank[:].bitcast(BF16)[:, 0:128]), waits=[(sT, tv)], inc=sV)
    v_xnf = sV.n
    if "xn" in dbg and t == 0:
        dbg_dump(n, "xn", n.xn_f[:], [(sV, v_xnf)])
    if "xn_b" in dbg and t == 0:
        dbg_dump(n, "xn_b", n.xn_f[:], [(sV, v_xnf)])
    if t == 0:
        if "lncol" in dbg:
            dbg_dump(n, "lncol", n.lncol[:], [(sV, v_xnf)])
        if "xt" in dbg:
            dbg_dump(n, "xt", n.xt_tok[:], [(sV, v_xnf)])
        if "xsb" in dbg:
            dbg_dump(n, "xsb", n.x_sb[:, 0:2 * D], [(sV, v_xnf)])
        if "idc" in dbg:
            dbg_dump(n, "idc", n.identc[:], [(sV, v_xnf)])
    if TRUNC == "ln1":
        return

    projection_phase(n, t, v_xnf, xn_guard)
    if TRUNC == "proj":
        return

    smalls_phase(n, t)
    if TRUNC == "smalls":
        return

    for p in range(NPAIR):
        delta_prep(n, t, p)
        for ci in range(2):
            delta_state(n, t, p, ci)
    n.mark[("gathers", t)] = n.sG.n
    if TRUNC == "delta":
        return

    wout_phase(n, t)
    if TRUNC == "wout":
        return

    ffn_phase(n, t)


def dbg_dump(n, name, src_ap, waits):
    op(n, "gp", lambda e, o=n.dbg_d[name], i=src_ap: e.dma_start(out=o, in_=i),
       waits=waits, inc=n.sD, k=16, dma=True)


def projection_phase(n, t, v_xnf, xn_guard):
    sT, sV, sE, sP = n.sT, n.sV, n.sE, n.sP
    sO = n.sO
    P = n.P
    ws = n.ws
    dbg = n.dbg

    vc_guard = n.mark.get(("statevec", t - 1), 0)

    n.qk_ready = {}
    sumsq_mm = 0
    for oc in range(64):
        half, wv = ws.load(
            n.wqkv_b[:, oc * 128:(oc + 1) * 128].rearrange("(k p) c -> p k c", p=128),
            waits=[(sP, n.v_setupgp)] if (t == 0 and oc == 0) else ())
        bank = P[oc % 4]
        tv = None
        for kk in range(NKC):
            wl = []
            if kk == 0:
                wl = [wv, (sV, v_xnf)]
                if oc < 4:
                    wl.append((sV, n.mark.get(("ffndone", t - 1), 0)))
                else:
                    wl.append((sE, n.mark[("convev", t, oc - 4)]))
            tv = op(n, "te", lambda e, oc=oc, kk=kk, half=half, bank=bank: e.matmul(
                bank[:, 0:256], n.wbuf[:, half * 2048 + kk * 128: half * 2048 + (kk + 1) * 128],
                n.xn_f[:, kk * T:(kk + 1) * T],
                start=(kk == 0), stop=(kk == NKC - 1)), waits=wl, inc=sT)
        ws.done(tv)
        conv_emit(n, t, oc, bank, tv, vc_guard)
    for oc in range(32):
        half, wv = ws.load(
            n.wz_b[:, oc * 128:(oc + 1) * 128].rearrange("(k p) c -> p k c", p=128))
        bank = P[4 + oc % 2]
        tv = None
        for kk in range(NKC):
            wl = []
            if kk == 0:
                if oc < 2:
                    wl = [wv, (sV, n.mark.get(("ffndone", t - 1), 0)),
                          (sV, vc_guard)]
                else:
                    wl = [wv, (sE, n.mark[("zev", t, oc - 2)])]
            tv = op(n, "te", lambda e, oc=oc, kk=kk, half=half, bank=bank: e.matmul(
                bank[:, 0:256], n.wbuf[:, half * 2048 + kk * 128: half * 2048 + (kk + 1) * 128],
                n.xn_f[:, kk * T:(kk + 1) * T],
                start=(kk == 0), stop=(kk == NKC - 1)), waits=wl, inc=sT)
        ws.done(tv)
        n.mark[("zev", t, oc)] = op(
            n, "se", lambda e, oc=oc, bank=bank: e.activation(
                n.z_sb[:, oc * T:(oc + 1) * T], bank[:, 0:256], AF.Silu),
            waits=[(sT, tv)], inc=sE)
    if "z" in dbg and t == 0:
        dbg_dump(n, "z", n.z_sb[:], [(sE, n.sE.n)])

    half, wv = ws.load(n.wba_b.rearrange("(k p) c -> p k c", p=128), cols=1024)
    bank = P[6]
    tv = None
    for kk in range(NKC):
        wl = ([wv, (sV, n.mark.get(("smalls", t - 1, NPAIR - 1), 0)),
               (sE, n.mark.get(("sigb", t - 1), 0))] if kk == 0 else [])
        tv = op(n, "te", lambda e, kk=kk, half=half: e.matmul(
            bank[0:64, 0:256], n.wbuf[:, half * 2048 + kk * 64: half * 2048 + (kk + 1) * 64],
            n.xn_f[:, kk * T:(kk + 1) * T],
            start=(kk == 0), stop=(kk == NKC - 1)), waits=wl, inc=sT)
    ws.done(tv)
    n.v_ba = op(n, "ve", lambda e, bank=bank: e.tensor_copy(
        n.ba_f[:], bank[0:64, 0:256]), waits=[(sT, tv)], inc=sV)
    n.mark[("projdone", t)] = sT.n

    l2_finish(n, t)


def conv_emit(n, t, oc, bank, tv_mm, vc_guard):
    sT, sV, sE, sP = n.sT, n.sV, n.sE, n.sP
    sQ = n.sQ
    P = n.P
    dbg = n.dbg
    w = n.convw
    gv = op(n, "gp", lambda e, oc=oc: e.tensor_copy(
        n.cbuf[:, 1:4], n.tails[:, oc * 4 + 1: oc * 4 + 4]),
        waits=[(sV, n.mark.get(("convdone", t, oc - 1),
                               n.mark.get(("convdone", t - 1, 63),
                                          n.v_smallconst)))], inc=sQ)
    ev = op(n, "se", lambda e, bank=bank: e.activation(
        n.cbuf[:, 4:4 + T], bank[:, 0:256], AF.Copy),
        waits=[(sT, tv_mm),
               (sQ, n.mark.get(("convgp", t, oc - 1),
                               n.mark.get(("convgp", t - 1, 63), 0)))],
        inc=sE)
    n.mark[("convev", t, oc)] = ev
    gv2 = op(n, "gp", lambda e, oc=oc: e.tensor_copy(
        n.tails[:, oc * 4 + 1: oc * 4 + 4], n.cbuf[:, 1 + T: 4 + T]),
        waits=[(sE, ev)], inc=sQ)
    n.mark[("convgp", t, oc)] = gv2
    op(n, "ve", lambda e, oc=oc: e.tensor_scalar_mul(
        n.cv_sb[:], n.cbuf[:, 4:4 + T], w[:, oc * 4 + 3: oc * 4 + 4]),
        waits=[(sQ, gv), (sE, ev),
               (n.sE, n.mark.get(("silu", t, oc - 1),
                                 n.mark.get(("silu", t - 1, 63), 0)))],
        wide=True)
    v3 = None
    for s in (2, 1, 0):
        v3 = op(n, "ve", lambda e, oc=oc, s=s: e.scalar_tensor_tensor(
            n.cv_sb[:], n.cbuf[:, 1 + s: 1 + s + T],
            w[:, oc * 4 + s: oc * 4 + s + 1], n.cv_sb[:],
            op0=ALU.mult, op1=ALU.add), inc=(sV if s == 0 else None),
            wide=True)
    n.mark[("convdone", t, oc)] = v3
    if oc < 16:
        dst = n.q_sb[:, oc * T:(oc + 1) * T]
    elif oc < 32:
        dst = n.k_sb[:, (oc - 16) * T:(oc - 15) * T]
    else:
        dst = n.vc_sb[:, (oc - 32) * T:(oc - 31) * T]
    ev2 = op(n, "se", lambda e, dst=dst: e.activation(dst, n.cv_sb[:], AF.Silu),
             waits=[(sV, v3), (sT, n.mark.get(("woutmm", t - 1), 0))], inc=sE)
    n.mark[("silu", t, oc)] = ev2
    if oc < 32:
        v4 = op(n, "ve", lambda e, dst=dst: e.tensor_mul(n.qsq_sb[:], dst, dst),
                waits=[(sE, ev2)], inc=sV)
        h = oc % 16
        is_q = oc < 16
        colr = slice(0, 256) if is_q else slice(256, 512)
        wl5 = [(sV, v4)]
        if oc == 0:
            wl5.append((sE, n.mark.get(("rsqdone", t - 1), 0)))
        tv = op(n, "te", lambda e, h=h, colr=colr: e.matmul(
            n.P[7][0:16, colr], n.onehc[:, h * 16:(h + 1) * 16], n.qsq_sb[:],
            start=(oc == 0), stop=(oc == 31)), waits=wl5, inc=sT)
        n.mark[("l2mm", t, oc)] = tv


def l2_finish(n, t):
    sT, sV, sE = n.sT, n.sV, n.sE
    P = n.P
    dbg = n.dbg
    tv = n.mark[("l2mm", t, 31)]
    op(n, "ve", lambda e: e.tensor_scalar(n.rsq_f[:, 0:256], P[7][0:16, 0:256],
                                          float(DK), float(DK) * NRM_EPS,
                                          op0=ALU.mult, op1=ALU.add),
       waits=[(sT, tv), (sT, n.mark.get(("rsqT", t - 1), 0))])
    op(n, "ve", lambda e: e.tensor_scalar(n.rsq_f[:, 256:512], P[7][0:16, 256:512],
                                          1.0, NRM_EPS, op0=ALU.mult, op1=ALU.add))
    v1 = op(n, "ve", lambda e: e.reciprocal(n.rsq_f[:], n.rsq_f[:]), inc=sV, k=3)
    e1 = op(n, "se", lambda e: e.activation(n.rsq_f[:], n.rsq_f[:], AF.Sqrt),
            waits=[(sV, v1)], inc=sE)
    n.v_rsq = e1
    n.mark[("rsqdone", t)] = e1
    if "rsq" in dbg and t == 0:
        dbg_dump(n, "rsq", n.rsq_f[:], [(sE, e1)])


def smalls_phase(n, t):
    sT, sV, sE = n.sT, n.sV, n.sE
    P = n.P
    dbg = n.dbg
    v_ba = n.v_ba

    e_b = op(n, "se", lambda e: e.activation(n.beta_f[0:32, :], n.ba_f[0:32, :],
                                             AF.Sigmoid), waits=[(sV, v_ba)], inc=sE)
    n.mark[("sigb", t)] = e_b
    op(n, "ve", lambda e: e.tensor_scalar_add(n.sm1[32:64, :], n.ba_f[32:64, :],
                                              n.dtb[32:64, 0:1]), waits=[(sV, v_ba)])
    v_u = op(n, "ve", lambda e: e.tensor_scalar(n.sm2[32:64, :], n.sm1[32:64, :],
                                                0.0, 0.0, op0=ALU.max, op1=ALU.bypass),
             inc=sV, k=2)
    e_a = op(n, "se", lambda e: e.activation(n.sm3[32:64, :], n.sm1[32:64, :], AF.Abs),
             waits=[(sV, v_u)], inc=sE)
    e_e = op(n, "se", lambda e: e.activation(n.sm3[32:64, :], n.sm3[32:64, :],
                                             AF.Exp, scale=-1.0), inc=sE)
    e_l = op(n, "se", lambda e: e.activation(n.sm3[32:64, :], n.sm3[32:64, :],
                                             AF.Ln, bias=1.0), inc=sE)
    op(n, "ve", lambda e: e.tensor_add(n.sm2[32:64, :], n.sm2[32:64, :],
                                       n.sm3[32:64, :]), waits=[(sE, e_l)])
    v_g = op(n, "ve", lambda e: e.tensor_scalar_mul(n.g_f[32:64, :], n.sm2[32:64, :],
                                                    n.nA[32:64, 0:1]), inc=sV, k=2)
    v_gc = None
    for c in range(T // 64):
        wlc = [(sV, v_g)] if c == 0 else []
        if c == 0:
            wlc.append((n.sG, n.mark.get(("gathers", t - 1), 0)))
        v_gc = op(n, "ve", lambda e, c=c: e.tensor_tensor_scan(
            n.gc_f[32:64, c * 64:(c + 1) * 64], n.g_f[32:64, c * 64:(c + 1) * 64],
            n.g_f[32:64, c * 64:(c + 1) * 64], 0.0, op0=ALU.add, op1=ALU.bypass),
            waits=wlc, inc=sV)
    n.v_gc = v_gc
    if "gc" in dbg and t == 0:
        dbg_dump(n, "gc", n.gc_f[32:64, :], [(sV, v_gc)])
    if "beta" in dbg and t == 0:
        dbg_dump(n, "beta", n.beta_f[0:32, :], [(sE, e_b)])
    if "g" in dbg and t == 0:
        dbg_dump(n, "g", n.g_f[32:64, :], [(sV, v_g)])

    sv_prev = n.mark.get(("statevec", t - 1), 0)
    prev_t4 = 0
    prev_vgl = sv_prev
    prev_vgct = prev_egam = prev_vbt = prev_vdec = prev_vrk = 0
    for p in range(NPAIR):
        po = p * 32
        t1 = op(n, "te", lambda e, p=p: e.transpose(
            P[5][0:128, 0:32], n.gc_f[32:64, p * 128:(p + 1) * 128],
            n.identf[32:64, 32:64]),
            waits=[(sV, v_gc), (sV, sv_prev), (sV, prev_vgct),
                   (sE, prev_egam if p else n.mark.get(("egam", t - 1), 0))],
            inc=sT)
        op(n, "ve", lambda e, po=po: e.tensor_copy(n.gc_tok[:, po:po + 32],
                                                   P[5][0:128, 0:32]),
           waits=[(sT, t1)])
        v_gct = op(n, "ve", lambda e, po=po: e.tensor_copy(
            n.t32a[:, 0:32], P[5][0:128, 0:32]), inc=sV, k=2)
        e_gam = op(n, "se", lambda e, po=po: e.activation(
            n.gam_tok[:, po:po + 32], P[5][0:128, 0:32], AF.Exp),
            waits=[(sV, v_gct)], inc=sE)
        n.mark[("egam", t)] = e_gam
        prev_vgct, prev_egam = v_gct, e_gam
        t2 = op(n, "te", lambda e, p=p: e.transpose(
            P[5][0:128, 32:64], n.beta_f[0:32, p * 128:(p + 1) * 128],
            n.identf[0:32, 0:32]),
            waits=[(sE, e_b), (sE, e_gam), (sV, prev_vbt)], inc=sT)
        v_bt = op(n, "ve", lambda e, po=po: e.tensor_copy(
            n.beta_tok[:, po:po + 32], P[5][0:128, 32:64]), waits=[(sT, t2)], inc=sV)
        prev_vbt = v_bt
        for c in range(2):
            cc = p * 128 + c * 64 + 63
            op(n, "ve", lambda e, c=c, cc=cc: e.tensor_scalar_mul(
                n.sm1[32:64, c * 64:(c + 1) * 64], n.onessm[32:64, 0:64],
                n.gc_f[32:64, cc:cc + 1]), waits=[(sV, v_gc), (sT, prev_t4)])
        v_glb = op(n, "ve", lambda e, p=p: e.tensor_sub(
            n.sm2[32:64, 0:128], n.sm1[32:64, 0:128],
            n.gc_f[32:64, p * 128:(p + 1) * 128]), inc=sV, k=3)
        e_d = op(n, "se", lambda e: e.activation(n.sm3[32:64, 0:128],
                                                 n.sm2[32:64, 0:128], AF.Exp),
                 waits=[(sV, v_glb)], inc=sE)
        t3 = op(n, "te", lambda e: e.transpose(P[5][0:128, 64:96],
                                               n.sm3[32:64, 0:128],
                                               n.identf[32:64, 32:64]),
                waits=[(sE, e_d), (sV, prev_vdec)], inc=sT)
        v_dec = op(n, "ve", lambda e, po=po: e.tensor_copy(
            n.dec_tok[:, po:po + 32], P[5][0:128, 64:96]),
            waits=[(sT, t3)], inc=sV)
        prev_vdec = v_dec
        for c in range(2):
            cc = p * 128 + c * 64 + 63
            ts_gl = op(n, "ve", lambda e, cc=cc: e.tensor_scalar_mul(
                n.sm1[32:64, 0:128], n.onessm[32:64, 0:128],
                n.gc_f[32:64, cc:cc + 1]),
                waits=[(sV, v_gc), (sT, prev_t4)], inc=sV)
            e_g2 = op(n, "se", lambda e: e.activation(
                n.sm1[32:64, 0:128], n.sm1[32:64, 0:128], AF.Exp),
                waits=[(sV, ts_gl)], inc=sE)
            t4 = op(n, "te", lambda e: e.transpose(
                P[5][0:128, 96:128], n.sm1[32:64, 0:128],
                n.identf[32:64, 32:64]),
                waits=[(sE, e_g2), (sV, prev_vgl)], inc=sT)
            prev_t4 = t4
            v_gl = op(n, "ve", lambda e, po=po, c=c, p=p: e.tensor_copy(
                n.glast_bc[:, (p * 2 + c) * 32:(p * 2 + c) * 32 + 32],
                P[5][0:128, 96:128]), waits=[(sT, t4)], inc=sV)
            prev_vgl = v_gl
        t5 = op(n, "te", lambda e, p=p: e.transpose(
            P[6][0:128, 0:16], n.rsq_f[0:16, p * 128:(p + 1) * 128],
            n.identf[0:16, 0:16]),
            waits=[(sE, n.v_rsq), (sV, sv_prev), (sV, prev_vrk)], inc=sT)
        t6 = op(n, "te", lambda e, p=p: e.transpose(
            P[6][0:128, 16:32], n.rsq_f[0:16, 256 + p * 128: 256 + (p + 1) * 128],
            n.identf[0:16, 0:16]), inc=sT)
        n.mark[("rsqT", t)] = t6
        po16 = p * 16
        op(n, "ve", lambda e, po16=po16: e.tensor_copy(
            n.rq_tok[:, po16:po16 + 16], P[6][0:128, 0:16]), waits=[(sT, sT.n)])
        v_rk = op(n, "ve", lambda e, po16=po16: e.tensor_copy(
            n.rk_tok[:, po16:po16 + 16], P[6][0:128, 16:32]), inc=sV, k=2)
        prev_vrk = v_rk
        def rkx(pp):
            return n.rk_tok[:, pp * 16:(pp + 1) * 16].rearrange(
                "p (a b) -> p a b", b=1).broadcast_to([128, 16, 2])

        def v3d(ap):
            return ap.rearrange("p (a b) -> p a b", b=2)
        op(n, "ve", lambda e, po=po: e.tensor_mul(
            n.t32a[:, 0:32], n.beta_tok[:, po:po + 32], n.gam_tok[:, po:po + 32]),
            waits=[(sV, v_bt), (sV, v_rk), (sE, e_gam)])
        op(n, "ve", lambda e, po=po, p=p: e.tensor_tensor(
            v3d(n.bgr_tok[:, po:po + 32]), v3d(n.t32a[:, 0:32]), rkx(p),
            op=ALU.mult))
        op(n, "ve", lambda e, po=po, p=p: e.tensor_tensor(
            v3d(n.grq_tok[:, po:po + 32]), v3d(n.gam_tok[:, po:po + 32]),
            n.rq_tok[:, p * 16:(p + 1) * 16].rearrange(
                "p (a b) -> p a b", b=1).broadcast_to([128, 16, 2]),
            op=ALU.mult))
        op(n, "ve", lambda e, po=po, p=p: e.tensor_tensor(
            v3d(n.brk_tok[:, po:po + 32]), v3d(n.beta_tok[:, po:po + 32]), rkx(p),
            op=ALU.mult))
        v_der = op(n, "ve", lambda e, po=po, p=p: e.tensor_tensor(
            v3d(n.dec_tok[:, po:po + 32]), v3d(n.dec_tok[:, po:po + 32]), rkx(p),
            op=ALU.mult), waits=[(sV, v_dec)], inc=sV, k=5)
        n.mark[("smalls", t, p)] = v_der
    if "gctok" in dbg and t == 0:
        dbg_dump(n, "gctok", n.gc_tok[:], [(sV, n.sV.n)])


def delta_prep(n, t, p):
    sT, sV, sE, sG = n.sT, n.sV, n.sE, n.sG
    P = n.P
    dbg = n.dbg
    po = p * 32
    po16 = p * 16

    n.growv = {}
    for g in range(8):
        wl = [(n.sV, n.v_gc)]
        if g >= 1:
            wl.append((sT, n.mark[("rowmm", t, p, g - 1)]))
        elif p > 0:
            wl.append((sT, n.mark[("rowmm", t, p - 1, 7)]))
        elif ("rowmm", t - 1, NPAIR - 1, 7) in n.mark:
            wl.append((sT, n.mark[("rowmm", t - 1, NPAIR - 1, 7)]))
        n.growv[g] = op(n, "sy", lambda e, g=g, p=p: e.dma_start(
            out=n.grow[0:1, 0:512],
            in_=n.gc_f[32 + g * 4: 36 + g * 4, p * 128:(p + 1) * 128]),
            waits=wl, inc=sG, k=16)
        group_prep(n, t, p, g)

    for g in range(8):
        for j in range(4):
            h = g * 4 + j
            wl = [(n.sE, n.mark[("silu", t, 32 + h)])]
            if g == 0 and j == 0:
                wl.append((sV, n.mark[("smalls", t, p)]))
            if j == 0 and g > 0:
                wl.append((sV, n.mark[("vT", t, p, g - 1)]))
            tv = op(n, "te", lambda e, h=h, j=j, p=p: e.transpose(
                P[6][:].bitcast(BF16)[:, j * 128:(j + 1) * 128],
                n.vc_sb[:, h * T + p * 128: h * T + p * 128 + 128], n.identc[:]),
                waits=wl, inc=sT)
        vv = op(n, "ve", lambda e, g=g: e.tensor_copy(
            n.vT_sb[:, g * 512:(g + 1) * 512], P[6][:].bitcast(BF16)[:, 0:512]),
            waits=[(sT, tv)], inc=sV)
        n.mark[("vT", t, p, g)] = vv
    for g in range(4):
        for j in range(4):
            kv = g * 4 + j
            wl = [(n.sE, n.mark[("silu", t, 16 + kv)])]
            if j == 0:
                wl.append((sV, n.mark[("vT", t, p, 7)] if g == 0
                           else n.mark[("kT", t, p, g - 1)]))
            tv = op(n, "te", lambda e, kv=kv, j=j, p=p: e.transpose(
                P[6][:].bitcast(BF16)[:, j * 128:(j + 1) * 128],
                n.k_sb[:, kv * T + p * 128: kv * T + p * 128 + 128], n.identc[:]),
                waits=wl, inc=sT)
        vv = op(n, "ve", lambda e, g=g: e.tensor_copy(
            n.kT_sb[:, g * 512:(g + 1) * 512], P[6][:].bitcast(BF16)[:, 0:512]),
            waits=[(sT, tv)], inc=sV)
        n.mark[("kT", t, p, g)] = vv


def group_prep(n, t, p, g):
    sT, sV, sE, sG = n.sT, n.sV, n.sE, n.sG
    P = n.P
    po = p * 32
    po16 = p * 16
    h0 = 4 * g
    dbg = n.dbg
    bD, bS, bP_, bQ_, bV_ = P[0], P[1], P[2], P[3], P[4]

    def colAP(tensor, off, reps=128):
        return tensor[:, off:off + 4].rearrange("p (a b) -> p a b", b=1)\
            .broadcast_to([128, 4, reps])

    def maskAP():
        return n.maskSc[:].rearrange("p (a f) -> p a f", a=1)\
            .broadcast_to([128, 4, 128])

    def identAP():
        return n.identc[:].rearrange("p (a f) -> p a f", a=1)\
            .broadcast_to([128, 4, 128])

    def kvrepAP(tensor, base):
        return tensor[:, base:base + 256].rearrange("p (a f) -> p a f", a=2)\
            .rearrange("p (a r) f -> p a r f", r=1).broadcast_to([128, 2, 2, 128])

    wl = [(sG, n.growv[g]), (sV, n.mark.get(("argdone", t, p, g - 1),
                                            n.mark.get(("smalls", t, p), 0)))]
    if g == 0:
        wl.append((sV, n.mark[("smalls", t, p)]))
    rv = op(n, "te", lambda e: e.matmul(
        bD[:, 0:512], n.ones1[0:1, :], n.grow[0:1, 0:512],
        start=True, stop=True), waits=wl, inc=sT)
    n.mark[("rowmm", t, p, g)] = rv

    kv0 = 2 * g
    wl2 = [(sE, n.mark[("silu", t, 16 + kv0)]), (sE, n.mark[("silu", t, 17 + kv0)]),
           (sE, n.mark[("silu", t, kv0)]), (sE, n.mark[("silu", t, 1 + kv0)]),
           (sV, n.mark.get(("scoresdone", t, p, g - 1), 0))]
    for j, kv in enumerate((kv0, kv0 + 1)):
        ksl = n.k_sb[:, kv * T + p * 128: kv * T + p * 128 + 128]
        qsl = n.q_sb[:, kv * T + p * 128: kv * T + p * 128 + 128]
        op(n, "te", lambda e, j=j, ksl=ksl: e.matmul(
            bS[:, j * 128:(j + 1) * 128], ksl, ksl,
            start=(j == 0), stop=False), waits=(wl2 if j == 0 else ()))
        sv = op(n, "te", lambda e, j=j, ksl=ksl, qsl=qsl: e.matmul(
            bS[:, (2 + j) * 128:(3 + j) * 128], ksl, qsl,
            start=False, stop=(j == 1)), inc=sT)
    n.mark[("scores", t, p, g)] = sv

    op(n, "ve", lambda e, po=po, h0=h0: e.tensor_tensor(
        n.t32a[:].rearrange("p (a f) -> p a f", a=4),
        bD[:, 0:512].rearrange("p (a f) -> p a f", a=4),
        colAP(n.gc_tok, po + h0), op=ALU.subtract), waits=[(sT, rv)])
    v_arg = op(n, "ve", lambda e: e.tensor_tensor(
        n.arg_sb[:].rearrange("p (a f) -> p a f", a=4),
        n.t32a[:].rearrange("p (a f) -> p a f", a=4),
        maskAP(), op=ALU.add), inc=sV, k=2)
    n.mark[("argdone", t, p, g)] = v_arg
    e_es = op(n, "se", lambda e: e.activation(n.es_sb[:], n.arg_sb[:], AF.Exp),
              waits=[(sV, v_arg)], inc=sE)
    v_ei = op(n, "ve", lambda e: e.tensor_tensor(
        n.ei_sb[:].rearrange("p (a f) -> p a f", a=4),
        n.es_sb[:].rearrange("p (a f) -> p a f", a=4),
        identAP(), op=ALU.add), waits=[(sE, e_es)], inc=sV)
    op(n, "ve", lambda e, g=g: e.tensor_tensor(
        n.t32b[:].rearrange("p (a b f) -> p a b f", a=2, b=2),
        kvrepAP(bS, 256),
        n.ei_sb[:].rearrange("p (a b f) -> p a b f", a=2, b=2), op=ALU.mult),
        waits=[(sT, sv), (sV, v_ei)])
    v_A = op(n, "ve", lambda e, p=p, h0=h0, po16=po16: e.tensor_tensor(
        n.A_sb[:, h0 * 128:(h0 + 4) * 128].rearrange("p (a b f) -> p a b f", a=2, b=2),
        n.t32b[:].rearrange("p (a b f) -> p a b f", a=2, b=2),
        n.rk_tok[:, po16 + h0 // 2: po16 + h0 // 2 + 2].rearrange(
            "p (a b c) -> p a b c", b=1, c=1).broadcast_to([128, 2, 2, 128]),
        op=ALU.mult), inc=sV, k=2)
    n.mark[("A", t, p, g)] = v_A
    op(n, "ve", lambda e, g=g: e.tensor_tensor(
        n.t32b[:].rearrange("p (a b f) -> p a b f", a=2, b=2),
        kvrepAP(bS, 0),
        n.es_sb[:].rearrange("p (a b f) -> p a b f", a=2, b=2), op=ALU.mult))
    v_pt = op(n, "ve", lambda e, po16=po16, h0=h0: e.tensor_tensor(
        n.ptil_sb[:].rearrange("p (a b f) -> p a b f", a=2, b=2),
        n.t32b[:].rearrange("p (a b f) -> p a b f", a=2, b=2),
        n.rk_tok[:, po16 + h0 // 2: po16 + h0 // 2 + 2].rearrange(
            "p (a b c) -> p a b c", b=1, c=1).broadcast_to([128, 2, 2, 128]),
        op=ALU.mult), inc=sV, k=2)
    n.mark[("scoresdone", t, p, g)] = v_pt

    def pw(i):
        return n.pw_sb[:, i * 512:(i + 1) * 512]

    for j in range(4):
        wlq1 = [(sV, v_pt)] if j == 0 else []
        if j == 0:
            wlq1.append((sV, n.mark.get(("pwdone", t, p, g - 1),
                                        n.mark.get(("pwdone", t, p - 1, 7),
                                                   n.mark.get(("pwdoneT", t - 1), 0)))))
        tv = op(n, "te", lambda e, j=j: e.transpose(
            bQ_[:].bitcast(BF16)[:, j * 128:(j + 1) * 128],
            n.ptil_sb[:, j * 128:(j + 1) * 128],
            n.identc[:]), waits=wlq1, inc=sT)
    v_q1 = op(n, "ve", lambda e, po=po, h0=h0: e.tensor_tensor(
        pw(0).rearrange("p (a f) -> p a f", a=4),
        bQ_[:].bitcast(BF16)[:, 0:512].rearrange("p (a f) -> p a f", a=4),
        colAP(n.brk_tok, po + h0), op=ALU.mult), waits=[(sT, tv)], inc=sV)
    for j in range(4):
        tv = op(n, "te", lambda e, j=j: e.transpose(
            bP_[:].bitcast(BF16)[:, j * 128:(j + 1) * 128],
            pw(0)[:, j * 128:(j + 1) * 128],
            n.identc[:]), waits=([(sV, v_q1)] if j == 0 else ()), inc=sT)
    v_p1 = op(n, "ve", lambda e: e.tensor_copy(pw(1), bP_[:].bitcast(BF16)[:, 0:512]),
              waits=[(sT, tv)], inc=sV)
    lvl = [(0, 1, 2, 3), (2, 3, 4, 5), (4, 5, 6, 7), (6, 7, 8, 9), (8, 9, 10, None)]
    prev_ev = v_p1
    for (qi, pi, qo, po_) in lvl:
        for j in range(4):
            tv = op(n, "te", lambda e, j=j, qi=qi, pi=pi: e.matmul(
                bQ_[:, j * 128:(j + 1) * 128], pw(pi)[:, j * 128:(j + 1) * 128],
                pw(qi)[:, j * 128:(j + 1) * 128], start=(j == 0), stop=(j == 3)),
                waits=([(sV, prev_ev)] if j == 0 else ()), inc=sT)
        v_q = op(n, "ve", lambda e, qo=qo: e.tensor_copy(pw(qo), bQ_[:, 0:512]),
                 waits=[(sT, tv)], inc=sV)
        prev_ev = v_q
        if po_ is not None:
            for j in range(4):
                tv = op(n, "te", lambda e, j=j, qo=qo: e.transpose(
                    bP_[:].bitcast(BF16)[:, j * 128:(j + 1) * 128],
                    pw(qo)[:, j * 128:(j + 1) * 128],
                    n.identc[:]), waits=([(sV, v_q)] if j == 0 else ()), inc=sT)
            prev_ev = op(n, "ve", lambda e, po_=po_: e.tensor_copy(
                pw(po_), bP_[:].bitcast(BF16)[:, 0:512]), waits=[(sT, tv)], inc=sV)

    n.mark[("pwdone", t, p, g)] = prev_ev
    if (p, g) == (NPAIR - 1, 7):
        n.mark[("pwdoneT", t)] = prev_ev
    v_v = op(n, "ve", lambda e: e.tensor_tensor(
        n.zz_sb[:].rearrange("p (a f) -> p a f", a=4),
        pw(10).rearrange("p (a f) -> p a f", a=4), identAP(), op=ALU.add),
        waits=[(sV, prev_ev)], inc=sV)
    for qi, last in ((8, 0), (6, 0), (4, 0), (2, 0), (0, 1)):
        for j in range(4):
            tv = op(n, "te", lambda e, j=j, qi=qi: e.matmul(
                bV_[:, j * 128:(j + 1) * 128], pw(qi)[:, j * 128:(j + 1) * 128],
                n.zz_sb[:, j * 128:(j + 1) * 128], start=(j == 0), stop=(j == 3)),
                waits=([(sV, v_v)] if j == 0 else ()), inc=sT)
        if not last:
            v_v = op(n, "ve", lambda e: e.tensor_add(
                n.zz_sb[:], n.zz_sb[:], bV_[:, 0:512]), waits=[(sT, tv)], inc=sV)
        else:
            v_v = op(n, "ve", lambda e, h0=h0: e.tensor_sub(
                n.V_sb[:, h0 * 128:(h0 + 4) * 128], n.zz_sb[:], bV_[:, 0:512]),
                waits=[(sT, tv)], inc=sV)
    n.mark[("V", t, p, g)] = v_v
    if "V0" in dbg and (t, p, g) == (0, 0, 0):
        dbg_dump(n, "V0", n.V_sb[:, 0:128], [(sV, v_v)])
    if "A0" in dbg and (t, p, g) == (0, 0, 0):
        dbg_dump(n, "A0", n.A_sb[:, 0:128], [(sV, n.mark[("A", t, p, g)])])


def delta_state(n, t, p, ci):
    sT, sV, sE = n.sT, n.sV, n.sE
    P = n.P
    dbg = n.dbg
    po = p * 32
    po16 = p * 16
    cb = ci * 64
    bK, bO1, bO2 = P[0], P[1], P[2]

    def colAP4(tensor, off, reps=128):
        return tensor[cb:cb + 64, off:off + 4].rearrange("p (a b) -> p a b", b=1)\
            .broadcast_to([64, 4, reps])

    def colAPf(tensor, off, reps=128):
        return tensor[:, off:off + 4].rearrange("p (a b) -> p a b", b=1)\
            .broadcast_to([128, 4, reps])

    def rqAP(pn=64):
        pass

    for g in range(8):
        h0 = 4 * g
        kv0 = 2 * g
        wlk = [(sV, n.mark[("V", t, p, 7)]),
               (sV, n.mark.get(("Sbready", t, p, ci, g), n.v_smallconst)),
               (sV, n.mark.get(("uev", t, p, ci, g - 1), 0))]
        tv = None
        for j in range(4):
            h = h0 + j
            kv = kv0 + j // 2
            tv = op(n, "te", lambda e, j=j, h=h, kv=kv, p=p, cb=cb: e.matmul(
                bK[cb:cb + 64, j * 128:(j + 1) * 128],
                n.k_sb[:, kv * T + p * 128 + cb: kv * T + p * 128 + cb + 64],
                n.Sb_sb[:, h * 128:(h + 1) * 128],
                start=(j == 0), stop=(j == 3)), waits=(wlk if j == 0 else ()),
                inc=sT)
        op(n, "ve", lambda e, po=po, h0=h0, cb=cb: e.tensor_tensor(
            n.t32a[cb:cb + 64, :].rearrange("p (a f) -> p a f", a=4),
            bK[cb:cb + 64, 0:512].rearrange("p (a f) -> p a f", a=4),
            colAP4(n.bgr_tok, po + h0), op=ALU.mult), waits=[(sT, tv)])
        op(n, "ve", lambda e, po=po, h0=h0, g=g, cb=cb: e.tensor_tensor(
            n.t32b[cb:cb + 64, :].rearrange("p (a f) -> p a f", a=4),
            n.vT_sb[cb:cb + 64, g * 512:(g + 1) * 512].rearrange(
                "p (a f) -> p a f", a=4),
            colAP4(n.beta_tok, po + h0), op=ALU.mult),
            waits=[(sV, n.mark[("vT", t, p, g)])])
        v_rhs = op(n, "ve", lambda e, cb=cb: e.tensor_sub(
            n.rhs_sb[cb:cb + 64, :], n.t32b[cb:cb + 64, :],
            n.t32a[cb:cb + 64, :]), inc=sV, k=3)
        n.mark[("rhsdone", t, p, ci, g)] = v_rhs
        tv = None
        for j in range(4):
            h = h0 + j
            tv = op(n, "te", lambda e, j=j, h=h, cb=cb: e.matmul(
                bK[cb:cb + 64, j * 128:(j + 1) * 128],
                n.V_sb[cb:cb + 64, h * 128 + cb: h * 128 + cb + 64],
                n.rhs_sb[cb:cb + 64, j * 128:(j + 1) * 128],
                start=(j == 0), stop=(j == 3)),
                waits=([(sV, v_rhs)] if j == 0 else ()), inc=sT)
        n.mark[("ujmm", t, p, ci, g)] = tv
        v_u = op(n, "ve", lambda e, cb=cb: e.tensor_copy(
            n.u_sb[cb:cb + 64, :], bK[cb:cb + 64, 0:512]), waits=[(sT, tv)], inc=sV)
        n.mark[("uev", t, p, ci, g)] = v_u
        if "u0" in dbg and (t, p, ci, g) == (0, 0, 0, 0):
            dbg_dump(n, "u0", n.u_sb[0:64, :], [(sV, v_u)])
        wlq = [(sV, n.mark.get(("coreg", t, p, ci, g - 1), 0))]
        tvq = None
        for j in range(4):
            h = h0 + j
            kv = kv0 + j // 2
            tvq = op(n, "te", lambda e, j=j, h=h, kv=kv, p=p, cb=cb: e.matmul(
                bO1[cb:cb + 64, j * 128:(j + 1) * 128],
                n.q_sb[:, kv * T + p * 128 + cb: kv * T + p * 128 + cb + 64],
                n.Sb_sb[:, h * 128:(h + 1) * 128],
                start=(j == 0), stop=(j == 3)), waits=(wlq if j == 0 else ()),
                inc=sT)
        wla = [(sV, v_u), (sV, n.mark[("A", t, p, g)]),
               (sV, n.mark.get(("Sbready", t, p, ci + 1, g - 1), 0))]
        tva = None
        for j in range(4):
            h = h0 + j
            tva = op(n, "te", lambda e, j=j, h=h, cb=cb: e.matmul(
                bO2[cb:cb + 64, j * 128:(j + 1) * 128],
                n.A_sb[cb:cb + 64, h * 128 + cb: h * 128 + cb + 64],
                n.u_sb[cb:cb + 64, j * 128:(j + 1) * 128],
                start=(j == 0), stop=(j == 3)),
                waits=(wla if j == 0 else ()), inc=sT)
        v_o1 = op(n, "ve", lambda e, po=po, h0=h0, cb=cb: e.tensor_tensor(
            n.t32a[cb:cb + 64, :].rearrange("p (a f) -> p a f", a=4),
            bO1[cb:cb + 64, 0:512].rearrange("p (a f) -> p a f", a=4),
            colAP4(n.grq_tok, po + h0), op=ALU.mult), waits=[(sT, tvq)], inc=sV)
        n.mark[("oe1", t, p, ci, g)] = v_o1
        v_o2 = op(n, "ve", lambda e, po16=po16, h0=h0, cb=cb: e.tensor_tensor(
            n.t32b[cb:cb + 64, :].rearrange("p (a b f) -> p a b f", a=2, b=2),
            bO2[cb:cb + 64, 0:512].rearrange("p (a b f) -> p a b f", a=2, b=2),
            n.rq_tok[cb:cb + 64, po16 + h0 // 2: po16 + h0 // 2 + 2].rearrange(
                "p (a b c) -> p a b c", b=1, c=1).broadcast_to([64, 2, 2, 128]),
            op=ALU.mult), waits=[(sT, tva)], inc=sV)
        n.mark[("oe2", t, p, ci, g)] = v_o2
        v_os = op(n, "ve", lambda e, cb=cb: e.tensor_add(
            n.o1_sb[cb:cb + 64, :], n.t32a[cb:cb + 64, :], n.t32b[cb:cb + 64, :]),
            inc=sV)
        op(n, "ve", lambda e, cb=cb: e.tensor_mul(
            n.on_sb[cb:cb + 64, :], n.o1_sb[cb:cb + 64, :], n.o1_sb[cb:cb + 64, :]))
        op(n, "ve", lambda e, cb=cb: e.reduce_sum(
            n.lncol[cb:cb + 64, 8:12],
            n.on_sb[cb:cb + 64, :].rearrange("p (a f) -> p a f", a=4), axis=AXL.X))
        op(n, "ve", lambda e, cb=cb: e.tensor_scalar(
            n.lncol[cb:cb + 64, 8:12], n.lncol[cb:cb + 64, 8:12],
            1.0 / DV, RMS_EPS, op0=ALU.mult, op1=ALU.add))
        v_rr = op(n, "ve", lambda e, cb=cb: e.reciprocal(
            n.lncol[cb:cb + 64, 8:12], n.lncol[cb:cb + 64, 8:12]), inc=sV, k=4)
        e_sq = op(n, "se", lambda e, cb=cb: e.activation(
            n.lncol[cb:cb + 64, 12:16], n.lncol[cb:cb + 64, 8:12], AF.Sqrt),
            waits=[(sV, v_rr)], inc=sE)
        v_on = op(n, "ve", lambda e, cb=cb: e.tensor_tensor(
            n.on_sb[cb:cb + 64, :].rearrange("p (a f) -> p a f", a=4),
            n.o1_sb[cb:cb + 64, :].rearrange("p (a f) -> p a f", a=4),
            n.lncol[cb:cb + 64, 12:16].rearrange("p (a b) -> p a b", b=1)
                .broadcast_to([64, 4, 128]), op=ALU.mult),
            waits=[(sE, e_sq)], inc=sV)
        tvc = None
        for j in range(4):
            tvc = op(n, "te", lambda e, j=j, cb=cb: e.transpose(
                bO1[:].bitcast(BF16)[:, j * 64:(j + 1) * 64],
                n.on_sb[cb:cb + 64, j * 128:(j + 1) * 128],
                n.identc[cb:cb + 64, cb:cb + 64]),
                waits=([(sV, v_on)] if j == 0 else ()), inc=sT)
        zoff = p * 128 + cb

        def vc_view(e_, h0=h0, zoff=zoff):
            return n.vc_sb[:, h0 * T:(h0 + 4) * T].rearrange(
                "p (a f) -> p a f", a=4)[:, :, zoff:zoff + 64]

        def z_view(e_, h0=h0, zoff=zoff):
            return n.z_sb[:, h0 * T:(h0 + 4) * T].rearrange(
                "p (a f) -> p a f", a=4)[:, :, zoff:zoff + 64]

        v_cg = op(n, "ve", lambda e, h0=h0, zoff=zoff: e.scalar_tensor_tensor(
            vc_view(e, h0, zoff),
            bO1[:].bitcast(BF16)[:, 0:256].rearrange("p (a f) -> p a f", a=4),
            n.nw[:, 0:1], z_view(e, h0, zoff),
            op0=ALU.mult, op1=ALU.mult),
            waits=[(sT, tvc), (sE, n.mark[("zev", t, 31)])],
            inc=sV)
        n.mark[("coreg", t, p, ci, g)] = v_cg
        v_kh = op(n, "ve", lambda e, g=g, po=po, h0=h0, cb=cb: e.tensor_tensor(
            n.kh_sb[cb:cb + 64, :].rearrange("p (a b f) -> p a b f", a=2, b=2),
            n.kT_sb[cb:cb + 64, g * 256:(g + 1) * 256].rearrange(
                "p (a r f) -> p a r f", a=2, r=1).broadcast_to([64, 2, 2, 128]),
            n.dec_tok[cb:cb + 64, po + h0: po + h0 + 4].rearrange(
                "p (a b c) -> p a b c", b=2, c=1).broadcast_to([64, 2, 2, 128]),
            op=ALU.mult), waits=[(sV, n.mark[("kT", t, p, g // 2)])], inc=sV)
        tvd = None
        for j in range(4):
            h = h0 + j
            tvd = op(n, "te", lambda e, j=j, cb=cb: e.matmul(
                bO2[:, j * 128:(j + 1) * 128],
                n.kh_sb[cb:cb + 64, j * 128:(j + 1) * 128],
                n.u_sb[cb:cb + 64, j * 128:(j + 1) * 128],
                start=(j == 0), stop=(j == 3)),
                waits=([(sV, v_kh), (sV, v_o2)] if j == 0 else ()), inc=sT)
        gl_off = (p * 2 + ci) * 32 + h0
        op(n, "ve", lambda e, gl_off=gl_off, h0=h0: e.tensor_tensor(
            n.t32a[:].rearrange("p (a f) -> p a f", a=4),
            n.S_sb[:, h0 * 128:(h0 + 4) * 128].rearrange("p (a f) -> p a f", a=4),
            colAPf(n.glast_bc, gl_off), op=ALU.mult), waits=[(sT, tvd)])
        op(n, "ve", lambda e, h0=h0: e.tensor_add(
            n.S_sb[:, h0 * 128:(h0 + 4) * 128], n.t32a[:], bO2[:, 0:512]))
        v_sb = op(n, "ve", lambda e, h0=h0: e.tensor_copy(
            n.Sb_sb[:, h0 * 128:(h0 + 4) * 128],
            n.S_sb[:, h0 * 128:(h0 + 4) * 128]), inc=sV, k=3)
        n.mark[("Sbready", t, p, ci + 1, g)] = v_sb
        if ci == 1:
            if p == NPAIR - 1:
                n.mark[("Sbready", t + 1, 0, 0, g)] = v_sb
            else:
                n.mark[("Sbready", t, p + 1, 0, g)] = v_sb
    n.mark[("SbT", t, p, ci + 1)] = sV.n
    n.mark[("statevec", t)] = sV.n
    if "S0" in dbg and (t, p, ci) == (0, NPAIR - 1, 1):
        dbg_dump(n, "S0", n.S_sb[:], [(sV, n.sV.n)])
    if "core" in dbg and (t, p, ci) == (0, NPAIR - 1, 1):
        dbg_dump(n, "core", n.vc_sb[:], [(sV, n.sV.n)])


def wout_phase(n, t):
    sT, sV, sE = n.sT, n.sV, n.sE
    P = n.P
    ws = n.ws
    par = t % 2
    xoff = par * (T // 128) * D
    v_core = n.mark[("statevec", t)]

    tv = None
    for h in range(HV):
        half, wv = ws.load(n.wo_b[h * 128:(h + 1) * 128, :],
                           waits=([(sV, v_core)] if h == 0 else ()))
        for tc in range(T // 128):
            for nb in range(4):
                bank = P[tc * 4 + nb]
                wl = []
                if h == 0 and nb == 0:
                    wl = [wv, (sV, v_core),
                          (sV, n.mark.get(("woutev", t - 1), 0))]
                elif nb == 0 and tc == 0:
                    wl = [wv]
                tv = op(n, "te", lambda e, h=h, tc=tc, nb=nb, half=half, bank=bank:
                        e.matmul(
                            bank[:, 0:512],
                            n.vc_sb[:, h * T + tc * 128: h * T + tc * 128 + 128],
                            n.wbuf[:, half * 2048 + nb * 512:
                                   half * 2048 + (nb + 1) * 512],
                            start=(h == 0), stop=(h == HV - 1)),
                        waits=wl, inc=sT)
        ws.done(tv)
    n.mark[("woutmm", t)] = tv
    v = None
    for tc in range(T // 128):
        for nb in range(4):
            bank = P[tc * 4 + nb]
            xs = n.x_sb[:, xoff + tc * D + nb * 512: xoff + tc * D + (nb + 1) * 512]
            v = op(n, "ve", lambda e, bank=bank, xs=xs: e.tensor_add(
                xs, xs, bank[:, 0:512]), waits=([(sT, tv)] if (tc, nb) == (0, 0)
                                                else ()), inc=sV, wide=True)
    n.mark[("woutev", t)] = v
    if "x1" in n.dbg and t == 0:
        dbg_dump(n, "x1", n.x_sb[:, xoff:xoff + (T // 128) * D], [(n.sV, v)])


def ffn_phase(n, t):
    sT, sV, sE, sO = n.sT, n.sV, n.sE, n.sO
    P = n.P
    ws = n.ws
    par = t % 2
    xoff = par * (T // 128) * D
    v_res = n.mark[("woutev", t)]

    store_vals = []
    prev_vxq = v_res
    for tc in range(T // 128):
        xs = n.x_sb[:, xoff + tc * D: xoff + (tc + 1) * D]
        op(n, "ve", lambda e, xs=xs: e.reduce_sum(n.lncol[:, 0:1], xs, axis=AXL.X),
           waits=[(sV, v_res)])
        op(n, "ve", lambda e: e.tensor_scalar_mul(n.lncol[:, 2:3], n.lncol[:, 0:1],
                                                  1.0 / D))
        v_m = op(n, "ve", lambda e: e.tensor_mul(
            n.lncol[:, 4:5], n.lncol[:, 2:3], n.lncol[:, 2:3]), inc=sV, k=3)
        e_sq = op(n, "se", lambda e, xs=xs: e.activation(
            n.ffnq[:], xs, AF.Square, accum_out=n.lncol[:, 1:2]),
            waits=[(sV, prev_vxq)], inc=sE)
        op(n, "ve", lambda e: e.tensor_scalar_mul(n.lncol[:, 3:4], n.lncol[:, 1:2],
                                                  1.0 / D), waits=[(sE, e_sq)])
        op(n, "ve", lambda e: e.tensor_sub(n.lncol[:, 5:6], n.lncol[:, 3:4],
                                           n.lncol[:, 4:5]))
        op(n, "ve", lambda e: e.tensor_scalar_add(n.lncol[:, 6:7], n.lncol[:, 5:6],
                                                  LN_EPS))
        v_rc = op(n, "ve", lambda e: e.reciprocal(n.lncol[:, 7:8], n.lncol[:, 6:7]),
                  inc=sV, k=3)
        e_st = op(n, "se", lambda e: e.activation(n.lncol[:, 8:9], n.lncol[:, 7:8],
                                                  AF.Sqrt), waits=[(sV, v_rc)],
                  inc=sE)
        op(n, "ve", lambda e: e.tensor_mul(n.lncol[:, 9:10], n.lncol[:, 5:6],
                                           n.lncol[:, 7:8]), waits=[(sE, e_st)])
        op(n, "ve", lambda e: e.tensor_scalar_add(n.lncol[:, 9:10],
                                                  n.lncol[:, 9:10], NRM_EPS))
        v_r2 = op(n, "ve", lambda e: e.reciprocal(n.lncol[:, 10:11],
                                                  n.lncol[:, 9:10]), inc=sV, k=2)
        e_r2 = op(n, "se", lambda e: e.activation(n.lncol[:, 11:12],
                                                  n.lncol[:, 10:11], AF.Sqrt),
                  waits=[(sV, v_r2)], inc=sE)
        op(n, "ve", lambda e, xs=xs: e.tensor_scalar(
            n.ffnq[:], xs, n.lncol[:, 2:3], n.lncol[:, 8:9],
            op0=ALU.subtract, op1=ALU.mult), waits=[(sE, e_r2)], wide=True)
        v_u2 = op(n, "ve", lambda e: e.tensor_mul(n.ffnq[:], n.ffnq[:],
                                                  n.gamma_bc[:]), inc=sV, k=2,
                  wide=True)
        op(n, "ve", lambda e: e.reduce_max(n.lncol[:, 12:13], n.ffnq[:], axis=AXL.X,
                                           apply_absolute_value=True))
        op(n, "ve", lambda e: e.tensor_mul(n.lncol[:, 12:13], n.lncol[:, 12:13],
                                           n.lncol[:, 11:12]))
        op(n, "ve", lambda e: e.tensor_scalar_add(n.lncol[:, 13:14],
                                                  n.lncol[:, 12:13], QEPS))
        v_am = op(n, "ve", lambda e: e.reciprocal(n.lncol[:, 14:15],
                                                  n.lncol[:, 13:14]), inc=sV, k=4)
        op(n, "ve", lambda e: e.tensor_scalar_mul(n.lncol[:, 14:15],
                                                  n.lncol[:, 14:15], 127.0),
           waits=[(sV, v_am)])
        v_qs = op(n, "ve", lambda e: e.tensor_mul(n.lncol[:, 14:15],
                                                  n.lncol[:, 14:15],
                                                  n.lncol[:, 11:12]), inc=sV, k=2)
        op(n, "ve", lambda e: e.tensor_scalar_mul(n.ffnq[:], n.ffnq[:],
                                                  n.lncol[:, 14:15]),
           waits=[(sV, v_qs)], wide=True)
        op(n, "ve", lambda e: e.tensor_scalar(n.ffnq[:], n.ffnq[:], MAGIC, MAGIC,
                                              op0=ALU.add, op1=ALU.subtract),
           wide=True)
        v_xq = op(n, "ve", lambda e: e.tensor_scalar(
            n.xq_tok[:], n.ffnq[:], 127.0, -128.0, op0=ALU.min, op1=ALU.max),
            inc=sV, k=3, wide=True)
        prev_vxq = v_xq
        op(n, "ve", lambda e: e.tensor_mul(n.lncol[:, 15:16], n.lncol[:, 13:14],
                                           n.mw_bc[:, 0:1]))
        v_osc = op(n, "ve", lambda e: e.tensor_scalar_mul(
            n.lncol[:, 15:16], n.lncol[:, 15:16], 1.0 / 127.0), inc=sV, k=2)
        for kk in range(NKC):
            bank = P[kk % 2]
            if kk < 2:
                wl = [(sV, v_xq), (sV, n.mark.get(("xqev", kk - 2 + NKC), 0))]
            else:
                wl = [(sV, n.mark[("xqev", kk - 2)])]
            tv = op(n, "te", lambda e, kk=kk, bank=bank: e.transpose(
                bank[:].bitcast(BF16)[:, 0:128],
                n.xq_tok[:, kk * 128:(kk + 1) * 128], n.identc[:]),
                waits=wl, inc=sT)
            n.mark[("xqT", t)] = tv
            n.mark[("xqev", kk)] = op(
                n, "ve", lambda e, kk=kk, tc=tc, bank=bank: e.tensor_copy(
                    n.xn_f[:, kk * T + tc * 128: kk * T + tc * 128 + 128],
                    bank[:].bitcast(BF16)[:, 0:128]), waits=[(sT, tv)], inc=sV)
            if kk >= NKC - 2:
                n.mark[("xqev", kk + NKC)] = n.mark[("xqev", kk)]
        v_xqf = n.sV.n
        for ob in range(4):
            bank = P[4 + ob % 2]
            tv = None
            for kq in range(4):
                half, wv = ws.load(
                    n.wf_b[kq * 512:(kq + 1) * 512, ob * 512:(ob + 1) * 512]
                    .rearrange("(k p) c -> p k c", p=128))
                for ki in range(4):
                    kk = kq * 4 + ki
                    wl = []
                    if ki == 0:
                        wl = [wv]
                        if kq == 0:
                            wl.append((sV, v_xqf))
                            wl.append((sV, n.mark.get(("ffnev", t, tc, ob - 2),
                                      n.mark.get(("ffnev", t, tc - 1, 2 + ob), 0))))
                    tv = op(n, "te", lambda e, kk=kk, ki=ki, kq=kq, tc=tc,
                            half=half, bank=bank: e.matmul(
                                bank[:, 0:512],
                                n.xn_f[:, kk * T + tc * 128: kk * T + tc * 128 + 128],
                                n.wbuf[:, half * 2048 + ki * 512:
                                       half * 2048 + (ki + 1) * 512],
                                start=(kk == 0), stop=(kk == NKC - 1)),
                            waits=wl, inc=sT)
                ws.done(tv)
            xs2 = n.x_sb[:, xoff + tc * D + ob * 512: xoff + tc * D + (ob + 1) * 512]
            n.mark[("ffnev", t, tc, ob)] = op(
                n, "ve", lambda e, bank=bank, xs2=xs2: e.scalar_tensor_tensor(
                    xs2, bank[:, 0:512], n.lncol[:, 15:16], xs2,
                    op0=ALU.mult, op1=ALU.add),
                waits=[(sT, tv), (sV, v_osc)], inc=sV, wide=True)
        v_ffn = n.sV.n
        sv = op(n, "sy", lambda e, t=t, tc=tc, xs=xs: e.dma_start(
            out=n.out_d[t * T + tc * 128: t * T + (tc + 1) * 128, :], in_=xs),
            waits=[(sV, v_ffn)], inc=sO, k=16)
        store_vals.append(sv)
    n.mark[("ffnq_free", t)] = n.sV.n
    n.mark[("ffndone", t)] = n.sV.n
    n.store_v[t] = store_vals[-1]


def make_consts():
    ident = np.eye(128, dtype=BF)
    jj = np.arange(128)[:, None]
    tt = np.arange(128)[None, :]
    same = (jj // 64) == (tt // 64)
    strict = same & ((tt % 64) > (jj % 64))
    maskS = np.where(strict, 0.0, NEG).astype(np.float32)
    onehot = np.zeros((128, 16 * 16), dtype=BF)
    for h in range(16):
        onehot[:, h * 16 + h] = 1.0
    return ident, maskS, onehot


def kernel(**inputs):
    x = np.ascontiguousarray(inputs["L0_attn_hidden_states"])
    ident, maskS, onehot = make_consts()
    base = {
        "w_qkv": np.ascontiguousarray(inputs["L0_attn_w_qkv"]),
        "w_z": np.ascontiguousarray(inputs["L0_attn_w_z"]),
        "w_b": np.ascontiguousarray(inputs["L0_attn_w_b"]),
        "w_a": np.ascontiguousarray(inputs["L0_attn_w_a"]),
        "w_out": np.ascontiguousarray(inputs["L0_attn_w_out"]),
        "conv_w": np.ascontiguousarray(inputs["L0_attn_conv_w"]),
        "dt_bias": np.ascontiguousarray(inputs["L0_attn_dt_bias"]).reshape(HV, 1),
        "A_log": np.ascontiguousarray(inputs["L0_attn_A_log"]).reshape(HV, 1),
        "norm_w": np.ascontiguousarray(inputs["L0_attn_norm_weight"]).reshape(DV, 1),
        "gamma": np.ascontiguousarray(inputs["L0_ffn_gamma"]).reshape(1, D),
        "W_ffn": np.ascontiguousarray(inputs["L0_ffn_W"]),
        "ident": ident, "maskS": maskS, "onehot": onehot,
    }
    nc = build_program()
    in_maps = [dict(base, x=np.ascontiguousarray(x[i])) for i in range(8)]
    res = run_bass_kernel_spmd(nc, in_maps, core_ids=list(range(8)))
    return np.stack([res.results[i]["out"] for i in range(8)], axis=0)


# version 05
# speedup vs baseline: 1.2860x; 1.0255x over previous
import contextlib
import numpy as np
import ml_dtypes
import concourse.bass as bass
import concourse.mybir as mybir
from concourse.bass_utils import run_bass_kernel_spmd

F32 = mybir.dt.float32
BF16 = mybir.dt.bfloat16
AF = mybir.ActivationFunctionType
ALU = mybir.AluOpType
AXL = mybir.AxisListType
BF = ml_dtypes.bfloat16

S, D = 4096, 2048
HK, HV, DK, DV = 16, 32, 128, 128
KDIM, VDIM = HK * DK, HV * DV
CONV_DIM = 2 * KDIM + VDIM
KSIZE = 4
T = 256
NT = S // T
NPAIR = T // 128
NKC = D // 128
LN_EPS, NRM_EPS, RMS_EPS, QEPS = 1e-5, 1e-6, 1e-6, 1e-5
MAGIC = float(np.float32(1.5 * 2 ** 23))
NEG = -1e30

DEBUG_SHAPES = {
    "xn": (128, NKC * T), "q": (128, HK * T), "k": (128, HK * T),
    "v": (128, HV * T), "z": (128, HV * T), "g": (32, T), "gc": (32, T),
    "beta": (32, T), "rsq": (16, 2 * T), "A0": (128, 128), "V0": (128, 128),
    "core": (128, HV * T), "x1": (128, (T // 128) * D), "S0": (128, HV * DV),
    "gctok": (128, NPAIR * 32), "u0": (64, 512),
    "lncol": (128, 16), "xt": (128, 2048), "xsb": (128, 4096),
    "idc": (128, 128), "xn_b": (128, NKC * T),
}


class Sem:
    def __init__(self, handle):
        self.h = handle
        self.n = 0

    def inc(self, k=1):
        self.n += k
        return self.n


class NS:
    pass


ORDER_EDGES = False
TRUNC = None
AUTOCHAIN = ("ve", "se", "gp")


def op(n, eng, emit, waits=(), inc=None, k=1, dma=None, wide=False):
    val = inc.inc(k) if inc is not None else None
    wl = tuple((s.h, v) for (s, v) in waits if v > 0)
    ih = inc.h if inc is not None else None
    oe = None
    is_dma = (eng == "sy") if dma is None else dma
    if ORDER_EDGES or (eng in AUTOCHAIN and not is_dma):
        key = (eng, "d" if is_dma else "c")
        owl = []
        if n.olast.get(key) and not (n.owide.get(key) and eng == "ve"
                                     and not ORDER_EDGES):
            owl.append(n.olast[key])
        n.owide[key] = wide
        if is_dma and eng == "gp" and n.olast.get(("gp", "c")):
            owl.append(n.olast[("gp", "c")])
        if ih is not None:
            n.olast[key] = (ih, val)
            oe = (tuple(owl), None, 0)
        else:
            os_ = n.osem[key]
            step = 16 if is_dma else 1
            os_.inc(step)
            n.olast[key] = (os_.h, os_.n)
            oe = (tuple(owl), os_.h, step)

    def f(e, emit=emit, wl=wl, ih=ih, k=k, oe=oe):
        if oe is not None:
            for h, v in oe[0]:
                e.wait_ge(h, v)
        for h, v in wl:
            e.wait_ge(h, v)
        ins = emit(e)
        if ih is not None:
            ins.then_inc(ih, k)
        if oe is not None and oe[1] is not None:
            ins.then_inc(oe[1], oe[2])
    n.ops[eng].append(f)
    return val


def build_program(debug=()):
    nc = bass.Bass("TRN2", target_bir_lowering=False,
                   detect_race_conditions=False, debug=bool(debug))
    dbg = set(debug)
    n = NS()
    n.nc = nc
    n.dbg = dbg

    def dram(name, shape, dt, kind):
        return nc.dram_tensor(name, list(shape), dt, kind=kind).ap()

    n.x_d = dram("x", (S, D), F32, "ExternalInput")
    n.wqkv_d = dram("w_qkv", (D, CONV_DIM), F32, "ExternalInput")
    n.wz_d = dram("w_z", (D, VDIM), F32, "ExternalInput")
    n.wb_d = dram("w_b", (D, HV), F32, "ExternalInput")
    n.wa_d = dram("w_a", (D, HV), F32, "ExternalInput")
    n.wo_d = dram("w_out", (VDIM, D), F32, "ExternalInput")
    n.convw_d = dram("conv_w", (CONV_DIM, KSIZE), F32, "ExternalInput")
    n.dtb_d = dram("dt_bias", (HV, 1), F32, "ExternalInput")
    n.alog_d = dram("A_log", (HV, 1), F32, "ExternalInput")
    n.nw_d = dram("norm_w", (DV, 1), F32, "ExternalInput")
    n.gam_d = dram("gamma", (1, D), F32, "ExternalInput")
    n.wffn_d = dram("W_ffn", (D, D), F32, "ExternalInput")
    n.ident_d = dram("ident", (128, 128), BF16, "ExternalInput")
    n.maskS_d = dram("maskS", (128, 128), F32, "ExternalInput")
    n.oneh_d = dram("onehot", (128, 16 * 16), BF16, "ExternalInput")
    n.out_d = dram("out", (S, D), F32, "ExternalOutput")
    n.wqkv_b = dram("wqkv_b", (D, CONV_DIM), BF16, "Internal")
    n.wz_b = dram("wz_b", (D, VDIM), BF16, "Internal")
    n.wba_b = dram("wba_b", (D, 2 * HV), BF16, "Internal")
    n.wo_b = dram("wo_b", (VDIM, D), BF16, "Internal")
    n.wf_b = dram("wf_b", (D, D), BF16, "Internal")
    n.wqkv_s = dram("wqkv_s", (128, (CONV_DIM // 128) * (D // 128) * 128),
                    BF16, "Internal")
    n.wz_s = dram("wz_s", (128, (VDIM // 128) * (D // 128) * 128),
                  BF16, "Internal")
    n.dbg_d = {nm: dram("dbg_" + nm, DEBUG_SHAPES[nm],
                        BF16 if nm.endswith("_b") else F32, "ExternalOutput")
               for nm in dbg}

    est = contextlib.ExitStack()
    sbt = lambda nm, shp, dt: est.enter_context(nc.sbuf_tensor(nm, list(shp), dt))
    pst = lambda nm: est.enter_context(nc.psum_tensor(nm, [128, 512], F32))
    mks = lambda nm: Sem(est.enter_context(nc.semaphore(nm)))
    n.osem = {}
    n.olast = {}
    n.owide = {}
    okeys = [("ve", "c"), ("se", "c"), ("gp", "c")]
    if ORDER_EDGES:
        okeys += [("sy", "d"), ("gp", "d"), ("te", "c")]
    for ek, ck in okeys:
        n.osem[(ek, ck)] = mks(f"ord_{ek}_{ck}")

    with est:
        n.x_sb = sbt("x_sb", (128, 2 * (T // 128) * D), F32)
        n.xn_f = sbt("xn_f", (128, NKC * T), BF16)
        n.q_sb = sbt("q_sb", (128, HK * T), BF16)
        n.k_sb = sbt("k_sb", (128, HK * T), BF16)
        n.vc_sb = sbt("vc_sb", (128, HV * T), BF16)
        n.z_sb = sbt("z_sb", (128, HV * T), BF16)
        n.A_sb = sbt("A_sb", (128, HV * 128), BF16)
        n.V_sb = sbt("V_sb", (128, HV * 128), BF16)
        n.vT_sb = sbt("vT_sb", (128, HV * 128), BF16)
        n.kT_sb = sbt("kT_sb", (128, HK * 128), BF16)
        n.S_sb = sbt("S_sb", (128, HV * DV), F32)
        n.Sb_sb = sbt("Sb_sb", (128, HV * DV), BF16)
        n.pw_sb = sbt("pw_sb", (128, 11 * 512), BF16)
        n.ptil_sb = sbt("ptil_sb", (128, 512), BF16)
        n.zz_sb = sbt("zz_sb", (128, 512), BF16)
        n.u_sb = sbt("u_sb", (128, 512), BF16)
        n.rhs_sb = sbt("rhs_sb", (128, 512), BF16)
        n.kh_sb = sbt("kh_sb", (128, 512), BF16)
        n.es_sb = sbt("es_sb", (128, 512), BF16)
        n.ei_sb = sbt("ei_sb", (128, 512), BF16)
        n.o1_sb = sbt("o1_sb", (128, 512), BF16)
        n.on_sb = sbt("on_sb", (128, 512), BF16)
        n.t32a = sbt("t32a", (128, 512), F32)
        n.t32b = sbt("t32b", (128, 512), F32)
        n.arg_sb = sbt("arg_sb", (128, 512), F32)
        n.wbuf = sbt("wbuf", (128, 2 * 2048), BF16)
        n.cbuf = sbt("cbuf", (128, T + 4), BF16)
        n.ta_sb = sbt("ta_sb", (128, T), BF16)
        n.tb_sb = sbt("tb_sb", (128, T), BF16)
        n.tc_sb = sbt("tc_sb", (128, T), BF16)
        n.td_sb = sbt("td_sb", (128, T), BF16)
        n.cv_sb = sbt("cv_sb", (128, T), BF16)
        n.qsq_sb = sbt("qsq_sb", (128, T), BF16)
        n.tails = sbt("tails", (128, 64 * 4), BF16)
        n.convw = sbt("convw", (128, 64 * KSIZE), F32)
        n.g_f = sbt("g_f", (64, T), F32)
        n.gc_f = sbt("gc_f", (64, T), F32)
        n.sm1 = None
        n.sm2 = None
        n.sm3 = None
        n.beta_f = sbt("beta_f", (64, T), F32)
        n.sm1 = n.t32a[0:64, 0:256]
        n.sm2 = n.t32a[0:64, 256:512]
        n.sm3 = n.t32b[0:64, 0:256]
        n.ba_f = n.t32b[0:64, 256:512]
        n.onessm = sbt("onessm", (64, 128), F32)
        n.gc_tok = sbt("gc_tok", (128, NPAIR * 32), F32)
        n.beta_tok = sbt("beta_tok", (128, NPAIR * 32), F32)
        n.gam_tok = sbt("gam_tok", (128, NPAIR * 32), F32)
        n.dec_tok = sbt("dec_tok", (128, NPAIR * 32), F32)
        n.glast_bc = sbt("glast_bc", (128, NPAIR * 2 * 32), F32)
        n.bgr_tok = sbt("bgr_tok", (128, NPAIR * 32), F32)
        n.grq_tok = sbt("grq_tok", (128, NPAIR * 32), F32)
        n.rq_tok = sbt("rq_tok", (128, NPAIR * 16), F32)
        n.rk_tok = sbt("rk_tok", (128, NPAIR * 16), F32)
        n.brk_tok = sbt("brk_tok", (128, NPAIR * 32), F32)
        n.rsq_f = sbt("rsq_f", (16, 2 * T), F32)
        n.grow = sbt("grow", (1, 512), F32)
        n.lncol = sbt("lncol", (128, 16), F32)
        n.gamma_bc = sbt("gamma_bc", (128, D), BF16)
        n.ffnq = sbt("ffnq", (128, D), F32)
        n.xt_tok = None
        n.xq_tok = sbt("xq_tok", (128, D), BF16)
        n.identc = sbt("identc", (128, 128), BF16)
        n.identf = sbt("identf", (64, 64), F32)
        n.maskSc = sbt("maskSc", (128, 128), F32)
        n.onehc = sbt("onehc", (128, 16 * 16), BF16)
        n.ones1 = sbt("ones1", (1, 128), F32)
        n.onescol = sbt("onescol", (128, 1), F32)
        n.dtb = sbt("dtb", (64, 1), F32)
        n.nA = sbt("nA", (64, 1), F32)
        n.nw = sbt("nw", (128, 1), F32)
        n.mw_bc = sbt("mw_bc", (128, 1), F32)
        n.wsc = sbt("wsc", (128, 1), F32)
        n.xt_tok = n.xq_tok
        n.setup1 = n.ffnq
        n.setup2 = n.xq_tok
        n.wabs = sbt("wabs", (128, 4), F32)

        n.P = [pst("pb%d" % i) for i in range(8)]

        n.sW0, n.sW1, n.sG, n.sO = mks("sW0"), mks("sW1"), mks("sG"), mks("sO")
        n.sXa, n.sXb, n.sC = mks("sXa"), mks("sXb"), mks("sC")
        n.sT, n.sV, n.sE, n.sP = mks("sT"), mks("sV"), mks("sE"), mks("sP")
        n.sQ = mks("sQ")
        n.sD = mks("sD")
        n.ops = {e: [] for e in ("te", "ve", "se", "gp", "sy")}

        schedule(n)

        blk = est.enter_context(nc.Block())

        @blk.sync
        def _(eng):
            for f in n.ops["sy"]:
                f(eng)

        @blk.tensor
        def _(eng):
            for f in n.ops["te"]:
                f(eng)

        @blk.vector
        def _(eng):
            for f in n.ops["ve"]:
                f(eng)

        @blk.scalar
        def _(eng):
            for f in n.ops["se"]:
                f(eng)

        @blk.gpsimd
        def _(eng):
            for f in n.ops["gp"]:
                f(eng)

    return nc


def schedule(n):
    sT, sV, sE, sP = n.sT, n.sV, n.sE, n.sP
    sG, sO, sC = n.sG, n.sO, n.sC

    setup(n)
    for t in range(NT):
        tile_body(n, t)
    endw = [(sO, sO.n)]
    if n.dbg:
        endw.append((n.sD, n.sD.n))
    op(n, "gp", lambda e: e.engine_nop(), waits=endw)


def setup(n):
    sT, sV, sE, sP = n.sT, n.sV, n.sE, n.sP
    sG, sO, sC = n.sG, n.sO, n.sC
    sXa = n.sXa
    P = n.P

    def syd(out, in_, waits=()):
        return op(n, "sy", lambda e, o=out, i=in_: e.dma_start(out=o, in_=i),
                  waits=waits, inc=sC, k=16)

    def gpd(out, in_, waits=()):
        return op(n, "gp", lambda e, o=out, i=in_: e.dma_start(out=o, in_=i),
                  waits=waits, inc=sP, k=16, dma=True)

    syd(n.identc[:], n.ident_d)
    syd(n.maskSc[:], n.maskS_d)
    syd(n.onehc[:], n.oneh_d)
    syd(n.dtb[32:64, :], n.dtb_d)
    syd(n.nA[32:64, :], n.alog_d)
    syd(n.nw[:], n.nw_d)
    syd(n.setup1[0:1, 0:D], n.gam_d)
    v_cw = syd(n.convw[:], n.convw_d.rearrange("(c p) s -> p c s", p=128))
    n.v_convw = v_cw
    v_consts = sC.n
    v_gam = v_consts

    n.xload_v = {}
    v0 = op(n, "sy", lambda e: e.dma_start(
        out=n.x_sb[:, 0:(T // 128) * D],
        in_=n.x_d[0:T, :].rearrange("(a p) d -> p a d", p=128)),
        inc=n.sXa, k=16)
    n.xload_v[0] = (n.sXa, v0)

    op(n, "ve", lambda e: e.memset(n.ones1[:], 1.0), waits=[(sC, v_consts)])
    op(n, "ve", lambda e: e.memset(n.onescol[:], 1.0))
    op(n, "ve", lambda e: e.memset(n.onessm[:], 1.0))
    op(n, "ve", lambda e: e.memset(n.S_sb[:], 0.0))
    op(n, "ve", lambda e: e.memset(n.Sb_sb[:], 0.0))
    op(n, "ve", lambda e: e.memset(n.tails[:], 0.0))
    v_idf = op(n, "gp", lambda e: e.dma_start(out=n.identf[:],
                                              in_=n.ident_d[0:64, 0:64]),
               inc=sP, k=16, dma=True)
    n.v_idf = v_idf

    v_e = op(n, "se", lambda e: e.activation(n.nA[32:64, :], n.nA[32:64, :], AF.Exp),
             waits=[(sC, v_consts)], inc=sE)
    v_na = op(n, "ve", lambda e: e.tensor_scalar_mul(n.nA[32:64, :], n.nA[32:64, :], -1.0),
              waits=[(sE, v_e)], inc=sV)
    n.v_smallconst = v_na

    for i in range(4):
        tv = op(n, "te", lambda e, i=i: e.matmul(
            P[0][:, 0:512], n.ones1[0:1, :], n.setup1[0:1, i * 512:(i + 1) * 512],
            start=True, stop=True),
            waits=([(sC, v_gam), (sV, v_na)] if i == 0 else
                   [(sV, n.sV.n)]), inc=sT)
        op(n, "ve", lambda e, i=i: e.tensor_copy(
            n.gamma_bc[:, i * 512:(i + 1) * 512], P[0][:, 0:512]),
            waits=[(sT, tv)], inc=sV)
    n.v_gammm = sT.n

    for i in range(8):
        gpd(n.wqkv_b[i * 256:(i + 1) * 256, :], n.wqkv_d[i * 256:(i + 1) * 256, :])
    for i in range(4):
        gpd(n.wz_b[i * 512:(i + 1) * 512, :], n.wz_d[i * 512:(i + 1) * 512, :])
    gpd(n.wba_b[:, 0:HV], n.wb_d)
    gpd(n.wba_b[:, HV:2 * HV], n.wa_d)
    for i in range(16):
        gpd(n.wo_b[i * 256:(i + 1) * 256, :], n.wo_d[i * 256:(i + 1) * 256, :])
    n.v_wcast = sP.n

    v_z = op(n, "ve", lambda e: e.memset(n.wabs[:, 0:1], 0.0), inc=sV)
    prev_use = 0
    for i in range(16):
        lv = syd(n.setup1[:], n.wffn_d[i * 128:(i + 1) * 128, :],
                 waits=([(sV, prev_use)] if i else [(sT, n.v_gammm)]))
        op(n, "ve", lambda e: e.reduce_sum(n.wabs[:, 1:2], n.setup1[:], axis=AXL.X,
                                           apply_absolute_value=True),
           waits=[(sC, lv)])
        prev_use = op(n, "ve", lambda e: e.tensor_add(
            n.wabs[:, 0:1], n.wabs[:, 0:1], n.wabs[:, 1:2]), inc=sV, k=2)
    tv = op(n, "te", lambda e: e.matmul(P[1][0:1, 0:1], n.onescol[:],
                                        n.wabs[:, 0:1], start=True, stop=True),
            waits=[(sV, prev_use)], inc=sT)
    op(n, "ve", lambda e: e.tensor_scalar(n.wabs[0:1, 2:3], P[1][0:1, 0:1],
                                          1.0 / (D * D), QEPS,
                                          op0=ALU.mult, op1=ALU.add),
       waits=[(sT, tv)])
    v_r = op(n, "ve", lambda e: e.reciprocal(n.wabs[0:1, 3:4], n.wabs[0:1, 2:3]),
             inc=sV, k=2)
    tv = op(n, "te", lambda e: e.matmul(P[1][:, 4:5], n.ones1[0:1, :],
                                        n.wabs[0:1, 3:4], start=True, stop=True),
            waits=[(sV, v_r)])
    tv = op(n, "te", lambda e: e.matmul(P[1][:, 5:6], n.ones1[0:1, :],
                                        n.wabs[0:1, 2:3], start=True, stop=True),
            inc=sT)
    op(n, "ve", lambda e: e.tensor_copy(n.wsc[:], P[1][:, 4:5]), waits=[(sT, tv)])
    v_wsc = op(n, "ve", lambda e: e.tensor_copy(n.mw_bc[:], P[1][:, 5:6]), inc=sV)

    prev_gp = 0
    prev_ve = v_wsc
    for i in range(16):
        lv = syd(n.setup1[:], n.wffn_d[i * 128:(i + 1) * 128, :],
                 waits=[(sV, prev_ve)])
        op(n, "ve", lambda e: e.tensor_scalar_mul(n.ffnq[:], n.setup1[:],
                                                  n.wsc[:, 0:1]),
           waits=[(sC, lv), (sV, v_wsc)])
        op(n, "ve", lambda e: e.tensor_scalar(n.ffnq[:], n.ffnq[:], MAGIC, MAGIC,
                                              op0=ALU.add, op1=ALU.subtract))
        qv = op(n, "ve", lambda e: e.tensor_scalar(n.setup2[:], n.ffnq[:], 1.0, -1.0,
                                                   op0=ALU.min, op1=ALU.max),
                inc=sV, k=3)
        prev_ve = qv
        for k in range(16):
            tv = op(n, "te", lambda e, k=k: e.transpose(
                P[2][:].bitcast(BF16)[:, (k % 4) * 128:(k % 4) * 128 + 128],
                n.setup2[:, k * 128:(k + 1) * 128], n.identc[:]),
                waits=([(sV, qv), (sP, prev_gp)] if k == 0 else
                       ([(sV, sV.n)] if k % 4 == 0 else [])), inc=sT)
            if k % 4 == 3:
                prev_ve = op(n, "ve", lambda e, k=k: e.tensor_copy(
                    n.wbuf[:, (k - 3) * 128:(k + 1) * 128],
                    P[2][:].bitcast(BF16)[:, 0:512]),
                    waits=[(sT, tv)], inc=sV)
        prev_gp = gpd(n.wf_b[:, i * 128:(i + 1) * 128].rearrange(
            "(a p) c -> p a c", p=128), n.wbuf[:, 0:2048],
            waits=[(sV, prev_ve)])
    n.v_wffn = sP.n
    n.v_setupgp = sP.n

    first = True
    for occ in range(CONV_DIM // 128):
        wl = [(sP, n.v_wcast)] if first else []
        first = False
        syd(n.wqkv_s[:, occ * 2048:(occ + 1) * 2048]
            .rearrange("p (k c) -> p k c", c=128),
            n.wqkv_b[:, occ * 128:(occ + 1) * 128]
            .rearrange("(k p) c -> p k c", p=128), waits=wl)
    for occ in range(VDIM // 128):
        syd(n.wz_s[:, occ * 2048:(occ + 1) * 2048]
            .rearrange("p (k c) -> p k c", c=128),
            n.wz_b[:, occ * 128:(occ + 1) * 128]
            .rearrange("(k p) c -> p k c", p=128))
    n.v_swiz = sC.n

    n.v_setupT = sT.n
    n.setup_sC = sC.n
    n.mark = {}
    n.mark[("ffnq_free", -1)] = sV.n


class WStream:
    def __init__(self, n):
        self.n = n
        self.idx = 0
        self.consumed = []

    def load(self, dram_ap, waits=(), cols=2048):
        n = self.n
        half = self.idx % 2
        sem = n.sW0 if half == 0 else n.sW1
        wl = list(waits)
        if self.idx >= 2:
            wl.append((n.sT, self.consumed[self.idx - 2]))
        v = op(n, "sy", lambda e, o=n.wbuf[:, half * 2048:half * 2048 + cols],
               i=dram_ap: e.dma_start(out=o, in_=i), waits=wl, inc=sem, k=16)
        self.idx += 1
        return half, (sem, v)

    def done(self, sT_val):
        self.consumed.append(sT_val)


def tile_body(n, t):
    sT, sV, sE, sP = n.sT, n.sV, n.sE, n.sP
    sG, sO, sC = n.sG, n.sO, n.sC
    P = n.P
    par = t % 2
    xoff = par * (T // 128) * D
    if not hasattr(n, "ws"):
        n.ws = WStream(n)
        n.store_v = {}
    ws = n.ws
    dbg = n.dbg

    if t + 1 < NT:
        npar = (t + 1) % 2
        sem = n.sXa if npar == 0 else n.sXb
        wl = []
        if t >= 1:
            wl.append((sO, n.store_v[t - 1]))
        vx = op(
            n, "sy", lambda e, t=t, npar=npar: e.dma_start(
                out=n.x_sb[:, npar * (T // 128) * D:(npar + 1) * (T // 128) * D],
                in_=n.x_d[(t + 1) * T:(t + 2) * T, :].rearrange(
                    "(a p) d -> p a d", p=128)),
            waits=wl, inc=sem, k=16)
        n.xload_v[t + 1] = (sem, vx)

    xn_guard = n.mark.get(("projdone", t - 1), 0)
    for tc in range(T // 128):
        xs = n.x_sb[:, xoff + tc * D: xoff + (tc + 1) * D]
        wl = [n.xload_v[t]] + ([(sV, n.mark[("ffnq_free", t - 1)])]
                                     if ("ffnq_free", t - 1) in n.mark else [])
        op(n, "ve", lambda e, xs=xs: e.reduce_sum(n.lncol[:, 0:1], xs, axis=AXL.X),
           waits=wl)
        v_red = op(n, "ve", lambda e, xs=xs: e.tensor_scalar_mul(
            n.lncol[:, 2:3], n.lncol[:, 0:1], 1.0 / D), inc=sV)
        e_sq = op(n, "se", lambda e, xs=xs: e.activation(
            n.ffnq[:], xs, AF.Square, accum_out=n.lncol[:, 1:2]),
            waits=[n.xload_v[t]] + ([(sV, n.mark[("ffnq_free", t - 1)])]
                                          if ("ffnq_free", t - 1) in n.mark else []),
            inc=sE)
        op(n, "ve", lambda e: e.tensor_scalar_mul(n.lncol[:, 3:4], n.lncol[:, 1:2],
                                                  1.0 / D), waits=[(sE, e_sq)])
        op(n, "ve", lambda e: e.tensor_mul(n.lncol[:, 4:5], n.lncol[:, 2:3],
                                           n.lncol[:, 2:3]))
        op(n, "ve", lambda e: e.tensor_sub(n.lncol[:, 5:6], n.lncol[:, 3:4],
                                           n.lncol[:, 4:5]))
        op(n, "ve", lambda e: e.tensor_scalar_add(n.lncol[:, 6:7], n.lncol[:, 5:6],
                                                  LN_EPS))
        v_rc = op(n, "ve", lambda e: e.reciprocal(n.lncol[:, 7:8], n.lncol[:, 6:7]),
                  inc=sV)
        e_st = op(n, "se", lambda e: e.activation(n.lncol[:, 8:9], n.lncol[:, 7:8],
                                                  AF.Sqrt), waits=[(sV, v_rc)],
                  inc=sE)
        v_xn = op(n, "ve", lambda e, xs=xs: e.tensor_scalar(
            n.xt_tok[:], xs, n.lncol[:, 2:3], n.lncol[:, 8:9],
            op0=ALU.subtract, op1=ALU.mult),
            waits=[(sE, e_st),
                   (sT, n.mark.get(("xqT", t - 1), n.v_setupT if t == 0 else 0))],
            inc=sV)
        for kk in range(NKC):
            bank = P[kk % 2]
            if kk < 2:
                wl = [(sV, v_xn), (sV, n.mark.get(("xnev", kk - 2), 0)),
                      (sV, n.mark.get(("ffndone", t - 1), 0))]
            else:
                wl = [(sV, n.mark[("xnev", kk - 2)])]
            tv = op(n, "te", lambda e, kk=kk, bank=bank: e.transpose(
                bank[:].bitcast(BF16)[:, 0:128],
                n.xt_tok[:, kk * 128:(kk + 1) * 128], n.identc[:]),
                waits=wl, inc=sT)
            n.mark[("xnev", kk)] = op(
                n, "ve", lambda e, kk=kk, tc=tc, bank=bank: e.tensor_copy(
                    n.xn_f[:, kk * T + tc * 128: kk * T + tc * 128 + 128],
                    bank[:].bitcast(BF16)[:, 0:128]), waits=[(sT, tv)], inc=sV)
    v_xnf = sV.n
    if "xn" in dbg and t == 0:
        dbg_dump(n, "xn", n.xn_f[:], [(sV, v_xnf)])
    if "xn_b" in dbg and t == 0:
        dbg_dump(n, "xn_b", n.xn_f[:], [(sV, v_xnf)])
    if t == 0:
        if "lncol" in dbg:
            dbg_dump(n, "lncol", n.lncol[:], [(sV, v_xnf)])
        if "xt" in dbg:
            dbg_dump(n, "xt", n.xt_tok[:], [(sV, v_xnf)])
        if "xsb" in dbg:
            dbg_dump(n, "xsb", n.x_sb[:, 0:2 * D], [(sV, v_xnf)])
        if "idc" in dbg:
            dbg_dump(n, "idc", n.identc[:], [(sV, v_xnf)])
    if TRUNC == "ln1":
        return

    projection_phase(n, t, v_xnf, xn_guard)
    if TRUNC == "proj":
        return

    smalls_phase(n, t)
    if TRUNC == "smalls":
        return

    for p in range(NPAIR):
        delta_prep(n, t, p)
        for ci in range(2):
            delta_state(n, t, p, ci)
    n.mark[("gathers", t)] = n.sG.n
    if TRUNC == "delta":
        return

    wout_phase(n, t)
    if TRUNC == "wout":
        return

    ffn_phase(n, t)


def dbg_dump(n, name, src_ap, waits):
    op(n, "gp", lambda e, o=n.dbg_d[name], i=src_ap: e.dma_start(out=o, in_=i),
       waits=waits, inc=n.sD, k=16, dma=True)


def projection_phase(n, t, v_xnf, xn_guard):
    sT, sV, sE, sP = n.sT, n.sV, n.sE, n.sP
    sO = n.sO
    P = n.P
    ws = n.ws
    dbg = n.dbg

    vc_guard = n.mark.get(("statevec", t - 1), 0)

    n.qk_ready = {}
    sumsq_mm = 0
    for oc in range(64):
        half, wv = ws.load(
            n.wqkv_s[:, oc * 2048:(oc + 1) * 2048],
            waits=[(sP, n.v_setupgp), (n.sC, n.v_swiz)]
            if (t == 0 and oc == 0) else ())
        bank = P[oc % 4]
        tv = None
        for kk in range(NKC):
            wl = []
            if kk == 0:
                wl = [wv, (sV, v_xnf)]
                if oc < 4:
                    wl.append((sV, n.mark.get(("ffndone", t - 1), 0)))
                else:
                    wl.append((sE, n.mark[("convev", t, oc - 4)]))
            tv = op(n, "te", lambda e, oc=oc, kk=kk, half=half, bank=bank: e.matmul(
                bank[:, 0:256], n.wbuf[:, half * 2048 + kk * 128: half * 2048 + (kk + 1) * 128],
                n.xn_f[:, kk * T:(kk + 1) * T],
                start=(kk == 0), stop=(kk == NKC - 1)), waits=wl, inc=sT)
        ws.done(tv)
        conv_emit(n, t, oc, bank, tv, vc_guard)
    for oc in range(32):
        half, wv = ws.load(n.wz_s[:, oc * 2048:(oc + 1) * 2048])
        bank = P[4 + oc % 2]
        tv = None
        for kk in range(NKC):
            wl = []
            if kk == 0:
                if oc < 2:
                    wl = [wv, (sV, n.mark.get(("ffndone", t - 1), 0)),
                          (sV, vc_guard)]
                else:
                    wl = [wv, (sE, n.mark[("zev", t, oc - 2)])]
            tv = op(n, "te", lambda e, oc=oc, kk=kk, half=half, bank=bank: e.matmul(
                bank[:, 0:256], n.wbuf[:, half * 2048 + kk * 128: half * 2048 + (kk + 1) * 128],
                n.xn_f[:, kk * T:(kk + 1) * T],
                start=(kk == 0), stop=(kk == NKC - 1)), waits=wl, inc=sT)
        ws.done(tv)
        n.mark[("zev", t, oc)] = op(
            n, "se", lambda e, oc=oc, bank=bank: e.activation(
                n.z_sb[:, oc * T:(oc + 1) * T], bank[:, 0:256], AF.Silu),
            waits=[(sT, tv)], inc=sE)
    if "z" in dbg and t == 0:
        dbg_dump(n, "z", n.z_sb[:], [(sE, n.sE.n)])

    half, wv = ws.load(n.wba_b.rearrange("(k p) c -> p k c", p=128), cols=1024)
    bank = P[6]
    tv = None
    for kk in range(NKC):
        wl = ([wv, (sV, n.mark.get(("smalls", t - 1, NPAIR - 1), 0)),
               (sE, n.mark.get(("sigb", t - 1), 0))] if kk == 0 else [])
        tv = op(n, "te", lambda e, kk=kk, half=half: e.matmul(
            bank[0:64, 0:256], n.wbuf[:, half * 2048 + kk * 64: half * 2048 + (kk + 1) * 64],
            n.xn_f[:, kk * T:(kk + 1) * T],
            start=(kk == 0), stop=(kk == NKC - 1)), waits=wl, inc=sT)
    ws.done(tv)
    n.v_ba = op(n, "ve", lambda e, bank=bank: e.tensor_copy(
        n.ba_f[:], bank[0:64, 0:256]), waits=[(sT, tv)], inc=sV)
    n.mark[("projdone", t)] = sT.n

    l2_finish(n, t)


def conv_emit(n, t, oc, bank, tv_mm, vc_guard):
    sT, sV, sE, sP = n.sT, n.sV, n.sE, n.sP
    sQ = n.sQ
    P = n.P
    dbg = n.dbg
    w = n.convw
    gv = op(n, "gp", lambda e, oc=oc: e.tensor_copy(
        n.cbuf[:, 1:4], n.tails[:, oc * 4 + 1: oc * 4 + 4]),
        waits=[(sV, n.mark.get(("convdone", t, oc - 1),
                               n.mark.get(("convdone", t - 1, 63),
                                          n.v_smallconst)))], inc=sQ)
    ev = op(n, "se", lambda e, bank=bank: e.activation(
        n.cbuf[:, 4:4 + T], bank[:, 0:256], AF.Copy),
        waits=[(sT, tv_mm),
               (sQ, n.mark.get(("convgp", t, oc - 1),
                               n.mark.get(("convgp", t - 1, 63), 0)))],
        inc=sE)
    n.mark[("convev", t, oc)] = ev
    gv2 = op(n, "gp", lambda e, oc=oc: e.tensor_copy(
        n.tails[:, oc * 4 + 1: oc * 4 + 4], n.cbuf[:, 1 + T: 4 + T]),
        waits=[(sE, ev)], inc=sQ)
    n.mark[("convgp", t, oc)] = gv2
    op(n, "ve", lambda e, oc=oc: e.tensor_scalar_mul(
        n.cv_sb[:], n.cbuf[:, 4:4 + T], w[:, oc * 4 + 3: oc * 4 + 4]),
        waits=[(sQ, gv), (sE, ev),
               (n.sE, n.mark.get(("silu", t, oc - 1),
                                 n.mark.get(("silu", t - 1, 63), 0)))],
        wide=True)
    v3 = None
    for s in (2, 1, 0):
        v3 = op(n, "ve", lambda e, oc=oc, s=s: e.scalar_tensor_tensor(
            n.cv_sb[:], n.cbuf[:, 1 + s: 1 + s + T],
            w[:, oc * 4 + s: oc * 4 + s + 1], n.cv_sb[:],
            op0=ALU.mult, op1=ALU.add), inc=(sV if s == 0 else None),
            wide=True)
    n.mark[("convdone", t, oc)] = v3
    if oc < 16:
        dst = n.q_sb[:, oc * T:(oc + 1) * T]
    elif oc < 32:
        dst = n.k_sb[:, (oc - 16) * T:(oc - 15) * T]
    else:
        dst = n.vc_sb[:, (oc - 32) * T:(oc - 31) * T]
    ev2 = op(n, "se", lambda e, dst=dst: e.activation(dst, n.cv_sb[:], AF.Silu),
             waits=[(sV, v3), (sT, n.mark.get(("woutmm", t - 1), 0))], inc=sE)
    n.mark[("silu", t, oc)] = ev2
    if oc < 32:
        v4 = op(n, "ve", lambda e, dst=dst: e.tensor_mul(n.qsq_sb[:], dst, dst),
                waits=[(sE, ev2)], inc=sV)
        h = oc % 16
        is_q = oc < 16
        colr = slice(0, 256) if is_q else slice(256, 512)
        wl5 = [(sV, v4)]
        if oc == 0:
            wl5.append((sE, n.mark.get(("rsqdone", t - 1), 0)))
        tv = op(n, "te", lambda e, h=h, colr=colr: e.matmul(
            n.P[7][0:16, colr], n.onehc[:, h * 16:(h + 1) * 16], n.qsq_sb[:],
            start=(oc == 0), stop=(oc == 31)), waits=wl5, inc=sT)
        n.mark[("l2mm", t, oc)] = tv


def l2_finish(n, t):
    sT, sV, sE = n.sT, n.sV, n.sE
    P = n.P
    dbg = n.dbg
    tv = n.mark[("l2mm", t, 31)]
    op(n, "ve", lambda e: e.tensor_scalar(n.rsq_f[:, 0:256], P[7][0:16, 0:256],
                                          float(DK), float(DK) * NRM_EPS,
                                          op0=ALU.mult, op1=ALU.add),
       waits=[(sT, tv), (sT, n.mark.get(("rsqT", t - 1), 0))])
    op(n, "ve", lambda e: e.tensor_scalar(n.rsq_f[:, 256:512], P[7][0:16, 256:512],
                                          1.0, NRM_EPS, op0=ALU.mult, op1=ALU.add))
    v1 = op(n, "ve", lambda e: e.reciprocal(n.rsq_f[:], n.rsq_f[:]), inc=sV, k=3)
    e1 = op(n, "se", lambda e: e.activation(n.rsq_f[:], n.rsq_f[:], AF.Sqrt),
            waits=[(sV, v1)], inc=sE)
    n.v_rsq = e1
    n.mark[("rsqdone", t)] = e1
    if "rsq" in dbg and t == 0:
        dbg_dump(n, "rsq", n.rsq_f[:], [(sE, e1)])


def smalls_phase(n, t):
    sT, sV, sE = n.sT, n.sV, n.sE
    P = n.P
    dbg = n.dbg
    v_ba = n.v_ba

    e_b = op(n, "se", lambda e: e.activation(n.beta_f[0:32, :], n.ba_f[0:32, :],
                                             AF.Sigmoid), waits=[(sV, v_ba)], inc=sE)
    n.mark[("sigb", t)] = e_b
    op(n, "ve", lambda e: e.tensor_scalar_add(n.sm1[32:64, :], n.ba_f[32:64, :],
                                              n.dtb[32:64, 0:1]), waits=[(sV, v_ba)])
    v_u = op(n, "ve", lambda e: e.tensor_scalar(n.sm2[32:64, :], n.sm1[32:64, :],
                                                0.0, 0.0, op0=ALU.max, op1=ALU.bypass),
             inc=sV, k=2)
    e_a = op(n, "se", lambda e: e.activation(n.sm3[32:64, :], n.sm1[32:64, :], AF.Abs),
             waits=[(sV, v_u)], inc=sE)
    e_e = op(n, "se", lambda e: e.activation(n.sm3[32:64, :], n.sm3[32:64, :],
                                             AF.Exp, scale=-1.0), inc=sE)
    e_l = op(n, "se", lambda e: e.activation(n.sm3[32:64, :], n.sm3[32:64, :],
                                             AF.Ln, bias=1.0), inc=sE)
    op(n, "ve", lambda e: e.tensor_add(n.sm2[32:64, :], n.sm2[32:64, :],
                                       n.sm3[32:64, :]), waits=[(sE, e_l)])
    v_g = op(n, "ve", lambda e: e.tensor_scalar_mul(n.g_f[32:64, :], n.sm2[32:64, :],
                                                    n.nA[32:64, 0:1]), inc=sV, k=2)
    v_gc = None
    for c in range(T // 64):
        wlc = [(sV, v_g)] if c == 0 else []
        if c == 0:
            wlc.append((n.sG, n.mark.get(("gathers", t - 1), 0)))
        v_gc = op(n, "ve", lambda e, c=c: e.tensor_tensor_scan(
            n.gc_f[32:64, c * 64:(c + 1) * 64], n.g_f[32:64, c * 64:(c + 1) * 64],
            n.g_f[32:64, c * 64:(c + 1) * 64], 0.0, op0=ALU.add, op1=ALU.bypass),
            waits=wlc, inc=sV)
    n.v_gc = v_gc
    if "gc" in dbg and t == 0:
        dbg_dump(n, "gc", n.gc_f[32:64, :], [(sV, v_gc)])
    if "beta" in dbg and t == 0:
        dbg_dump(n, "beta", n.beta_f[0:32, :], [(sE, e_b)])
    if "g" in dbg and t == 0:
        dbg_dump(n, "g", n.g_f[32:64, :], [(sV, v_g)])

    sv_prev = n.mark.get(("statevec", t - 1), 0)
    prev_t4 = 0
    prev_vgl = sv_prev
    prev_vgct = prev_egam = prev_vbt = prev_vdec = prev_vrk = 0
    for p in range(NPAIR):
        po = p * 32
        t1 = op(n, "te", lambda e, p=p: e.transpose(
            P[5][0:128, 0:32], n.gc_f[32:64, p * 128:(p + 1) * 128],
            n.identf[32:64, 32:64]),
            waits=[(sV, v_gc), (sV, sv_prev), (sV, prev_vgct),
                   (sE, prev_egam if p else n.mark.get(("egam", t - 1), 0))],
            inc=sT)
        op(n, "ve", lambda e, po=po: e.tensor_copy(n.gc_tok[:, po:po + 32],
                                                   P[5][0:128, 0:32]),
           waits=[(sT, t1)])
        v_gct = op(n, "ve", lambda e, po=po: e.tensor_copy(
            n.t32a[:, 0:32], P[5][0:128, 0:32]), inc=sV, k=2)
        e_gam = op(n, "se", lambda e, po=po: e.activation(
            n.gam_tok[:, po:po + 32], P[5][0:128, 0:32], AF.Exp),
            waits=[(sV, v_gct)], inc=sE)
        n.mark[("egam", t)] = e_gam
        prev_vgct, prev_egam = v_gct, e_gam
        t2 = op(n, "te", lambda e, p=p: e.transpose(
            P[5][0:128, 32:64], n.beta_f[0:32, p * 128:(p + 1) * 128],
            n.identf[0:32, 0:32]),
            waits=[(sE, e_b), (sE, e_gam), (sV, prev_vbt)], inc=sT)
        v_bt = op(n, "ve", lambda e, po=po: e.tensor_copy(
            n.beta_tok[:, po:po + 32], P[5][0:128, 32:64]), waits=[(sT, t2)], inc=sV)
        prev_vbt = v_bt
        for c in range(2):
            cc = p * 128 + c * 64 + 63
            op(n, "ve", lambda e, c=c, cc=cc: e.tensor_scalar_mul(
                n.sm1[32:64, c * 64:(c + 1) * 64], n.onessm[32:64, 0:64],
                n.gc_f[32:64, cc:cc + 1]), waits=[(sV, v_gc), (sT, prev_t4)])
        v_glb = op(n, "ve", lambda e, p=p: e.tensor_sub(
            n.sm2[32:64, 0:128], n.sm1[32:64, 0:128],
            n.gc_f[32:64, p * 128:(p + 1) * 128]), inc=sV, k=3)
        e_d = op(n, "se", lambda e: e.activation(n.sm3[32:64, 0:128],
                                                 n.sm2[32:64, 0:128], AF.Exp),
                 waits=[(sV, v_glb)], inc=sE)
        t3 = op(n, "te", lambda e: e.transpose(P[5][0:128, 64:96],
                                               n.sm3[32:64, 0:128],
                                               n.identf[32:64, 32:64]),
                waits=[(sE, e_d), (sV, prev_vdec)], inc=sT)
        v_dec = op(n, "ve", lambda e, po=po: e.tensor_copy(
            n.dec_tok[:, po:po + 32], P[5][0:128, 64:96]),
            waits=[(sT, t3)], inc=sV)
        prev_vdec = v_dec
        for c in range(2):
            cc = p * 128 + c * 64 + 63
            ts_gl = op(n, "ve", lambda e, cc=cc: e.tensor_scalar_mul(
                n.sm1[32:64, 0:128], n.onessm[32:64, 0:128],
                n.gc_f[32:64, cc:cc + 1]),
                waits=[(sV, v_gc), (sT, prev_t4)], inc=sV)
            e_g2 = op(n, "se", lambda e: e.activation(
                n.sm1[32:64, 0:128], n.sm1[32:64, 0:128], AF.Exp),
                waits=[(sV, ts_gl)], inc=sE)
            t4 = op(n, "te", lambda e: e.transpose(
                P[5][0:128, 96:128], n.sm1[32:64, 0:128],
                n.identf[32:64, 32:64]),
                waits=[(sE, e_g2), (sV, prev_vgl)], inc=sT)
            prev_t4 = t4
            v_gl = op(n, "ve", lambda e, po=po, c=c, p=p: e.tensor_copy(
                n.glast_bc[:, (p * 2 + c) * 32:(p * 2 + c) * 32 + 32],
                P[5][0:128, 96:128]), waits=[(sT, t4)], inc=sV)
            prev_vgl = v_gl
        t5 = op(n, "te", lambda e, p=p: e.transpose(
            P[6][0:128, 0:16], n.rsq_f[0:16, p * 128:(p + 1) * 128],
            n.identf[0:16, 0:16]),
            waits=[(sE, n.v_rsq), (sV, sv_prev), (sV, prev_vrk)], inc=sT)
        t6 = op(n, "te", lambda e, p=p: e.transpose(
            P[6][0:128, 16:32], n.rsq_f[0:16, 256 + p * 128: 256 + (p + 1) * 128],
            n.identf[0:16, 0:16]), inc=sT)
        n.mark[("rsqT", t)] = t6
        po16 = p * 16
        op(n, "ve", lambda e, po16=po16: e.tensor_copy(
            n.rq_tok[:, po16:po16 + 16], P[6][0:128, 0:16]), waits=[(sT, sT.n)])
        v_rk = op(n, "ve", lambda e, po16=po16: e.tensor_copy(
            n.rk_tok[:, po16:po16 + 16], P[6][0:128, 16:32]), inc=sV, k=2)
        prev_vrk = v_rk
        def rkx(pp):
            return n.rk_tok[:, pp * 16:(pp + 1) * 16].rearrange(
                "p (a b) -> p a b", b=1).broadcast_to([128, 16, 2])

        def v3d(ap):
            return ap.rearrange("p (a b) -> p a b", b=2)
        op(n, "ve", lambda e, po=po: e.tensor_mul(
            n.t32a[:, 0:32], n.beta_tok[:, po:po + 32], n.gam_tok[:, po:po + 32]),
            waits=[(sV, v_bt), (sV, v_rk), (sE, e_gam)])
        op(n, "ve", lambda e, po=po, p=p: e.tensor_tensor(
            v3d(n.bgr_tok[:, po:po + 32]), v3d(n.t32a[:, 0:32]), rkx(p),
            op=ALU.mult))
        op(n, "ve", lambda e, po=po, p=p: e.tensor_tensor(
            v3d(n.grq_tok[:, po:po + 32]), v3d(n.gam_tok[:, po:po + 32]),
            n.rq_tok[:, p * 16:(p + 1) * 16].rearrange(
                "p (a b) -> p a b", b=1).broadcast_to([128, 16, 2]),
            op=ALU.mult))
        op(n, "ve", lambda e, po=po, p=p: e.tensor_tensor(
            v3d(n.brk_tok[:, po:po + 32]), v3d(n.beta_tok[:, po:po + 32]), rkx(p),
            op=ALU.mult))
        v_der = op(n, "ve", lambda e, po=po, p=p: e.tensor_tensor(
            v3d(n.dec_tok[:, po:po + 32]), v3d(n.dec_tok[:, po:po + 32]), rkx(p),
            op=ALU.mult), waits=[(sV, v_dec)], inc=sV, k=5)
        n.mark[("smalls", t, p)] = v_der
    if "gctok" in dbg and t == 0:
        dbg_dump(n, "gctok", n.gc_tok[:], [(sV, n.sV.n)])


def delta_prep(n, t, p):
    sT, sV, sE, sG = n.sT, n.sV, n.sE, n.sG
    P = n.P
    dbg = n.dbg
    po = p * 32
    po16 = p * 16

    n.growv = {}
    for g in range(8):
        wl = [(n.sV, n.v_gc)]
        if g >= 1:
            wl.append((sT, n.mark[("rowmm", t, p, g - 1)]))
        elif p > 0:
            wl.append((sT, n.mark[("rowmm", t, p - 1, 7)]))
        elif ("rowmm", t - 1, NPAIR - 1, 7) in n.mark:
            wl.append((sT, n.mark[("rowmm", t - 1, NPAIR - 1, 7)]))
        n.growv[g] = op(n, "sy", lambda e, g=g, p=p: e.dma_start(
            out=n.grow[0:1, 0:512],
            in_=n.gc_f[32 + g * 4: 36 + g * 4, p * 128:(p + 1) * 128]),
            waits=wl, inc=sG, k=16)
        group_prep(n, t, p, g)

    for g in range(8):
        for j in range(4):
            h = g * 4 + j
            wl = [(n.sE, n.mark[("silu", t, 32 + h)])]
            if g == 0 and j == 0:
                wl.append((sV, n.mark[("smalls", t, p)]))
            if j == 0 and g > 0:
                wl.append((sV, n.mark[("vT", t, p, g - 1)]))
            tv = op(n, "te", lambda e, h=h, j=j, p=p: e.transpose(
                P[6][:].bitcast(BF16)[:, j * 128:(j + 1) * 128],
                n.vc_sb[:, h * T + p * 128: h * T + p * 128 + 128], n.identc[:]),
                waits=wl, inc=sT)
        vv = op(n, "ve", lambda e, g=g: e.tensor_copy(
            n.vT_sb[:, g * 512:(g + 1) * 512], P[6][:].bitcast(BF16)[:, 0:512]),
            waits=[(sT, tv)], inc=sV)
        n.mark[("vT", t, p, g)] = vv
    for g in range(4):
        for j in range(4):
            kv = g * 4 + j
            wl = [(n.sE, n.mark[("silu", t, 16 + kv)])]
            if j == 0:
                wl.append((sV, n.mark[("vT", t, p, 7)] if g == 0
                           else n.mark[("kT", t, p, g - 1)]))
            tv = op(n, "te", lambda e, kv=kv, j=j, p=p: e.transpose(
                P[6][:].bitcast(BF16)[:, j * 128:(j + 1) * 128],
                n.k_sb[:, kv * T + p * 128: kv * T + p * 128 + 128], n.identc[:]),
                waits=wl, inc=sT)
        vv = op(n, "ve", lambda e, g=g: e.tensor_copy(
            n.kT_sb[:, g * 512:(g + 1) * 512], P[6][:].bitcast(BF16)[:, 0:512]),
            waits=[(sT, tv)], inc=sV)
        n.mark[("kT", t, p, g)] = vv


def group_prep(n, t, p, g):
    sT, sV, sE, sG = n.sT, n.sV, n.sE, n.sG
    P = n.P
    po = p * 32
    po16 = p * 16
    h0 = 4 * g
    dbg = n.dbg
    bD, bS, bP_, bQ_, bV_ = P[0], P[1], P[2], P[3], P[4]

    def colAP(tensor, off, reps=128):
        return tensor[:, off:off + 4].rearrange("p (a b) -> p a b", b=1)\
            .broadcast_to([128, 4, reps])

    def maskAP():
        return n.maskSc[:].rearrange("p (a f) -> p a f", a=1)\
            .broadcast_to([128, 4, 128])

    def identAP():
        return n.identc[:].rearrange("p (a f) -> p a f", a=1)\
            .broadcast_to([128, 4, 128])

    def kvrepAP(tensor, base):
        return tensor[:, base:base + 256].rearrange("p (a f) -> p a f", a=2)\
            .rearrange("p (a r) f -> p a r f", r=1).broadcast_to([128, 2, 2, 128])

    wl = [(sG, n.growv[g]), (sV, n.mark.get(("argdone", t, p, g - 1),
                                            n.mark.get(("smalls", t, p), 0)))]
    if g == 0:
        wl.append((sV, n.mark[("smalls", t, p)]))
    rv = op(n, "te", lambda e: e.matmul(
        bD[:, 0:512], n.ones1[0:1, :], n.grow[0:1, 0:512],
        start=True, stop=True), waits=wl, inc=sT)
    n.mark[("rowmm", t, p, g)] = rv

    kv0 = 2 * g
    wl2 = [(sE, n.mark[("silu", t, 16 + kv0)]), (sE, n.mark[("silu", t, 17 + kv0)]),
           (sE, n.mark[("silu", t, kv0)]), (sE, n.mark[("silu", t, 1 + kv0)]),
           (sV, n.mark.get(("scoresdone", t, p, g - 1), 0))]
    for j, kv in enumerate((kv0, kv0 + 1)):
        ksl = n.k_sb[:, kv * T + p * 128: kv * T + p * 128 + 128]
        qsl = n.q_sb[:, kv * T + p * 128: kv * T + p * 128 + 128]
        op(n, "te", lambda e, j=j, ksl=ksl: e.matmul(
            bS[:, j * 128:(j + 1) * 128], ksl, ksl,
            start=(j == 0), stop=False), waits=(wl2 if j == 0 else ()))
        sv = op(n, "te", lambda e, j=j, ksl=ksl, qsl=qsl: e.matmul(
            bS[:, (2 + j) * 128:(3 + j) * 128], ksl, qsl,
            start=False, stop=(j == 1)), inc=sT)
    n.mark[("scores", t, p, g)] = sv

    op(n, "ve", lambda e, po=po, h0=h0: e.tensor_tensor(
        n.t32a[:].rearrange("p (a f) -> p a f", a=4),
        bD[:, 0:512].rearrange("p (a f) -> p a f", a=4),
        colAP(n.gc_tok, po + h0), op=ALU.subtract), waits=[(sT, rv)])
    v_arg = op(n, "ve", lambda e: e.tensor_tensor(
        n.arg_sb[:].rearrange("p (a f) -> p a f", a=4),
        n.t32a[:].rearrange("p (a f) -> p a f", a=4),
        maskAP(), op=ALU.add), inc=sV, k=2)
    n.mark[("argdone", t, p, g)] = v_arg
    e_es = op(n, "se", lambda e: e.activation(n.es_sb[:], n.arg_sb[:], AF.Exp),
              waits=[(sV, v_arg)], inc=sE)
    v_ei = op(n, "ve", lambda e: e.tensor_tensor(
        n.ei_sb[:].rearrange("p (a f) -> p a f", a=4),
        n.es_sb[:].rearrange("p (a f) -> p a f", a=4),
        identAP(), op=ALU.add), waits=[(sE, e_es)], inc=sV)
    op(n, "ve", lambda e, g=g: e.tensor_tensor(
        n.t32b[:].rearrange("p (a b f) -> p a b f", a=2, b=2),
        kvrepAP(bS, 256),
        n.ei_sb[:].rearrange("p (a b f) -> p a b f", a=2, b=2), op=ALU.mult),
        waits=[(sT, sv), (sV, v_ei)])
    v_A = op(n, "ve", lambda e, p=p, h0=h0, po16=po16: e.tensor_tensor(
        n.A_sb[:, h0 * 128:(h0 + 4) * 128].rearrange("p (a b f) -> p a b f", a=2, b=2),
        n.t32b[:].rearrange("p (a b f) -> p a b f", a=2, b=2),
        n.rk_tok[:, po16 + h0 // 2: po16 + h0 // 2 + 2].rearrange(
            "p (a b c) -> p a b c", b=1, c=1).broadcast_to([128, 2, 2, 128]),
        op=ALU.mult), inc=sV, k=2)
    n.mark[("A", t, p, g)] = v_A
    op(n, "ve", lambda e, g=g: e.tensor_tensor(
        n.t32b[:].rearrange("p (a b f) -> p a b f", a=2, b=2),
        kvrepAP(bS, 0),
        n.es_sb[:].rearrange("p (a b f) -> p a b f", a=2, b=2), op=ALU.mult))
    v_pt = op(n, "ve", lambda e, po16=po16, h0=h0: e.tensor_tensor(
        n.ptil_sb[:].rearrange("p (a b f) -> p a b f", a=2, b=2),
        n.t32b[:].rearrange("p (a b f) -> p a b f", a=2, b=2),
        n.rk_tok[:, po16 + h0 // 2: po16 + h0 // 2 + 2].rearrange(
            "p (a b c) -> p a b c", b=1, c=1).broadcast_to([128, 2, 2, 128]),
        op=ALU.mult), inc=sV, k=2)
    n.mark[("scoresdone", t, p, g)] = v_pt

    def pw(i):
        return n.pw_sb[:, i * 512:(i + 1) * 512]

    for j in range(4):
        wlq1 = [(sV, v_pt)] if j == 0 else []
        if j == 0:
            wlq1.append((sV, n.mark.get(("pwdone", t, p, g - 1),
                                        n.mark.get(("pwdone", t, p - 1, 7),
                                                   n.mark.get(("pwdoneT", t - 1), 0)))))
        tv = op(n, "te", lambda e, j=j: e.transpose(
            bQ_[:].bitcast(BF16)[:, j * 128:(j + 1) * 128],
            n.ptil_sb[:, j * 128:(j + 1) * 128],
            n.identc[:]), waits=wlq1, inc=sT)
    v_q1 = op(n, "ve", lambda e, po=po, h0=h0: e.tensor_tensor(
        pw(0).rearrange("p (a f) -> p a f", a=4),
        bQ_[:].bitcast(BF16)[:, 0:512].rearrange("p (a f) -> p a f", a=4),
        colAP(n.brk_tok, po + h0), op=ALU.mult), waits=[(sT, tv)], inc=sV)
    for j in range(4):
        tv = op(n, "te", lambda e, j=j: e.transpose(
            bP_[:].bitcast(BF16)[:, j * 128:(j + 1) * 128],
            pw(0)[:, j * 128:(j + 1) * 128],
            n.identc[:]), waits=([(sV, v_q1)] if j == 0 else ()), inc=sT)
    v_p1 = op(n, "ve", lambda e: e.tensor_copy(pw(1), bP_[:].bitcast(BF16)[:, 0:512]),
              waits=[(sT, tv)], inc=sV)
    lvl = [(0, 1, 2, 3), (2, 3, 4, 5), (4, 5, 6, 7), (6, 7, 8, 9), (8, 9, 10, None)]
    prev_ev = v_p1
    for (qi, pi, qo, po_) in lvl:
        for j in range(4):
            tv = op(n, "te", lambda e, j=j, qi=qi, pi=pi: e.matmul(
                bQ_[:, j * 128:(j + 1) * 128], pw(pi)[:, j * 128:(j + 1) * 128],
                pw(qi)[:, j * 128:(j + 1) * 128], start=(j == 0), stop=(j == 3)),
                waits=([(sV, prev_ev)] if j == 0 else ()), inc=sT)
        v_q = op(n, "ve", lambda e, qo=qo: e.tensor_copy(pw(qo), bQ_[:, 0:512]),
                 waits=[(sT, tv)], inc=sV)
        prev_ev = v_q
        if po_ is not None:
            for j in range(4):
                tv = op(n, "te", lambda e, j=j, qo=qo: e.transpose(
                    bP_[:].bitcast(BF16)[:, j * 128:(j + 1) * 128],
                    pw(qo)[:, j * 128:(j + 1) * 128],
                    n.identc[:]), waits=([(sV, v_q)] if j == 0 else ()), inc=sT)
            prev_ev = op(n, "ve", lambda e, po_=po_: e.tensor_copy(
                pw(po_), bP_[:].bitcast(BF16)[:, 0:512]), waits=[(sT, tv)], inc=sV)

    n.mark[("pwdone", t, p, g)] = prev_ev
    if (p, g) == (NPAIR - 1, 7):
        n.mark[("pwdoneT", t)] = prev_ev
    v_v = op(n, "ve", lambda e: e.tensor_tensor(
        n.zz_sb[:].rearrange("p (a f) -> p a f", a=4),
        pw(10).rearrange("p (a f) -> p a f", a=4), identAP(), op=ALU.add),
        waits=[(sV, prev_ev)], inc=sV)
    for qi, last in ((8, 0), (6, 0), (4, 0), (2, 0), (0, 1)):
        for j in range(4):
            tv = op(n, "te", lambda e, j=j, qi=qi: e.matmul(
                bV_[:, j * 128:(j + 1) * 128], pw(qi)[:, j * 128:(j + 1) * 128],
                n.zz_sb[:, j * 128:(j + 1) * 128], start=(j == 0), stop=(j == 3)),
                waits=([(sV, v_v)] if j == 0 else ()), inc=sT)
        if not last:
            v_v = op(n, "ve", lambda e: e.tensor_add(
                n.zz_sb[:], n.zz_sb[:], bV_[:, 0:512]), waits=[(sT, tv)], inc=sV)
        else:
            v_v = op(n, "ve", lambda e, h0=h0: e.tensor_sub(
                n.V_sb[:, h0 * 128:(h0 + 4) * 128], n.zz_sb[:], bV_[:, 0:512]),
                waits=[(sT, tv)], inc=sV)
    n.mark[("V", t, p, g)] = v_v
    if "V0" in dbg and (t, p, g) == (0, 0, 0):
        dbg_dump(n, "V0", n.V_sb[:, 0:128], [(sV, v_v)])
    if "A0" in dbg and (t, p, g) == (0, 0, 0):
        dbg_dump(n, "A0", n.A_sb[:, 0:128], [(sV, n.mark[("A", t, p, g)])])


def delta_state(n, t, p, ci):
    sT, sV, sE = n.sT, n.sV, n.sE
    P = n.P
    dbg = n.dbg
    po = p * 32
    po16 = p * 16
    cb = ci * 64
    bK, bO1, bO2 = P[0], P[1], P[2]

    def colAP4(tensor, off, reps=128):
        return tensor[cb:cb + 64, off:off + 4].rearrange("p (a b) -> p a b", b=1)\
            .broadcast_to([64, 4, reps])

    def colAPf(tensor, off, reps=128):
        return tensor[:, off:off + 4].rearrange("p (a b) -> p a b", b=1)\
            .broadcast_to([128, 4, reps])

    def rqAP(pn=64):
        pass

    for g in range(8):
        h0 = 4 * g
        kv0 = 2 * g
        wlk = [(sV, n.mark[("V", t, p, 7)]),
               (sV, n.mark.get(("Sbready", t, p, ci, g), n.v_smallconst)),
               (sV, n.mark.get(("uev", t, p, ci, g - 1), 0))]
        tv = None
        for j in range(4):
            h = h0 + j
            kv = kv0 + j // 2
            tv = op(n, "te", lambda e, j=j, h=h, kv=kv, p=p, cb=cb: e.matmul(
                bK[cb:cb + 64, j * 128:(j + 1) * 128],
                n.k_sb[:, kv * T + p * 128 + cb: kv * T + p * 128 + cb + 64],
                n.Sb_sb[:, h * 128:(h + 1) * 128],
                start=(j == 0), stop=(j == 3)), waits=(wlk if j == 0 else ()),
                inc=sT)
        op(n, "ve", lambda e, po=po, h0=h0, cb=cb: e.tensor_tensor(
            n.t32a[cb:cb + 64, :].rearrange("p (a f) -> p a f", a=4),
            bK[cb:cb + 64, 0:512].rearrange("p (a f) -> p a f", a=4),
            colAP4(n.bgr_tok, po + h0), op=ALU.mult), waits=[(sT, tv)])
        op(n, "ve", lambda e, po=po, h0=h0, g=g, cb=cb: e.tensor_tensor(
            n.t32b[cb:cb + 64, :].rearrange("p (a f) -> p a f", a=4),
            n.vT_sb[cb:cb + 64, g * 512:(g + 1) * 512].rearrange(
                "p (a f) -> p a f", a=4),
            colAP4(n.beta_tok, po + h0), op=ALU.mult),
            waits=[(sV, n.mark[("vT", t, p, g)])])
        v_rhs = op(n, "ve", lambda e, cb=cb: e.tensor_sub(
            n.rhs_sb[cb:cb + 64, :], n.t32b[cb:cb + 64, :],
            n.t32a[cb:cb + 64, :]), inc=sV, k=3)
        n.mark[("rhsdone", t, p, ci, g)] = v_rhs
        tv = None
        for j in range(4):
            h = h0 + j
            tv = op(n, "te", lambda e, j=j, h=h, cb=cb: e.matmul(
                bK[cb:cb + 64, j * 128:(j + 1) * 128],
                n.V_sb[cb:cb + 64, h * 128 + cb: h * 128 + cb + 64],
                n.rhs_sb[cb:cb + 64, j * 128:(j + 1) * 128],
                start=(j == 0), stop=(j == 3)),
                waits=([(sV, v_rhs)] if j == 0 else ()), inc=sT)
        n.mark[("ujmm", t, p, ci, g)] = tv
        v_u = op(n, "ve", lambda e, cb=cb: e.tensor_copy(
            n.u_sb[cb:cb + 64, :], bK[cb:cb + 64, 0:512]), waits=[(sT, tv)], inc=sV)
        n.mark[("uev", t, p, ci, g)] = v_u
        if "u0" in dbg and (t, p, ci, g) == (0, 0, 0, 0):
            dbg_dump(n, "u0", n.u_sb[0:64, :], [(sV, v_u)])
        wlq = [(sV, n.mark.get(("coreg", t, p, ci, g - 1), 0))]
        tvq = None
        for j in range(4):
            h = h0 + j
            kv = kv0 + j // 2
            tvq = op(n, "te", lambda e, j=j, h=h, kv=kv, p=p, cb=cb: e.matmul(
                bO1[cb:cb + 64, j * 128:(j + 1) * 128],
                n.q_sb[:, kv * T + p * 128 + cb: kv * T + p * 128 + cb + 64],
                n.Sb_sb[:, h * 128:(h + 1) * 128],
                start=(j == 0), stop=(j == 3)), waits=(wlq if j == 0 else ()),
                inc=sT)
        wla = [(sV, v_u), (sV, n.mark[("A", t, p, g)]),
               (sV, n.mark.get(("Sbready", t, p, ci + 1, g - 1), 0))]
        tva = None
        for j in range(4):
            h = h0 + j
            tva = op(n, "te", lambda e, j=j, h=h, cb=cb: e.matmul(
                bO2[cb:cb + 64, j * 128:(j + 1) * 128],
                n.A_sb[cb:cb + 64, h * 128 + cb: h * 128 + cb + 64],
                n.u_sb[cb:cb + 64, j * 128:(j + 1) * 128],
                start=(j == 0), stop=(j == 3)),
                waits=(wla if j == 0 else ()), inc=sT)
        v_o1 = op(n, "ve", lambda e, po=po, h0=h0, cb=cb: e.tensor_tensor(
            n.t32a[cb:cb + 64, :].rearrange("p (a f) -> p a f", a=4),
            bO1[cb:cb + 64, 0:512].rearrange("p (a f) -> p a f", a=4),
            colAP4(n.grq_tok, po + h0), op=ALU.mult), waits=[(sT, tvq)], inc=sV)
        n.mark[("oe1", t, p, ci, g)] = v_o1
        v_o2 = op(n, "ve", lambda e, po16=po16, h0=h0, cb=cb: e.tensor_tensor(
            n.t32b[cb:cb + 64, :].rearrange("p (a b f) -> p a b f", a=2, b=2),
            bO2[cb:cb + 64, 0:512].rearrange("p (a b f) -> p a b f", a=2, b=2),
            n.rq_tok[cb:cb + 64, po16 + h0 // 2: po16 + h0 // 2 + 2].rearrange(
                "p (a b c) -> p a b c", b=1, c=1).broadcast_to([64, 2, 2, 128]),
            op=ALU.mult), waits=[(sT, tva)], inc=sV)
        n.mark[("oe2", t, p, ci, g)] = v_o2
        v_os = op(n, "ve", lambda e, cb=cb: e.tensor_add(
            n.o1_sb[cb:cb + 64, :], n.t32a[cb:cb + 64, :], n.t32b[cb:cb + 64, :]),
            inc=sV)
        op(n, "ve", lambda e, cb=cb: e.tensor_mul(
            n.on_sb[cb:cb + 64, :], n.o1_sb[cb:cb + 64, :], n.o1_sb[cb:cb + 64, :]))
        op(n, "ve", lambda e, cb=cb: e.reduce_sum(
            n.lncol[cb:cb + 64, 8:12],
            n.on_sb[cb:cb + 64, :].rearrange("p (a f) -> p a f", a=4), axis=AXL.X))
        op(n, "ve", lambda e, cb=cb: e.tensor_scalar(
            n.lncol[cb:cb + 64, 8:12], n.lncol[cb:cb + 64, 8:12],
            1.0 / DV, RMS_EPS, op0=ALU.mult, op1=ALU.add))
        v_rr = op(n, "ve", lambda e, cb=cb: e.reciprocal(
            n.lncol[cb:cb + 64, 8:12], n.lncol[cb:cb + 64, 8:12]), inc=sV, k=4)
        e_sq = op(n, "se", lambda e, cb=cb: e.activation(
            n.lncol[cb:cb + 64, 12:16], n.lncol[cb:cb + 64, 8:12], AF.Sqrt),
            waits=[(sV, v_rr)], inc=sE)
        v_on = op(n, "ve", lambda e, cb=cb: e.tensor_tensor(
            n.on_sb[cb:cb + 64, :].rearrange("p (a f) -> p a f", a=4),
            n.o1_sb[cb:cb + 64, :].rearrange("p (a f) -> p a f", a=4),
            n.lncol[cb:cb + 64, 12:16].rearrange("p (a b) -> p a b", b=1)
                .broadcast_to([64, 4, 128]), op=ALU.mult),
            waits=[(sE, e_sq)], inc=sV)
        tvc = None
        for j in range(4):
            tvc = op(n, "te", lambda e, j=j, cb=cb: e.transpose(
                bO1[:].bitcast(BF16)[:, j * 64:(j + 1) * 64],
                n.on_sb[cb:cb + 64, j * 128:(j + 1) * 128],
                n.identc[cb:cb + 64, cb:cb + 64]),
                waits=([(sV, v_on)] if j == 0 else ()), inc=sT)
        zoff = p * 128 + cb

        def vc_view(e_, h0=h0, zoff=zoff):
            return n.vc_sb[:, h0 * T:(h0 + 4) * T].rearrange(
                "p (a f) -> p a f", a=4)[:, :, zoff:zoff + 64]

        def z_view(e_, h0=h0, zoff=zoff):
            return n.z_sb[:, h0 * T:(h0 + 4) * T].rearrange(
                "p (a f) -> p a f", a=4)[:, :, zoff:zoff + 64]

        v_cg = op(n, "ve", lambda e, h0=h0, zoff=zoff: e.scalar_tensor_tensor(
            vc_view(e, h0, zoff),
            bO1[:].bitcast(BF16)[:, 0:256].rearrange("p (a f) -> p a f", a=4),
            n.nw[:, 0:1], z_view(e, h0, zoff),
            op0=ALU.mult, op1=ALU.mult),
            waits=[(sT, tvc), (sE, n.mark[("zev", t, 31)])],
            inc=sV)
        n.mark[("coreg", t, p, ci, g)] = v_cg
        v_kh = op(n, "ve", lambda e, g=g, po=po, h0=h0, cb=cb: e.tensor_tensor(
            n.kh_sb[cb:cb + 64, :].rearrange("p (a b f) -> p a b f", a=2, b=2),
            n.kT_sb[cb:cb + 64, g * 256:(g + 1) * 256].rearrange(
                "p (a r f) -> p a r f", a=2, r=1).broadcast_to([64, 2, 2, 128]),
            n.dec_tok[cb:cb + 64, po + h0: po + h0 + 4].rearrange(
                "p (a b c) -> p a b c", b=2, c=1).broadcast_to([64, 2, 2, 128]),
            op=ALU.mult), waits=[(sV, n.mark[("kT", t, p, g // 2)])], inc=sV)
        tvd = None
        for j in range(4):
            h = h0 + j
            tvd = op(n, "te", lambda e, j=j, cb=cb: e.matmul(
                bO2[:, j * 128:(j + 1) * 128],
                n.kh_sb[cb:cb + 64, j * 128:(j + 1) * 128],
                n.u_sb[cb:cb + 64, j * 128:(j + 1) * 128],
                start=(j == 0), stop=(j == 3)),
                waits=([(sV, v_kh), (sV, v_o2)] if j == 0 else ()), inc=sT)
        gl_off = (p * 2 + ci) * 32 + h0
        op(n, "ve", lambda e, gl_off=gl_off, h0=h0: e.tensor_tensor(
            n.t32a[:].rearrange("p (a f) -> p a f", a=4),
            n.S_sb[:, h0 * 128:(h0 + 4) * 128].rearrange("p (a f) -> p a f", a=4),
            colAPf(n.glast_bc, gl_off), op=ALU.mult), waits=[(sT, tvd)])
        op(n, "ve", lambda e, h0=h0: e.tensor_add(
            n.S_sb[:, h0 * 128:(h0 + 4) * 128], n.t32a[:], bO2[:, 0:512]))
        v_sb = op(n, "ve", lambda e, h0=h0: e.tensor_copy(
            n.Sb_sb[:, h0 * 128:(h0 + 4) * 128],
            n.S_sb[:, h0 * 128:(h0 + 4) * 128]), inc=sV, k=3)
        n.mark[("Sbready", t, p, ci + 1, g)] = v_sb
        if ci == 1:
            if p == NPAIR - 1:
                n.mark[("Sbready", t + 1, 0, 0, g)] = v_sb
            else:
                n.mark[("Sbready", t, p + 1, 0, g)] = v_sb
    n.mark[("SbT", t, p, ci + 1)] = sV.n
    n.mark[("statevec", t)] = sV.n
    if "S0" in dbg and (t, p, ci) == (0, NPAIR - 1, 1):
        dbg_dump(n, "S0", n.S_sb[:], [(sV, n.sV.n)])
    if "core" in dbg and (t, p, ci) == (0, NPAIR - 1, 1):
        dbg_dump(n, "core", n.vc_sb[:], [(sV, n.sV.n)])


def wout_phase(n, t):
    sT, sV, sE = n.sT, n.sV, n.sE
    P = n.P
    ws = n.ws
    par = t % 2
    xoff = par * (T // 128) * D
    v_core = n.mark[("statevec", t)]

    tv = None
    for h in range(HV):
        half, wv = ws.load(n.wo_b[h * 128:(h + 1) * 128, :],
                           waits=([(sV, v_core)] if h == 0 else ()))
        for tc in range(T // 128):
            for nb in range(4):
                bank = P[tc * 4 + nb]
                wl = []
                if h == 0 and nb == 0:
                    wl = [wv, (sV, v_core),
                          (sV, n.mark.get(("woutev", t - 1), 0))]
                elif nb == 0 and tc == 0:
                    wl = [wv]
                tv = op(n, "te", lambda e, h=h, tc=tc, nb=nb, half=half, bank=bank:
                        e.matmul(
                            bank[:, 0:512],
                            n.vc_sb[:, h * T + tc * 128: h * T + tc * 128 + 128],
                            n.wbuf[:, half * 2048 + nb * 512:
                                   half * 2048 + (nb + 1) * 512],
                            start=(h == 0), stop=(h == HV - 1)),
                        waits=wl, inc=sT)
        ws.done(tv)
    n.mark[("woutmm", t)] = tv
    v = None
    for tc in range(T // 128):
        for nb in range(4):
            bank = P[tc * 4 + nb]
            xs = n.x_sb[:, xoff + tc * D + nb * 512: xoff + tc * D + (nb + 1) * 512]
            v = op(n, "ve", lambda e, bank=bank, xs=xs: e.tensor_add(
                xs, xs, bank[:, 0:512]), waits=([(sT, tv)] if (tc, nb) == (0, 0)
                                                else ()), inc=sV, wide=True)
    n.mark[("woutev", t)] = v
    if "x1" in n.dbg and t == 0:
        dbg_dump(n, "x1", n.x_sb[:, xoff:xoff + (T // 128) * D], [(n.sV, v)])


def ffn_phase(n, t):
    sT, sV, sE, sO = n.sT, n.sV, n.sE, n.sO
    P = n.P
    ws = n.ws
    par = t % 2
    xoff = par * (T // 128) * D
    v_res = n.mark[("woutev", t)]

    store_vals = []
    prev_vxq = v_res
    for tc in range(T // 128):
        xs = n.x_sb[:, xoff + tc * D: xoff + (tc + 1) * D]
        op(n, "ve", lambda e, xs=xs: e.reduce_sum(n.lncol[:, 0:1], xs, axis=AXL.X),
           waits=[(sV, v_res)])
        op(n, "ve", lambda e: e.tensor_scalar_mul(n.lncol[:, 2:3], n.lncol[:, 0:1],
                                                  1.0 / D))
        v_m = op(n, "ve", lambda e: e.tensor_mul(
            n.lncol[:, 4:5], n.lncol[:, 2:3], n.lncol[:, 2:3]), inc=sV, k=3)
        e_sq = op(n, "se", lambda e, xs=xs: e.activation(
            n.ffnq[:], xs, AF.Square, accum_out=n.lncol[:, 1:2]),
            waits=[(sV, prev_vxq)], inc=sE)
        op(n, "ve", lambda e: e.tensor_scalar_mul(n.lncol[:, 3:4], n.lncol[:, 1:2],
                                                  1.0 / D), waits=[(sE, e_sq)])
        op(n, "ve", lambda e: e.tensor_sub(n.lncol[:, 5:6], n.lncol[:, 3:4],
                                           n.lncol[:, 4:5]))
        op(n, "ve", lambda e: e.tensor_scalar_add(n.lncol[:, 6:7], n.lncol[:, 5:6],
                                                  LN_EPS))
        v_rc = op(n, "ve", lambda e: e.reciprocal(n.lncol[:, 7:8], n.lncol[:, 6:7]),
                  inc=sV, k=3)
        e_st = op(n, "se", lambda e: e.activation(n.lncol[:, 8:9], n.lncol[:, 7:8],
                                                  AF.Sqrt), waits=[(sV, v_rc)],
                  inc=sE)
        op(n, "ve", lambda e: e.tensor_mul(n.lncol[:, 9:10], n.lncol[:, 5:6],
                                           n.lncol[:, 7:8]), waits=[(sE, e_st)])
        op(n, "ve", lambda e: e.tensor_scalar_add(n.lncol[:, 9:10],
                                                  n.lncol[:, 9:10], NRM_EPS))
        v_r2 = op(n, "ve", lambda e: e.reciprocal(n.lncol[:, 10:11],
                                                  n.lncol[:, 9:10]), inc=sV, k=2)
        e_r2 = op(n, "se", lambda e: e.activation(n.lncol[:, 11:12],
                                                  n.lncol[:, 10:11], AF.Sqrt),
                  waits=[(sV, v_r2)], inc=sE)
        op(n, "ve", lambda e, xs=xs: e.tensor_scalar(
            n.ffnq[:], xs, n.lncol[:, 2:3], n.lncol[:, 8:9],
            op0=ALU.subtract, op1=ALU.mult), waits=[(sE, e_r2)], wide=True)
        v_u2 = op(n, "ve", lambda e: e.tensor_mul(n.ffnq[:], n.ffnq[:],
                                                  n.gamma_bc[:]), inc=sV, k=2,
                  wide=True)
        op(n, "ve", lambda e: e.reduce_max(n.lncol[:, 12:13], n.ffnq[:], axis=AXL.X,
                                           apply_absolute_value=True))
        op(n, "ve", lambda e: e.tensor_mul(n.lncol[:, 12:13], n.lncol[:, 12:13],
                                           n.lncol[:, 11:12]))
        op(n, "ve", lambda e: e.tensor_scalar_add(n.lncol[:, 13:14],
                                                  n.lncol[:, 12:13], QEPS))
        v_am = op(n, "ve", lambda e: e.reciprocal(n.lncol[:, 14:15],
                                                  n.lncol[:, 13:14]), inc=sV, k=4)
        op(n, "ve", lambda e: e.tensor_scalar_mul(n.lncol[:, 14:15],
                                                  n.lncol[:, 14:15], 127.0),
           waits=[(sV, v_am)])
        v_qs = op(n, "ve", lambda e: e.tensor_mul(n.lncol[:, 14:15],
                                                  n.lncol[:, 14:15],
                                                  n.lncol[:, 11:12]), inc=sV, k=2)
        op(n, "ve", lambda e: e.tensor_scalar_mul(n.ffnq[:], n.ffnq[:],
                                                  n.lncol[:, 14:15]),
           waits=[(sV, v_qs)], wide=True)
        op(n, "ve", lambda e: e.tensor_scalar(n.ffnq[:], n.ffnq[:], MAGIC, MAGIC,
                                              op0=ALU.add, op1=ALU.subtract),
           wide=True)
        v_xq = op(n, "ve", lambda e: e.tensor_scalar(
            n.xq_tok[:], n.ffnq[:], 127.0, -128.0, op0=ALU.min, op1=ALU.max),
            inc=sV, k=3, wide=True)
        prev_vxq = v_xq
        op(n, "ve", lambda e: e.tensor_mul(n.lncol[:, 15:16], n.lncol[:, 13:14],
                                           n.mw_bc[:, 0:1]))
        v_osc = op(n, "ve", lambda e: e.tensor_scalar_mul(
            n.lncol[:, 15:16], n.lncol[:, 15:16], 1.0 / 127.0), inc=sV, k=2)
        for kk in range(NKC):
            bank = P[kk % 2]
            if kk < 2:
                wl = [(sV, v_xq), (sV, n.mark.get(("xqev", kk - 2 + NKC), 0))]
            else:
                wl = [(sV, n.mark[("xqev", kk - 2)])]
            tv = op(n, "te", lambda e, kk=kk, bank=bank: e.transpose(
                bank[:].bitcast(BF16)[:, 0:128],
                n.xq_tok[:, kk * 128:(kk + 1) * 128], n.identc[:]),
                waits=wl, inc=sT)
            n.mark[("xqT", t)] = tv
            n.mark[("xqev", kk)] = op(
                n, "ve", lambda e, kk=kk, tc=tc, bank=bank: e.tensor_copy(
                    n.xn_f[:, kk * T + tc * 128: kk * T + tc * 128 + 128],
                    bank[:].bitcast(BF16)[:, 0:128]), waits=[(sT, tv)], inc=sV)
            if kk >= NKC - 2:
                n.mark[("xqev", kk + NKC)] = n.mark[("xqev", kk)]
        v_xqf = n.sV.n
        for ob in range(4):
            bank = P[4 + ob % 2]
            tv = None
            for kq in range(4):
                half, wv = ws.load(
                    n.wf_b[kq * 512:(kq + 1) * 512, ob * 512:(ob + 1) * 512]
                    .rearrange("(k p) c -> p k c", p=128))
                for ki in range(4):
                    kk = kq * 4 + ki
                    wl = []
                    if ki == 0:
                        wl = [wv]
                        if kq == 0:
                            wl.append((sV, v_xqf))
                            wl.append((sV, n.mark.get(("ffnev", t, tc, ob - 2),
                                      n.mark.get(("ffnev", t, tc - 1, 2 + ob), 0))))
                    tv = op(n, "te", lambda e, kk=kk, ki=ki, kq=kq, tc=tc,
                            half=half, bank=bank: e.matmul(
                                bank[:, 0:512],
                                n.xn_f[:, kk * T + tc * 128: kk * T + tc * 128 + 128],
                                n.wbuf[:, half * 2048 + ki * 512:
                                       half * 2048 + (ki + 1) * 512],
                                start=(kk == 0), stop=(kk == NKC - 1)),
                            waits=wl, inc=sT)
                ws.done(tv)
            xs2 = n.x_sb[:, xoff + tc * D + ob * 512: xoff + tc * D + (ob + 1) * 512]
            n.mark[("ffnev", t, tc, ob)] = op(
                n, "ve", lambda e, bank=bank, xs2=xs2: e.scalar_tensor_tensor(
                    xs2, bank[:, 0:512], n.lncol[:, 15:16], xs2,
                    op0=ALU.mult, op1=ALU.add),
                waits=[(sT, tv), (sV, v_osc)], inc=sV, wide=True)
        v_ffn = n.sV.n
        sv = op(n, "sy", lambda e, t=t, tc=tc, xs=xs: e.dma_start(
            out=n.out_d[t * T + tc * 128: t * T + (tc + 1) * 128, :], in_=xs),
            waits=[(sV, v_ffn)], inc=sO, k=16)
        store_vals.append(sv)
    n.mark[("ffnq_free", t)] = n.sV.n
    n.mark[("ffndone", t)] = n.sV.n
    n.store_v[t] = store_vals[-1]


def make_consts():
    ident = np.eye(128, dtype=BF)
    jj = np.arange(128)[:, None]
    tt = np.arange(128)[None, :]
    same = (jj // 64) == (tt // 64)
    strict = same & ((tt % 64) > (jj % 64))
    maskS = np.where(strict, 0.0, NEG).astype(np.float32)
    onehot = np.zeros((128, 16 * 16), dtype=BF)
    for h in range(16):
        onehot[:, h * 16 + h] = 1.0
    return ident, maskS, onehot


def kernel(**inputs):
    x = np.ascontiguousarray(inputs["L0_attn_hidden_states"])
    ident, maskS, onehot = make_consts()
    base = {
        "w_qkv": np.ascontiguousarray(inputs["L0_attn_w_qkv"]),
        "w_z": np.ascontiguousarray(inputs["L0_attn_w_z"]),
        "w_b": np.ascontiguousarray(inputs["L0_attn_w_b"]),
        "w_a": np.ascontiguousarray(inputs["L0_attn_w_a"]),
        "w_out": np.ascontiguousarray(inputs["L0_attn_w_out"]),
        "conv_w": np.ascontiguousarray(inputs["L0_attn_conv_w"]),
        "dt_bias": np.ascontiguousarray(inputs["L0_attn_dt_bias"]).reshape(HV, 1),
        "A_log": np.ascontiguousarray(inputs["L0_attn_A_log"]).reshape(HV, 1),
        "norm_w": np.ascontiguousarray(inputs["L0_attn_norm_weight"]).reshape(DV, 1),
        "gamma": np.ascontiguousarray(inputs["L0_ffn_gamma"]).reshape(1, D),
        "W_ffn": np.ascontiguousarray(inputs["L0_ffn_W"]),
        "ident": ident, "maskS": maskS, "onehot": onehot,
    }
    nc = build_program()
    in_maps = [dict(base, x=np.ascontiguousarray(x[i])) for i in range(8)]
    res = run_bass_kernel_spmd(nc, in_maps, core_ids=list(range(8)))
    return np.stack([res.results[i]["out"] for i in range(8)], axis=0)
